# version 02
# baseline (speedup 1.0000x reference)
import math
import numpy as np
import ml_dtypes

import concourse.bass as bass
import concourse.mybir as mybir
from concourse.alu_op_type import AluOpType
from concourse.bass_utils import run_bass_kernel_spmd
from contextlib import ExitStack

F32 = mybir.dt.float32
BF16 = mybir.dt.bfloat16
U8 = mybir.dt.uint8
AF = mybir.ActivationFunctionType
AX = mybir.AxisListType.X
OP = AluOpType

ENGINES = ("tensor", "vector", "scalar", "gpsimd", "sync")
DMA_QUEUES = ("sync", "gpsimd", "scalar")


class Op:
    __slots__ = ("eng", "idx", "fn", "deps", "step", "inc_val", "marked",
                 "tag", "sem_key")

    def __init__(self, eng, idx, fn, deps, step, tag, sem_key):
        self.eng = eng
        self.idx = idx
        self.fn = fn
        self.deps = deps
        self.step = step
        self.inc_val = None
        self.marked = False
        self.tag = tag
        self.sem_key = sem_key


class Builder:

    def __init__(self, nc, es):
        self.nc = nc
        self.es = es
        self.ops = {e: [] for e in ENGINES}
        self.pre = {e: [] for e in ENGINES}
        self.sems = {}
        self.n_emitted = 0
        self.streams = {}
        self._frozen_pre = False

    def stream(self, key, queue, wait_all=False):
        if key in self.streams:
            assert self.streams[key]["queue"] == queue, key
            return key
        self.streams[key] = {"wait_all": wait_all, "queue": queue, "total": 0}
        return key

    def emit(self, eng, fn, deps=(), dma=False, tag=None, sem_key=None,
             front=False):
        deps = [d for d in deps if d is not None]
        for d in deps:
            assert isinstance(d, Op)
            d.marked = True
        if sem_key is None:
            sem_key = eng
        if front:
            assert not deps and dma and not self._frozen_pre
            op = Op(eng, -10000 + len(self.pre[eng]), fn, deps, 16, tag,
                    sem_key)
            op.marked = True
            self.streams[sem_key]["total"] += 16
            self.pre[eng].append(op)
            self.n_emitted += 1
            return op
        op = Op(eng, len(self.ops[eng]), fn, deps, 16 if dma else 1, tag,
                sem_key)
        if dma:
            op.marked = True
            self.streams[sem_key]["total"] += 16
        self.ops[eng].append(op)
        self.n_emitted += 1
        return op

    def dma(self, queue, out, in_, deps=(), tag=None, stream=None,
            wait_all=False, front=False, **kw):
        assert queue in DMA_QUEUES
        if stream is None:
            is_store = type(out.tensor).__name__ == "DRamTensorHandle"
            if is_store:
                stream = queue + "_st"
                wait_all = False
            else:
                stream = queue + "_ld"
                wait_all = True
        self.stream(stream, queue, wait_all)
        info = self.streams[stream]
        deps = list(deps)
        if not info["wait_all"] and info.get("last_op") is not None:
            deps.append(info["last_op"])
        op = self.emit(
            queue, lambda e: e.dma_start(out=out, in_=in_, **kw), deps,
            dma=True, tag=tag, sem_key=stream, front=front)
        info["last_op"] = op
        return op

    def finalize(self):
        nc = self.nc
        keys = list(ENGINES) + list(self.streams)
        for k in keys:
            self.sems[k] = self.es.enter_context(nc.semaphore("sem_" + k))
        counts = {k: 0 for k in keys}
        for e in ENGINES:
            for op in self.pre[e] + self.ops[e]:
                if op.marked:
                    counts[op.sem_key] += op.step
                    op.inc_val = counts[op.sem_key]
        block = self.es.enter_context(nc.Block())
        for e in ENGINES:
            self._emit_engine(block, e)

    def check(self):
        counts = {}
        for e in ENGINES:
            for op in self.pre[e] + self.ops[e]:
                if op.marked or op.step == 16:
                    counts[op.sem_key] = counts.get(op.sem_key, 0) + op.step
                    op.inc_val = counts[op.sem_key]
        ptr = {e: 0 for e in ENGINES}
        allops = {e: self.pre[e] + self.ops[e] for e in ENGINES}
        sem = {}
        progress = True
        while progress:
            progress = False
            for e in ENGINES:
                while ptr[e] < len(allops[e]):
                    op = allops[e][ptr[e]]
                    ok = True
                    for d in op.deps:
                        if d.sem_key == e:
                            continue
                        v = self._wait_val(d)
                        if sem.get(d.sem_key, 0) < v:
                            ok = False
                            break
                    if not ok:
                        break
                    if op.marked:
                        sem[op.sem_key] = sem.get(op.sem_key, 0) + op.step
                    ptr[e] += 1
                    progress = True
        stuck = {e: (ptr[e], len(allops[e])) for e in ENGINES
                 if ptr[e] < len(allops[e])}
        if stuck:
            msg = []
            for e, (p, n) in stuck.items():
                op = allops[e][p]
                blocked = []
                for d in op.deps:
                    if d.sem_key != e and sem.get(d.sem_key, 0) < self._wait_val(d):
                        blocked.append((d.sem_key, self._wait_val(d),
                                        sem.get(d.sem_key, 0), d.tag))
                msg.append(f"{e}@{p}/{n} tag={op.tag} blocked_on={blocked}")
            raise RuntimeError("DEADLOCK:\n" + "\n".join(msg))

    def _wait_val(self, d):
        if d.sem_key in self.streams and self.streams[d.sem_key]["wait_all"]:
            return self.streams[d.sem_key]["total"]
        return d.inc_val

    def _emit_engine(self, block, eng):
        ops = self.pre[eng] + self.ops[eng]
        sems = self.sems
        if not ops and not any(s["queue"] == eng for s in self.streams.values()):
            return
        drains = [(k, s["total"]) for k, s in self.streams.items()
                  if s["queue"] == eng and s["total"]]

        def body(engobj):
            waited = {}
            for op in ops:
                for d in op.deps:
                    if d.eng == eng and d.sem_key == eng:
                        assert d.idx < op.idx
                    assert d.marked and d.inc_val is not None, (d.tag, op.tag)
                    v = self._wait_val(d)
                    if waited.get(d.sem_key, 0) < v:
                        engobj.wait_ge(sems[d.sem_key], v)
                        waited[d.sem_key] = v
                inst = op.fn(engobj)
                if op.marked:
                    assert inst is not None, (eng, op.idx, op.tag)
                    inst.then_inc(sems[op.sem_key], op.step)
            for k, tot in drains:
                engobj.wait_ge(sems[k], tot)

        getattr(block, eng)(body)




def sub_l0_attn_a(seg, rv, xn, xv, w, deps):
    b = seg.b
    tag = "l0aa"
    banks = getattr(seg, "_shared_banks", None)
    if banks is None:
        banks = [PB(seg, tag + f"_b{i}") for i in range(8)]
    banks = banks[0:6]
    if not hasattr(seg, "A"):
        seg.A = seg.make_arena("A", 60 * 1024)
    A = seg.A
    A.reset()
    lnops = emit_layernorm(seg, rv, xv, deps, banks, eps=1e-5,
                           tag=tag + "_ln", alloc=A.alloc)
    qd = seg.dram(tag + "_q", [D, T], BF16)
    kd = seg.dram(tag + "_k", [D, T], BF16)
    vd = seg.dram(tag + "_v", [T, D], BF16)
    od = seg.dram(tag + "_o", [D, T], BF16)
    wq = seg.din(tag + "_wq", w["qw"])
    wk = seg.din(tag + "_wk", w["kw"])
    wv = seg.din(tag + "_wv", w["vw"])
    wo = seg.din(tag + "_wo", w["ow"])
    wbuf = seg.sb(tag + "_wb", [128, DC * 1024], BF16)
    A.reset()
    wqv, wq_op = load_weight_bf16(seg, tag + "q", wq, D, D, tile=wbuf,
                                  stream=tag + "_w")
    qevs = emit_proj(seg, tag + "q", wqv, wq_op, xv, DC, D,
                     evac_to_dram(seg, tag + "qe", qd, alloc=A.alloc),
                     lnops, banks)
    wkv, wk_op = load_weight_bf16(seg, tag + "k", wk, D, D, tile=wbuf,
                                  deps=qevs, stream=tag + "_w")
    kevs = emit_proj(seg, tag + "k", wkv, wk_op, xv, DC, D,
                     evac_to_dram(seg, tag + "ke", kd, alloc=A.alloc),
                     lnops, banks)
    wvv, wv_op = load_weight_bf16(seg, tag + "v", wv, D, D, tile=wbuf,
                                  deps=kevs, stream=tag + "_w")
    vevs = emit_proj_tok(seg, tag + "v", wvv, wv_op, xv, DC, D, vd, lnops,
                         banks, alloc=A.alloc)
    A.reset()
    aouts = emit_attention(seg, tag + "_at", qd, kd, vd, od,
                           qevs + kevs + vevs, causal=True, scale=1.0 / 32.0,
                           pbs=banks, alloc=A.alloc)
    oldv = b.dma("sync", xv, od.rearrange("(c p) t -> p c t", p=128),
                 deps=aouts, tag=tag + "_oldv")
    wov, wo_op = load_weight_bf16(seg, tag + "o", wo, D, D, tile=wbuf,
                                  deps=vevs, stream=tag + "_w")

    def resadd(oc, tb, pb, mm):
        ap = rv[:, oc, tb * TB:(tb + 1) * TB]
        return pb.read(
            "vector",
            lambda e, a=ap, p=pb.tile[:, 0:TB]: e.tensor_tensor(
                a, a, p, OP.add),
            tag=tag + "_res")

    oevs = emit_proj(seg, tag + "o", wov, wo_op, xv, DC, D, resadd,
                     [oldv], banks)
    return oevs


def sub_l0_attn_b(seg, rv, xv, w, deps):
    b = seg.b
    tag = "l0ab"
    banks = seg._shared_banks[0:6]
    if not hasattr(seg, "A"):
        seg.A = seg.make_arena("A", 60 * 1024)
    A = seg.A
    A.reset()
    lnops = emit_layernorm(seg, rv, xv, deps, banks, eps=1e-5,
                           tag=tag + "_ln", alloc=A.alloc)
    qd = seg.dram(tag + "_q", [D, T], BF16)
    kd = seg.dram(tag + "_k", [D, T], BF16)
    vd = seg.dram(tag + "_v", [T, D], BF16)
    od = seg.dram(tag + "_o", [D, T], BF16)
    wq = seg.din(tag + "_wq", w["Wq"])
    wk = seg.din(tag + "_wk", w["Wk"])
    wv = seg.din(tag + "_wv", w["Wv"])
    wo = seg.din(tag + "_wo", w["Wo"])
    bqc, bq_op = seg.const(tag + "_bq", w["bq"].reshape(DC, 128).T.copy())
    bkc, bk_op = seg.const(tag + "_bk", w["bk"].reshape(DC, 128).T.copy())
    bvc, bv_op = seg.const(tag + "_bv", bf(w["bv"].reshape(1, D)))
    boc, bo_op = seg.const(tag + "_bo", w["bo"].reshape(DC, 128).T.copy())
    wbuf = seg.sb(tag + "_wb", [128, DC * 1024], BF16)
    A.reset()
    wqv, wq_op = load_weight_bf16(seg, tag + "q", wq, D, D, tile=wbuf,
                                  stream=tag + "_w")
    qevs = emit_proj(seg, tag + "q", wqv, wq_op, xv, DC, D,
                     evac_to_dram(seg, tag + "qe", qd, bias=bqc,
                                  bias_op=bq_op, alloc=A.alloc),
                     lnops, banks)
    wkv, wk_op = load_weight_bf16(seg, tag + "k", wk, D, D, tile=wbuf,
                                  deps=qevs, stream=tag + "_w")
    kevs = emit_proj(seg, tag + "k", wkv, wk_op, xv, DC, D,
                     evac_to_dram(seg, tag + "ke", kd, bias=bkc,
                                  bias_op=bk_op, alloc=A.alloc),
                     lnops, banks)
    wvv, wv_op = load_weight_bf16(seg, tag + "v", wv, D, D, tile=wbuf,
                                  deps=kevs, stream=tag + "_w")
    vevs = emit_proj_tok(seg, tag + "v", wvv, wv_op, xv, DC, D, vd, lnops,
                         banks, bias=(bvc, bv_op), alloc=A.alloc)
    xh = (2.0 ** 8) ** (1.0 / HEADS)
    slopes = np.asarray([1.0 / xh ** (i + 1) for i in range(HEADS)], np.float32)
    A.reset()
    aouts = emit_attention(seg, tag + "_at", qd, kd, vd, od,
                           qevs + kevs + vevs, causal=False, scale=1.0 / 8.0,
                           alibi_slopes=slopes, pbs=banks, alloc=A.alloc)
    oldv = b.dma("sync", xv, od.rearrange("(c p) t -> p c t", p=128),
                 deps=aouts, tag=tag + "_oldv")
    wov, wo_op = load_weight_bf16(seg, tag + "o", wo, D, D, tile=wbuf,
                                  deps=vevs, stream=tag + "_w")

    def resadd(oc, tb, pb, mm):
        ap = rv[:, oc, tb * TB:(tb + 1) * TB]
        return pb.read(
            "vector",
            lambda e, a=ap, p=pb.tile[:, 0:TB], bo_=boc[:, oc:oc + 1]:
                e.scalar_tensor_tensor(a, p, bo_, a, OP.add, OP.add),
            deps=[bo_op], tag=tag + "_res")

    oevs = emit_proj(seg, tag + "o", wov, wo_op, xv, DC, D, resadd,
                     [oldv], banks)
    return oevs


HMOE = 2816
HCM = HMOE // 128


def rope_tables():
    inv = 1.0 / 10000.0 ** (np.arange(0, HD, 2, dtype=np.float32) / HD)
    t = np.arange(T, dtype=np.float32)
    fr = np.einsum("i,j->ij", t, inv)
    emb = np.concatenate([fr, fr], -1)
    cos = np.cos(emb).astype(np.float32).T
    sin = np.sin(emb).astype(np.float32).T
    return bf(np.tile(cos, (2, 1))), bf(np.tile(sin, (2, 1)))


def rot_half_mat():
    R = np.zeros((HD, HD), np.float32)
    for i in range(32):
        R[i, 32 + i] = -1.0
        R[32 + i, i] = 1.0
    R2 = np.zeros((128, 128), np.float32)
    R2[0:64, 0:64] = R
    R2[64:128, 64:128] = R
    return R2.T.copy()


def evac_rope_to_dram(seg, tag, dram_fm, rotbank, r2t, r2t_op, cos2, sin2,
                      cs_op, alloc=None):
    alloc = alloc or seg.sb
    stages = [alloc(f"{tag}_stg{i}", [128, TB], BF16) for i in range(3)]
    t1 = alloc(f"{tag}_t1", [128, TB], F32)
    t2 = alloc(f"{tag}_t2", [128, TB], F32)
    state = {"i": 0, "last_st": [None] * 3}

    def out_fn(oc, tb, pb, mm):
        si = state["i"] % 3
        state["i"] += 1
        stg = stages[si]
        qe = pb.read("vector",
                     lambda e, o=stg[:], p=pb.tile[:, 0:TB]: e.tensor_copy(o, p),
                     deps=[state["last_st"][si]], tag=tag + "_qe")
        rot = rotbank.mm_acc([(rotbank.tile[:, 0:TB], r2t[:], stg[:])],
                             deps=[qe, r2t_op], tag=tag + "_rot")
        tc = seg.b.emit("vector",
                        lambda e, o=t2[:], i=stg[:],
                               c=cos2[:, tb * TB:(tb + 1) * TB]:
                            e.tensor_tensor(o, i, c, OP.mult),
                        deps=[qe] + list(cs_op), tag=tag + "_tc")
        ta = rotbank.read("vector",
                          lambda e, o=t1[:], p=rotbank.tile[:, 0:TB],
                                 s=sin2[:, tb * TB:(tb + 1) * TB]:
                              e.tensor_tensor(o, p, s, OP.mult),
                          tag=tag + "_ta")
        td = seg.b.emit("vector",
                        lambda e, o=stg[:], a=t1[:], c=t2[:]:
                            e.tensor_tensor(o, a, c, OP.add),
                        deps=[ta, tc], tag=tag + "_td")
        st = seg.b.dma("sync",
                       dram_fm[oc * 128:(oc + 1) * 128, tb * TB:(tb + 1) * TB],
                       stg[:], deps=[td], tag=tag + "_st",
                       stream=f"{tag}_st{si}")
        state["last_st"][si] = st
        return st

    return out_fn


def sub_l0_ffn(seg, rv, xv, w, deps, inst):
    b = seg.b
    tag = f"l0f{inst}"
    banks = seg._shared_banks[0:6]
    if not hasattr(seg, "A"):
        seg.A = seg.make_arena("A", 60 * 1024)
    A = seg.A
    g1c, g1_op = seg.const("l0f_g1", w["g1"].reshape(DC, 128).T.copy())
    g2c, g2_op = seg.const("l0f_g2", w["g2"].reshape(DC, 128).T.copy())
    A.reset()
    lnops = emit_layernorm(seg, rv, xv, deps, banks, eps=1e-5,
                           tag=tag + "_ln", out2=rv, alloc=A.alloc)
    A.reset()
    rmops = emit_layernorm(seg, rv, xv, lnops, banks, eps=1e-6,
                           tag=tag + "_r1", rms=True, gain=g1c, gain_op=g1_op,
                           alloc=A.alloc)
    qd = seg.dram(tag + "_q", [D, T], BF16)
    kd = seg.dram(tag + "_k", [D, T], BF16)
    vd = seg.dram(tag + "_v", [T, D], BF16)
    od = seg.dram(tag + "_o", [D, T], BF16)
    wq = seg.din("l0f_wq", w["wq"])
    wk = seg.din("l0f_wk", w["wk"])
    wv = seg.din("l0f_wv", w["wv"])
    wo = seg.din("l0f_wo", w["wo"])
    cosA, sinA = rope_tables()
    cos2, cos_op = seg.const("rope_cos", cosA)
    sin2, sin_op = seg.const("rope_sin", sinA)
    r2t, r2t_op = seg.const("rope_r2t", bf(rot_half_mat()))
    cs_op = cos_op
    wbuf = getattr(seg, "_wbuf", None)
    if wbuf is None:
        wbuf = seg.sb("wbuf", [128, DC * 1024], BF16)
        seg._wbuf = wbuf
    rotb = banks[2]
    A.reset()
    rope_fn = evac_rope_to_dram(seg, tag + "qe", qd, rotb, r2t, r2t_op,
                                cos2, sin2, [cos_op, sin_op], alloc=A.alloc)
    wqv, wq_op = load_weight_bf16(seg, tag + "q", wq, D, D, tile=wbuf,
                                  deps=getattr(seg, "_wbuf_war", []),
                                  stream="wbuf")
    qevs = emit_proj(seg, tag + "q", wqv, wq_op, xv, DC, D,
                     rope_fn, rmops, banks[0:2])
    rope_fn2 = evac_rope_to_dram(seg, tag + "ke", kd, rotb, r2t, r2t_op,
                                 cos2, sin2, [cos_op, sin_op], alloc=A.alloc)
    wkv, wk_op = load_weight_bf16(seg, tag + "k", wk, D, D, tile=wbuf,
                                  deps=qevs, stream="wbuf")
    kevs = emit_proj(seg, tag + "k", wkv, wk_op, xv, DC, D,
                     rope_fn2, rmops, banks[0:2])
    wvv, wv_op = load_weight_bf16(seg, tag + "v", wv, D, D, tile=wbuf,
                                  deps=kevs, stream="wbuf")
    vevs = emit_proj_tok(seg, tag + "v", wvv, wv_op, xv, DC, D, vd, rmops,
                         banks, alloc=A.alloc)
    A.reset()
    aouts = emit_attention(seg, tag + "_at", qd, kd, vd, od,
                           qevs + kevs + vevs, causal=True, scale=1.0 / 8.0,
                           pbs=banks, alloc=A.alloc)
    oldv = b.dma("sync", xv, od.rearrange("(c p) t -> p c t", p=128),
                 deps=aouts, tag=tag + "_oldv")
    wov, wo_op = load_weight_bf16(seg, tag + "o", wo, D, D, tile=wbuf,
                                  deps=vevs, stream="wbuf")

    def resadd(oc, tb, pb, mm):
        ap = rv[:, oc, tb * TB:(tb + 1) * TB]
        return pb.read(
            "vector",
            lambda e, a=ap, p=pb.tile[:, 0:TB]: e.tensor_tensor(
                a, a, p, OP.add),
            tag=tag + "_res")

    hops = emit_proj(seg, tag + "o", wov, wo_op, xv, DC, D, resadd,
                     [oldv], banks)
    seg._wbuf_war = hops
    A.reset()
    r2ops = emit_layernorm(seg, rv, xv, hops, banks, eps=1e-6,
                           tag=tag + "_r2", rms=True, gain=g2c, gain_op=g2_op,
                           alloc=A.alloc)
    A.reset()
    moeops = emit_l0_moe(seg, tag, rv, xv, w, r2ops, banks, alloc=A.alloc,
                         phase_dep=r2ops[-1])
    return moeops


def emit_l0_moe(seg, tag, rv, xv, w, deps, banks, alloc=None, phase_dep=None):
    b = seg.b
    alloc = alloc or seg.sb
    wr = seg.din("l0f_wr", w["w_router"])
    wg = seg.din("l0f_wg", w["wg"])
    wu = seg.din("l0f_wu", w["wu"])
    wd = seg.din("l0f_wd", w["wd"])
    NE = 8
    wrt = seg.sb(tag + "_wrt", [128, DC * 8], BF16)
    wrv = wrt[:].rearrange("p (c m) -> p c m", c=DC)
    wr_op = b.dma("gpsimd", wrv, wr.rearrange("(c p) m -> p c m", p=128),
                  tag=tag + "_wr", stream=tag + "_wrs", wait_all=True)
    logit = alloc(tag + "_lg", [128, 16 * 8], F32)
    lgv = logit[:].rearrange("p (a b) -> p a b", a=16)
    pb_l = banks[0]
    lg_ops = []
    for tt in range(16):
        mms = [(pb_l.tile[:, 0:8], xv[:, kc, tt * 128:(tt + 1) * 128],
                wrv[:, kc, :]) for kc in range(DC)]
        mm = pb_l.mm_acc(mms, deps=list(deps) + [wr_op], tag=tag + "_lmm")
        ev = pb_l.read("vector",
                       lambda e, o=lgv[:, tt, :], p=pb_l.tile[:, 0:8]:
                           e.tensor_copy(o, p),
                       tag=tag + "_lev")
        lg_ops.append(ev)
    comb = alloc(tag + "_cb", [128, 16 * 8], F32)
    cbv = comb[:].rearrange("p (a b) -> p a b", a=16)
    scr = alloc(tag + "_sc", [128, 16 * 4], F32)
    scv = scr[:].rearrange("p (a b) -> p a b", a=16)
    chains = []
    for tt in range(16):
        ch = []
        lg = lgv[:, tt, :]
        cb = cbv[:, tt, :]
        m1 = scv[:, tt, 0:1]
        m2 = scv[:, tt, 1:2]
        dn = scv[:, tt, 2:3]

        nm1 = scv[:, tt, 3:4]

        def s1(d2, lg=lg, m1=m1, tt=tt):
            return b.emit("vector", lambda e: e.reduce_max(m1, lg, AX),
                          deps=d2 + [lg_ops[tt]], tag=tag + "_m1")
        def s1b(d2, m1=m1, nm1=nm1):
            return b.emit("vector", lambda e: e.tensor_scalar(
                nm1, m1, -1.0, None, OP.mult), deps=d2, tag=tag + "_nm1")
        def s2(d2, lg=lg, m1=m1, cb=cb):
            return b.emit("vector", lambda e: e.tensor_scalar(
                cb, lg, m1, None, OP.is_ge),
                deps=d2, tag=tag + "_am")
        def s2b(d2, lg=lg, cb=cb):
            return b.emit("vector", lambda e: e.scalar_tensor_tensor(
                cb, cb, -1e30, lg, OP.mult, OP.add),
                deps=d2, tag=tag + "_msk")
        def s3(d2, cb=cb, m2=m2):
            return b.emit("vector", lambda e: e.reduce_max(m2, cb, AX),
                          deps=d2, tag=tag + "_m2")
        def s4(d2, lg=lg, m2=m2, cb=cb):
            return b.emit("vector", lambda e: e.tensor_scalar(
                cb, lg, m2, None, OP.is_ge),
                deps=d2, tag=tag + "_keep")
        def s5(d2, lg=lg, nm1=nm1):
            return b.emit("scalar", lambda e: e.activation(
                lg, lg, AF.Exp, bias=nm1),
                deps=d2, tag=tag + "_exp")
        def s6(d2, lg=lg, cb=cb):
            return b.emit("vector", lambda e: e.tensor_tensor(
                cb, cb, lg, OP.mult), deps=d2, tag=tag + "_pm")
        def s7(d2, cb=cb, dn=dn):
            return b.emit("vector", lambda e: e.reduce_sum(dn, cb, AX),
                          deps=d2, tag=tag + "_dn")
        def s8(d2, dn=dn):
            return b.emit("vector", lambda e: e.reciprocal(dn, dn),
                          deps=d2, tag=tag + "_rc")
        def s9(d2, cb=cb, dn=dn):
            return b.emit("vector", lambda e: e.tensor_scalar(
                cb, cb, dn, None, OP.mult), deps=d2, tag=tag + "_nw")
        ch += [s1, s1b, s2, s2b, s3, s4, s5, s6, s7, s8, s9]
        chains.append(ch)
    comb_ops = wavefront(chains)
    pb_t = banks[1]
    crow_sb = alloc(tag + "_crs", [1, TB], F32)
    wgt = [alloc(tag + f"_wgt{i}", [128, DC * 128], BF16) for i in range(2)]
    wut = [alloc(tag + f"_wut{i}", [128, DC * 128], BF16) for i in range(2)]
    wdt = [alloc(tag + f"_wdt{i}", [128, HCM * 128], BF16) for i in range(2)]
    act = alloc(tag + "_act", [128, HCM * TB], BF16)
    actv = act[:].rearrange("p (c t) -> p c t", c=HCM)
    cbsb = alloc(tag + "_cbs", [128, TB], F32)
    gs = [alloc(tag + f"_gs{i}", [128, TB], BF16) for i in range(2)]
    pb_g, pb_u, pb_c, pb_d = banks[2], banks[3], banks[4], banks[5]
    wg_last = [phase_dep, phase_dep]
    wu_last = [phase_dep, phase_dep]
    wd_last = [phase_dep, phase_dep]
    act_war = None
    cb_last = None
    rd = None
    for x in range(NE):
        for tb in range(NTB):
            ts = tb * TB
            crow_war = list(pb_t.readers)
            for j in range(4):
                def ftr(e, o=pb_t.tile[0:1, j * 128:(j + 1) * 128],
                        c=cbv[:, tb * 4 + j, x:x + 1], i=seg.identf[:],
                        first=(j == 0), last=(j == 3)):
                    return e.matmul(o, c, i, start=first, stop=last)
                trop = b.emit("tensor", ftr,
                              deps=(comb_ops + crow_war + [seg.identf_op])
                              if j == 0 else [], tag=tag + "_ctr")
            pb_t.writer = trop
            pb_t.readers = []
            crev = pb_t.read("vector",
                             lambda e, o=crow_sb[:], p=pb_t.tile[0:1, 0:TB]:
                                 e.tensor_copy(o, p),
                             deps=[cb_last], tag=tag + "_crev")
            bcast_row(seg, pb_c, crow_sb[:], TB,
                      deps=[crev], tag=tag + "_cbb")
            cb_ev = pb_c.read("vector",
                              lambda e, o=cbsb[:], p=pb_c.tile[:, 0:TB]:
                                  e.tensor_copy(o, p),
                              tag=tag + "_cbe")
            cb_last = cb_ev
            pend = []
            ev_ops = []

            def flush(pend=pend, ev_ops=ev_ops):
                hcp, g_evp, u_evp = pend.pop(0)
                ev = b.emit("vector",
                            lambda e, o=actv[:, hcp, :], g=gs[hcp % 2][:]:
                                e.tensor_tensor(o, o, g, OP.mult),
                            deps=[u_evp, g_evp], tag=tag + "_act")
                ev_ops.append(ev)

            for hc in range(HCM):
                par = hc % 2
                wg_ld = b.dma("gpsimd",
                              wgt[par][:].rearrange("p (c m) -> p c m", c=DC),
                              wg[x].rearrange("(c p) m -> p c m", p=128)
                              [:, :, hc * 128:(hc + 1) * 128],
                              deps=[wg_last[par]], tag=tag + "_wgl",
                              stream=f"{tag}_wg{par}")
                wu_ld = b.dma("gpsimd",
                              wut[par][:].rearrange("p (c m) -> p c m", c=DC),
                              wu[x].rearrange("(c p) m -> p c m", p=128)
                              [:, :, hc * 128:(hc + 1) * 128],
                              deps=[wu_last[par]], tag=tag + "_wul",
                              stream=f"{tag}_wu{par}")
                wgv = wgt[par][:].rearrange("p (c m) -> p c m", c=DC)
                wuv = wut[par][:].rearrange("p (c m) -> p c m", c=DC)
                gmm = pb_g.mm_acc(
                    [(pb_g.tile[:, 0:TB], wgv[:, kc, :],
                      xv[:, kc, ts:ts + TB]) for kc in range(DC)],
                    deps=list(deps) + [wg_ld], tag=tag + "_gmm")
                umm = pb_u.mm_acc(
                    [(pb_u.tile[:, 0:TB], wuv[:, kc, :],
                      xv[:, kc, ts:ts + TB]) for kc in range(DC)],
                    deps=list(deps) + [wu_ld], tag=tag + "_umm")
                wg_last[par] = gmm
                wu_last[par] = umm
                gs_war = ev_ops[hc - 2] if hc >= 2 else None
                g_ev = pb_g.read("scalar",
                                 lambda e, o=gs[par][:], p=pb_g.tile[:, 0:TB]:
                                     e.activation(o, p, AF.Silu),
                                 deps=[gs_war], tag=tag + "_silu")
                u_ev = pb_u.read("vector",
                                 lambda e, o=actv[:, hc, :],
                                        p=pb_u.tile[:, 0:TB]:
                                     e.tensor_tensor(o, p, cbsb[:], OP.mult),
                                 deps=[cb_ev, act_war],
                                 tag=tag + "_ue")
                pend.append((hc, g_ev, u_ev))
                if len(pend) > 1:
                    flush()
            flush()
            ev_last = ev_ops[-1]
            for dc in range(DC):
                par = dc % 2
                wd_ld = b.dma("gpsimd",
                              wdt[par][:].rearrange("p (c m) -> p c m", c=HCM),
                              wd[x].rearrange("(c p) m -> p c m", p=128)
                              [:, :, dc * 128:(dc + 1) * 128],
                              deps=[wd_last[par]], tag=tag + "_wdl",
                              stream=f"{tag}_wd{par}")
                wdv = wdt[par][:].rearrange("p (c m) -> p c m", c=HCM)
                mms = []
                for hc in range(HCM):
                    mdeps = [wd_ld] if hc == 0 else []
                    mms.append((pb_d.tile[:, 0:TB], wdv[:, hc, :],
                                actv[:, hc, :], mdeps))
                dmm = pb_d.mm_acc(mms, deps=[ev_last], tag=tag + "_dmm")
                wd_last[par] = dmm
                act_war = dmm
                rd = pb_d.read("vector",
                               lambda e, a=rv[:, dc, ts:ts + TB],
                                      p=pb_d.tile[:, 0:TB]:
                                   e.tensor_tensor(a, a, p, OP.add),
                               tag=tag + "_racc")
    return [rd]


def emit_softplus(seg, b, out_ap, in_ap, scr1, scr2, deps, tag, bias=None,
                  bias_op=None):
    import os
    one, one_op = seg.constcol(1.0)
    bdeps = [bias_op] if bias_op else []
    if os.environ.get("SOFTPASS"):
        return b.emit("vector", lambda e: e.tensor_copy(out_ap, in_ap),
                      deps=list(deps), tag=tag + "_passcp")
    a1 = b.emit("scalar", lambda e: e.activation(
        scr1[:], in_ap, AF.Abs, bias=bias if bias is not None else 0.0),
        deps=list(deps) + bdeps, tag=tag + "_abs")
    a2 = b.emit("scalar", lambda e: e.activation(scr1[:], scr1[:], AF.Exp,
                                                 scale=-1.0),
                deps=[a1], tag=tag + "_exp")
    a3 = b.emit("scalar", lambda e: e.activation(scr1[:], scr1[:], AF.Ln,
                                                 bias=one[0:128, 0:1]),
                deps=[a2, one_op], tag=tag + "_ln")
    if bias is not None:
        a4 = b.emit("vector", lambda e: e.tensor_scalar(
            scr2[:], in_ap, bias, 0.0, OP.add, OP.max),
            deps=list(deps) + bdeps + [a1], tag=tag + "_max")
    else:
        a4 = b.emit("vector", lambda e: e.tensor_scalar(
            scr2[:], in_ap, 0.0, None, OP.max),
            deps=list(deps) + [a1], tag=tag + "_max")
    a5 = b.emit("vector", lambda e: e.tensor_tensor(out_ap, scr1[:], scr2[:],
                                                    OP.add),
                deps=[a3, a4], tag=tag + "_add")
    return a5


def emit_nb_bcast_build(seg, tag, A, pb_row, pb_bc, sources, nn=16):
    b = seg.b
    brow = A.alloc(tag + "_brow", [1, TB], F32)
    bstg = [A.alloc(tag + f"_bstg{i}", [128, TB], BF16) for i in range(2)]
    last_bst = [None, None]
    bld_evs = []
    cnt = 0
    for n in range(nn):
        for (srcv, dramv, dep_evs, scale_cols, sc_op) in sources:
            for tb in range(NTB):
                war = list(pb_row.readers)
                for j in range(4):
                    def ftr(e, o=pb_row.tile[0:1, j * 128:(j + 1) * 128],
                            c=srcv[:, tb * 4 + j, n:n + 1], i=seg.identf[:],
                            first=(j == 0), last=(j == 3)):
                        return e.matmul(o, c, i, start=first, stop=last)
                    trop = b.emit("tensor", ftr,
                                  deps=(dep_evs + war + [seg.identf_op])
                                  if j == 0 else [], tag=tag + "_btr")
                pb_row.writer = trop
                pb_row.readers = []
                crev = pb_row.read("vector",
                                   lambda e, o=brow[:], p=pb_row.tile[0:1, 0:TB]:
                                       e.tensor_copy(o, p),
                                   tag=tag + "_brev")
                bcast_row(seg, pb_bc, brow[:], TB, deps=[crev],
                          tag=tag + "_bbc")
                si = cnt % 2
                cnt += 1
                if scale_cols is not None:
                    ev = pb_bc.read(
                        "vector",
                        lambda e, o=bstg[si][:], p=pb_bc.tile[:, 0:TB],
                               sc=scale_cols[:, n:n + 1]:
                            e.tensor_scalar(o, p, sc, None, OP.mult),
                        deps=[sc_op, last_bst[si]], tag=tag + "_bsc")
                else:
                    ev = pb_bc.read(
                        "vector",
                        lambda e, o=bstg[si][:], p=pb_bc.tile[:, 0:TB]:
                            e.tensor_copy(o, p),
                        deps=[last_bst[si]], tag=tag + "_bsc")
                st = b.dma("sync", dramv[n, :, tb * TB:(tb + 1) * TB],
                           bstg[si][:], deps=[ev], tag=tag + "_bst",
                           stream=f"{tag}_bb{si}")
                last_bst[si] = st
                bld_evs.append(st)
    return bld_evs


def sub_l1_attn_a(seg, rv, xv, w, deps, upto=99):
    b = seg.b
    tag = "l1aa"
    banks = seg._shared_banks[0:6]
    if not hasattr(seg, "A"):
        seg.A = seg.make_arena("A", 60 * 1024)
    A = seg.A
    n1c, n1_op = seg.const(tag + "_n1", w["norm1_w"].reshape(DC, 128).T.copy())
    n2c, n2_op = seg.const(tag + "_n2", w["norm2_w"].reshape(DC, 128).T.copy())
    Dpc, Dp_op = seg.const(tag + "_dp", w["D_param"].reshape(DC, 128).T.copy())
    scc, sc_op = seg.const(tag + "_sc", w["scale"].reshape(DC, 128).T.copy())
    bdtc, bdt_op = seg.const(tag + "_bdt", w["bdt"].reshape(DC, 128).T.copy())
    b1c, b1_op = seg.const(tag + "_b1", w["b1"].reshape(32, 128).T.copy())
    b2c, b2_op = seg.const(tag + "_b2", w["b2"].reshape(DC, 128).T.copy())
    bBr, bB_op = seg.const(tag + "_bB", bf(w["bB"].reshape(1, 16)))
    bCr, bC_op = seg.const(tag + "_bC", bf(w["bC"].reshape(1, 16)))
    wdt = seg.din(tag + "_wdtT", w["WdtT"])
    wbT = seg.din(tag + "_wbT", w["WBT"])
    wcT = seg.din(tag + "_wcT", w["WCT"])
    w1T = seg.din(tag + "_w1T", w["W1T"])
    w2T = seg.din(tag + "_w2T", w["W2T"])

    dtd = seg.dram(tag + "_dt", [DC, 128, T], BF16)
    ud = seg.dram(tag + "_u", [DC, 128, T], BF16)
    bvb_d = seg.dram(tag + "_bvb", [16, 128, T], BF16)
    cvb_d = seg.dram(tag + "_cvb", [16, 128, T], BF16)
    A.reset()
    lnops = emit_layernorm(seg, rv, xv, deps, banks, eps=1e-5,
                           tag=tag + "_ln", out2=rv, alloc=A.alloc)
    A.reset()
    rmops = emit_layernorm(seg, rv, xv, lnops, banks, eps=1e-6,
                           tag=tag + "_r1", rms=True, gain=n1c, gain_op=n1_op,
                           alloc=A.alloc)
    if upto <= 0:
        return rmops
    A.reset()
    wbuf = getattr(seg, "_wbuf", None)
    if wbuf is None:
        wbuf = seg.sb("wbuf", [128, DC * 1024], BF16)
        seg._wbuf = wbuf
    wv1, wv1_op = load_weight_bf16(seg, tag + "dt", wdt, D, D, tile=wbuf,
                                   deps=getattr(seg, "_wbuf_war", []),
                                   stream="wbuf")
    spA = [A.alloc(tag + f"_spA{i}", [128, TB], F32) for i in range(2)]
    spB = [A.alloc(tag + f"_spB{i}", [128, TB], F32) for i in range(2)]
    rct = [A.alloc(tag + f"_rct{i}", [128, TB], F32) for i in range(2)]
    dtt = [A.alloc(tag + f"_dtt{i}", [128, TB], F32) for i in range(2)]
    stg = [A.alloc(tag + f"_stg{i}", [128, TB], BF16) for i in range(2)]
    stg2 = [A.alloc(tag + f"_stg2{i}", [128, TB], BF16) for i in range(2)]
    last_std = {}

    def dt_evac(oc, tb, pb, mm):
        si = (oc * NTB + tb) % 2
        import os
        if os.environ.get("SIMPLE_DT"):
            d1 = last_std.get(("a", si))
            c0 = pb.read("vector", lambda e, o=stg[si][:], p=pb.tile[:, 0:TB]:
                         e.tensor_copy(o, p), deps=[d1], tag=tag + "_sdc")
            st1 = b.dma("sync", dtd[oc, :, tb * TB:(tb + 1) * TB], stg[si][:],
                        deps=[c0], tag=tag + "_dtst", stream=f"{tag}_dt{si}")
            st2 = b.dma("sync", ud[oc, :, tb * TB:(tb + 1) * TB], stg[si][:],
                        deps=[c0], tag=tag + "_ust", stream=f"{tag}_dt{si}")
            last_std[("a", si)] = st2
            return st2
        d1 = last_std.get(("a", si))
        d2 = last_std.get(("b", si))
        sp = emit_softplus(
            seg, b, dtt[si][:],
            pb.tile[:, 0:TB], spA[si], spB[si],
            [mm, d1, d2], tag + "_sp", bias=bdtc[:, oc:oc + 1],
            bias_op=bdt_op)
        pb.readers.append(sp)
        c1 = b.emit("vector", lambda e, o=stg[si][:], i=dtt[si][:]:
                    e.tensor_copy(o, i), deps=[sp, d1], tag=tag + "_dtc")
        st1 = b.dma("sync", dtd[oc, :, tb * TB:(tb + 1) * TB], stg[si][:],
                    deps=[c1], tag=tag + "_dtst", stream=f"{tag}_dt{si}")
        import os
        if os.environ.get("NORECIP"):
            st2b = b.dma("sync", ud[oc, :, tb * TB:(tb + 1) * TB], stg[si][:],
                         deps=[c1], tag=tag + "_ust", stream=f"{tag}_dt{si}")
            last_std[("a", si)] = st1
            last_std[("b", si)] = st2b
            return st2b
        r1 = b.emit("vector", lambda e, o=rct[si][:], i=dtt[si][:]:
                    e.reciprocal(o, i), deps=[c1], tag=tag + "_rcp")
        u1 = b.emit("vector",
                    lambda e, o=stg2[si][:], a=rct[si][:],
                           x_=xv[:, oc, tb * TB:(tb + 1) * TB]:
                        e.tensor_tensor(o, a, x_, OP.mult),
                    deps=[r1, d2], tag=tag + "_umul")
        st2 = b.dma("sync", ud[oc, :, tb * TB:(tb + 1) * TB], stg2[si][:],
                    deps=[u1], tag=tag + "_ust", stream=f"{tag}_dt{si}")
        last_std[("a", si)] = st1
        last_std[("b", si)] = st2
        return st2

    devs = emit_proj(seg, tag + "dt", wv1, wv1_op, xv, DC, D, dt_evac,
                     rmops, banks[0:2])
    if upto <= 1:
        return devs + rmops
    A.reset()
    wbT_t = seg.sb(tag + "_wbTt", [128, DC * 16], BF16)
    wcT_t = seg.sb(tag + "_wcTt", [128, DC * 16], BF16)
    wbv = wbT_t[:].rearrange("p (c m) -> p c m", c=DC)
    wcv = wcT_t[:].rearrange("p (c m) -> p c m", c=DC)
    wb_op = b.dma("gpsimd", wbv, wbT.rearrange("(c p) m -> p c m", p=128),
                  tag=tag + "_wbl", stream=tag + "_wbs", wait_all=True)
    wc_op = b.dma("gpsimd", wcv, wcT.rearrange("(c p) m -> p c m", p=128),
                  tag=tag + "_wcl", stream=tag + "_wbs", wait_all=True)
    acol_sb = seg.sb(tag + "_acols", [128, 16 * 2], F32)
    arow = seg.sb(tag + "_arow", [1, 16 * 2], F32)
    pb_r = banks[2]
    alogc, alog_op = seg.const(tag + "_alog", w["A_log"].reshape(1, 16).copy())
    a_e = b.emit("scalar", lambda e: e.activation(
        arow[0:1, 0:16], alogc[0:1, 0:16], AF.Exp),
        deps=[alog_op], tag=tag + "_aexp")
    a_n = b.emit("vector", lambda e: e.tensor_scalar(
        arow[0:1, 0:16], arow[0:1, 0:16], -1.0, None, OP.mult),
        deps=[a_e], tag=tag + "_aneg")
    a_r = b.emit("vector", lambda e: e.reciprocal(
        arow[0:1, 16:32], arow[0:1, 0:16]), deps=[a_n], tag=tag + "_arcp")
    bcast_row(seg, pb_r, arow[0:1, 0:32], 32, deps=[a_r], tag=tag + "_abc")
    a_ev = pb_r.read("vector", lambda e: e.tensor_copy(
        acol_sb[:], pb_r.tile[:, 0:32]), tag=tag + "_aev")
    acols = acol_sb[:].rearrange("p (k n) -> p k n", k=2)
    bv_tm = A.alloc(tag + "_bvtm", [128, 16 * 16], F32)
    cv_tm = A.alloc(tag + "_cvtm", [128, 16 * 16], F32)
    bvv = bv_tm[:].rearrange("p (a n) -> p a n", a=16)
    cvv = cv_tm[:].rearrange("p (a n) -> p a n", a=16)
    pb_b = banks[0]
    bc_evs = []
    for tt in range(16):
        for (wvx, rowc, row_op, outv) in ((wbv, bBr, bB_op, bvv),
                                          (wcv, bCr, bC_op, cvv)):
            mms = [(pb_b.tile[:, 0:16], xv[:, kc, tt * 128:(tt + 1) * 128],
                    wvx[:, kc, :]) for kc in range(DC)]
            mms.append((pb_b.tile[:, 0:16], seg.onesr_b[:], rowc[0:1, 0:16],
                        [row_op, seg.onesr_b_op]))
            mm = pb_b.mm_acc(mms, deps=rmops + [wb_op, wc_op],
                             tag=tag + "_bcmm")
            ev = pb_b.read("vector",
                           lambda e, o=outv[:, tt, :], p=pb_b.tile[:, 0:16]:
                               e.tensor_copy(o, p), tag=tag + "_bcev")
            bc_evs.append(ev)
    bld_evs = emit_nb_bcast_build(
        seg, tag, A, banks[3], banks[4],
        [(bvv, bvb_d, bc_evs, acols[:, 1, 0:16], a_ev),
         (cvv, cvb_d, bc_evs, None, None)], nn=16)
    if upto <= 2:
        return bld_evs + [a_ev]
    A.reset()
    dt_sb = [A.alloc(tag + "_dts", [128, T], BF16)] * 2
    u_sb = [A.alloc(tag + "_us", [128, T], BF16)] * 2
    bv_sb = A.alloc(tag + "_bvs", [128, T], BF16)
    cv_sb = A.alloc(tag + "_cvs", [128, T], BF16)
    dA = A.alloc(tag + "_dA", [128, T], F32)
    wt_ = A.alloc(tag + "_w", [128, T], F32)
    ht_ = A.alloc(tag + "_h", [128, T], F32)
    yt_ = A.alloc(tag + "_y", [128, T], F32)
    tt_ = A.alloc(tag + "_t", [128, T], BF16)
    last_dt_use = [[], []]
    last_bv_use = None
    last_cv_use = None
    prev_s1 = None
    prev_y1 = None
    prev_p2 = None
    rv_out_ops = []
    for dc in range(DC):
        dpar = 0
        ld_dt = b.dma("sync", dt_sb[dpar][:], dtd[dc, :, :],
                      deps=list(last_dt_use[dpar]) + (devs if dc < 2 else []),
                      tag=tag + "_lddt", stream=f"{tag}_sdt{dpar}")
        ld_u = b.dma("sync", u_sb[dpar][:], ud[dc, :, :],
                     deps=list(last_dt_use[dpar]) + (devs if dc < 2 else []),
                     tag=tag + "_ldu", stream=f"{tag}_sdt{dpar}")
        yz = b.emit("vector", lambda e: e.memset(yt_[:], 0.0),
                    deps=[prev_p2], tag=tag + "_yz")
        for n in range(16):
            ld_bv = b.dma("sync", bv_sb[:], bvb_d[n, :, :],
                          deps=[last_bv_use] +
                               (bld_evs if dc == 0 else []),
                          tag=tag + "_ldbv", stream=f"{tag}_sbv")
            ld_cv = b.dma("sync", cv_sb[:], cvb_d[n, :, :],
                          deps=[last_cv_use] +
                               (bld_evs if dc == 0 else []),
                          tag=tag + "_ldcv", stream=f"{tag}_scv")
            e1 = b.emit("scalar",
                        lambda e, sc=acols[:, 0, n:n + 1], i=dt_sb[dpar][:]:
                            e.activation(dA[:], i, AF.Exp, scale=sc),
                        deps=[a_ev, ld_dt, prev_s1], tag=tag + "_dA")
            w1_ = b.emit("vector",
                         lambda e, u=u_sb[dpar][:]: e.scalar_tensor_tensor(
                             wt_[:], dA[:], -1.0, u, OP.add, OP.mult),
                         deps=[e1, ld_u], tag=tag + "_wm")
            b1_ = b.emit("vector",
                         lambda e, bvx=bv_sb[:]: e.tensor_tensor(
                             wt_[:], wt_[:], bvx, OP.mult),
                         deps=[w1_, ld_bv], tag=tag + "_bm")
            s1 = b.emit("vector", lambda e: e.tensor_tensor_scan(
                ht_[:], dA[:], wt_[:], 0.0, OP.mult, OP.add),
                deps=[b1_], tag=tag + "_scan")
            m1 = b.emit("vector",
                        lambda e, cvx=cv_sb[:]: e.tensor_tensor(
                            tt_[:], ht_[:], cvx, OP.mult),
                        deps=[s1, ld_cv], tag=tag + "_cm")
            y1 = b.emit("vector", lambda e: e.tensor_tensor(
                yt_[:], yt_[:], tt_[:], OP.add),
                deps=[m1, yz, prev_y1], tag=tag + "_ya")
            prev_y1 = y1
            last_bv_use = b1_
            last_cv_use = m1
            prev_s1 = s1
            last_e1 = e1
            last_w1 = w1_
        last_dt_use[dpar] = [last_e1, last_w1]
        p1 = b.emit("vector",
                    lambda e, dcol=Dpc[:, dc:dc + 1], x_=xv[:, dc, :]:
                        e.scalar_tensor_tensor(yt_[:], x_, dcol, yt_[:],
                                               OP.mult, OP.add),
                    deps=[Dp_op], tag=tag + "_dp")
        p2 = b.emit("vector",
                    lambda e, scol=scc[:, dc:dc + 1], r_=rv[:, dc, :]:
                        e.scalar_tensor_tensor(r_, yt_[:], scol, r_,
                                               OP.mult, OP.add),
                    deps=[p1, sc_op], tag=tag + "_sc")
        prev_p2 = p2
        rv_out_ops.append(p2)
    if upto <= 3:
        return rv_out_ops
    A.reset()
    r2ops = emit_layernorm(seg, rv, xv, rv_out_ops, banks, eps=1e-6,
                           tag=tag + "_r2", rms=True, gain=n2c, gain_op=n2_op,
                           alloc=A.alloc)
    A.reset()
    w1t_ = [A.alloc(tag + f"_w1t{i}", [128, DC * 128], BF16) for i in range(2)]
    w2t_ = [A.alloc(tag + "_w2t", [128, 32 * 128], BF16)] * 2
    h_sb = A.alloc(tag + "_h", [128, 32 * TB], BF16)
    hv = h_sb[:].rearrange("p (c t) -> p c t", c=32)
    gsq = A.alloc(tag + "_gsq", [128, TB], F32)
    gcu = A.alloc(tag + "_gcu", [128, TB], F32)
    gth = A.alloc(tag + "_gth", [128, TB], F32)
    pb_h, pb_o = banks[2], banks[3]
    c_g = float(np.sqrt(2.0 / np.pi))
    w1_last = [r2ops[-1], r2ops[-1]]
    w2_last = [r2ops[-1], r2ops[-1]]
    h_war = None
    for tb in range(NTB):
        ts = tb * TB
        h_evs = []
        for hc in range(32):
            par = hc % 2
            w1ld = b.dma("gpsimd",
                         w1t_[par][:].rearrange("p (c m) -> p c m", c=DC),
                         w1T.rearrange("(c p) m -> p c m", p=128)
                         [:, :, hc * 128:(hc + 1) * 128],
                         deps=[w1_last[par]], tag=tag + "_w1l",
                         stream=f"{tag}_w1{par}")
            w1v = w1t_[par][:].rearrange("p (c m) -> p c m", c=DC)
            mm = pb_h.mm_acc(
                [(pb_h.tile[:, 0:TB], w1v[:, kc, :], xv[:, kc, ts:ts + TB])
                 for kc in range(DC)],
                deps=r2ops + [w1ld], tag=tag + "_hmm")
            w1_last[par] = mm
            g1 = pb_h.read("scalar",
                           lambda e, o=gsq[:], p=pb_h.tile[:, 0:TB],
                                  bc=b1c[:, hc:hc + 1]:
                               e.activation(o, p, AF.Square, bias=bc),
                           deps=[b1_op], tag=tag + "_gsq")
            g2 = pb_h.read("vector",
                           lambda e, o=gcu[:], p=pb_h.tile[:, 0:TB],
                                  bc=b1c[:, hc:hc + 1]:
                               e.tensor_scalar(o, p, bc, None, OP.add),
                           deps=[b1_op, g1], tag=tag + "_gs")
            g3 = b.emit("vector",
                        lambda e: e.scalar_tensor_tensor(
                            gth[:], gsq[:], 0.044715, gcu[:], OP.mult,
                            OP.mult),
                        deps=[g1, g2], tag=tag + "_g3")
            g4 = b.emit("vector", lambda e: e.tensor_tensor(
                gth[:], gth[:], gcu[:], OP.add), deps=[g3], tag=tag + "_g4")
            g5 = b.emit("scalar", lambda e: e.activation(
                gth[:], gth[:], AF.Tanh, scale=c_g), deps=[g4],
                tag=tag + "_g5")
            g6 = b.emit("vector", lambda e: e.scalar_tensor_tensor(
                gth[:], gth[:], 1.0, gcu[:], OP.add, OP.mult),
                deps=[g5], tag=tag + "_g6")
            g8 = b.emit("vector",
                        lambda e, o=hv[:, hc, 0:TB]: e.tensor_scalar(
                            o, gth[:], 0.5, None, OP.mult),
                        deps=[g6, h_war], tag=tag + "_g8")
            h_evs.append(g8)
        for dc in range(DC):
            par = 0
            w2ld = b.dma("gpsimd",
                         w2t_[par][:].rearrange("p (c m) -> p c m", c=32),
                         w2T.rearrange("(c p) m -> p c m", p=128)
                         [:, :, dc * 128:(dc + 1) * 128],
                         deps=[w2_last[par]], tag=tag + "_w2l",
                         stream=f"{tag}_w2{par}")
            w2v = w2t_[par][:].rearrange("p (c m) -> p c m", c=32)
            mms = []
            for hc in range(32):
                mdeps = [w2ld] if hc == 0 else []
                mms.append((pb_o.tile[:, 0:TB], w2v[:, hc, :], hv[:, hc, 0:TB],
                            mdeps))
            dmm = pb_o.mm_acc(mms, deps=[h_evs[-1]], tag=tag + "_omm")
            w2_last[par] = dmm
            h_war = dmm
            rd = pb_o.read("vector",
                           lambda e, a=rv[:, dc, ts:ts + TB],
                                  p=pb_o.tile[:, 0:TB],
                                  bc=b2c[:, dc:dc + 1]:
                               e.scalar_tensor_tensor(a, p, bc, a,
                                                      OP.add, OP.add),
                           deps=[b2_op], tag=tag + "_ores")
            rv_out_ops.append(rd)
    return [rv_out_ops[-1]]


def sub_l1_attn_b(seg, rv, xv, w, deps):
    b = seg.b
    tag = "l1ab"
    banks = seg._shared_banks[0:6]
    if not hasattr(seg, "A"):
        seg.A = seg.make_arena("A", 60 * 1024)
    A = seg.A
    EC = 16
    gmc, gm_op = seg.const(tag + "_gm", w["gamma"].reshape(DC, 128).T.copy())
    btc, bt_op = seg.const(tag + "_bt", w["beta"].reshape(DC, 128).T.copy())
    cwc, cw_op = seg.const(tag + "_cw", w["conv_w"].reshape(EC, 128, 4)
                           .transpose(1, 0, 2).reshape(128, EC * 4).copy())
    cbc, cb_op = seg.const(tag + "_cb", w["conv_b"].reshape(EC, 128).T.copy())
    bdtc, bdt_op = seg.const(tag + "_bdt", w["b_dt"].reshape(EC, 128).T.copy())
    dpc, dp_op = seg.const(tag + "_dpc", w["Dp"].reshape(EC, 128).T.copy())
    alogc, alog_op = seg.const(tag + "_alog", w["A_log"].reshape(EC, 128, 16)
                               .transpose(1, 0, 2).reshape(128, EC * 16).copy())
    w_in = seg.din(tag + "_winT", w["w_inT"])
    w_x = seg.din(tag + "_wxT", w["w_xT"])
    w_dt = seg.din(tag + "_wdtT", w["w_dtT"])
    w_out = seg.din(tag + "_woutT", w["w_outT"])
    xpd = seg.dram(tag + "_xp", [EC, 128, T], BF16)
    zd = seg.dram(tag + "_z", [EC, 128, T], BF16)
    xcd = seg.dram(tag + "_xc", [EC, 128, T], BF16)
    deltad = seg.dram(tag + "_dl", [EC, 128, T], BF16)
    yd = seg.dram(tag + "_y", [EC, 128, T], BF16)
    bvb_d = seg.dram(tag + "_bvb", [16, 128, T], BF16)
    cvb_d = seg.dram(tag + "_cvb", [16, 128, T], BF16)
    A.reset()
    lnops = emit_layernorm(seg, rv, xv, deps, banks, eps=1e-5,
                           tag=tag + "_ln", out2=rv, alloc=A.alloc)
    A.reset()
    l2ops = emit_layernorm(seg, rv, xv, lnops, banks, eps=1e-5,
                           tag=tag + "_l2", gain=gmc, gain_op=gm_op,
                           beta=btc, beta_op=bt_op, alloc=A.alloc)
    A.reset()
    wst = [A.alloc(tag + f"_wst{i}", [128, DC * 128], BF16) for i in range(2)]
    pstg = [A.alloc(tag + f"_pstg{i}", [128, TB], BF16) for i in range(3)]
    pb_p = banks[0]
    w_last = [l2ops[-1], l2ops[-1]]
    st_last = [None] * 3
    p1_sts = []
    scnt = 0
    for oc in range(32):
        par = oc % 2
        wld = b.dma("gpsimd", wst[par][:].rearrange("p (c m) -> p c m", c=DC),
                    w_in.rearrange("(c p) m -> p c m", p=128)
                    [:, :, oc * 128:(oc + 1) * 128],
                    deps=[w_last[par]], tag=tag + "_wl",
                    stream=f"{tag}_w{par}")
        wv_ = wst[par][:].rearrange("p (c m) -> p c m", c=DC)
        for tb in range(NTB):
            mm = pb_p.mm_acc(
                [(pb_p.tile[:, 0:TB], wv_[:, kc, :],
                  xv[:, kc, tb * TB:(tb + 1) * TB]) for kc in range(DC)],
                deps=l2ops + [wld], tag=tag + "_pmm")
            si = scnt % 3
            scnt += 1
            if oc < EC:
                ev = pb_p.read("vector",
                               lambda e, o=pstg[si][:], p=pb_p.tile[:, 0:TB]:
                                   e.tensor_copy(o, p),
                               deps=[st_last[si]], tag=tag + "_pev")
                st = b.dma("sync", xpd[oc, :, tb * TB:(tb + 1) * TB],
                           pstg[si][:], deps=[ev], tag=tag + "_pst",
                           stream=f"{tag}_p{si}")
            else:
                ev = pb_p.read("scalar",
                               lambda e, o=pstg[si][:], p=pb_p.tile[:, 0:TB]:
                                   e.activation(o, p, AF.Silu),
                               deps=[st_last[si]], tag=tag + "_zev")
                st = b.dma("sync", zd[oc - EC, :, tb * TB:(tb + 1) * TB],
                           pstg[si][:], deps=[ev], tag=tag + "_zst",
                           stream=f"{tag}_p{si}")
            st_last[si] = st
            p1_sts.append(st)
        w_last[par] = mm
    A.reset()
    cin = [A.alloc(tag + f"_cin{i}", [128, T + 4], BF16) for i in range(2)]
    cacc = A.alloc(tag + "_cacc", [128, T], F32)
    cstg = [A.alloc(tag + f"_cstg{i}", [128, T], BF16) for i in range(2)]
    cz_ops = [None, None]
    cst_last = [None, None]
    xc_sts = []
    for ec in range(EC):
        par = ec % 2
        z0 = b.emit("vector",
                    lambda e, o=cin[par][:, 0:4]: e.memset(o, 0.0),
                    deps=[cz_ops[par]], tag=tag + "_czero")
        ld = b.dma("sync", cin[par][:, 4:4 + T], xpd[ec, :, :],
                   deps=[cz_ops[par]] + (p1_sts if ec < 2 else []),
                   tag=tag + "_cld", stream=f"{tag}_c{par}")
        c1 = b.emit("vector",
                    lambda e, o=cacc[:], i=cin[par][:, 1:1 + T],
                           sc=cwc[:, ec * 4 + 0:ec * 4 + 1]:
                        e.tensor_scalar(o, i, sc, None, OP.mult),
                    deps=[z0, ld, cw_op, cz_ops[par]], tag=tag + "_c1")
        for j in range(1, 4):
            c1 = b.emit("vector",
                        lambda e, o=cacc[:], i=cin[par][:, 1 + j:1 + j + T],
                               sc=cwc[:, ec * 4 + j:ec * 4 + j + 1]:
                            e.scalar_tensor_tensor(o, i, sc, o, OP.mult,
                                                   OP.add),
                        deps=[c1], tag=tag + "_cj")
        sil = b.emit("scalar",
                     lambda e, o=cstg[par][:], i=cacc[:],
                            bc=cbc[:, ec:ec + 1]:
                         e.activation(o, i, AF.Silu, bias=bc),
                     deps=[c1, cb_op, cst_last[par]], tag=tag + "_csil")
        cz_ops[par] = sil
        st = b.dma("sync", xcd[ec, :, :], cstg[par][:], deps=[sil],
                   tag=tag + "_cst", stream=f"{tag}_cs{par}")
        cst_last[par] = st
        xc_sts.append(st)
    A.reset()
    wxt = seg.sb(tag + "_wxt", [128, EC * 96], BF16)
    wxv = wxt[:].rearrange("p (c m) -> p c m", c=EC)
    wx_op = b.dma("gpsimd", wxv, w_x.rearrange("(c p) m -> p c m", p=128),
                  tag=tag + "_wxl", stream=tag + "_wxs", wait_all=True)
    xcb = A.alloc(tag + "_xcb", [128, EC * TB], BF16)
    xcbv = xcb[:].rearrange("p (c t) -> p c t", c=EC)
    dt_sb = seg.sb(tag + "_dtr", [64, T], BF16)
    bv_tm = A.alloc(tag + "_bvtm", [128, 16 * 16], F32)
    cv_tm = A.alloc(tag + "_cvtm", [128, 16 * 16], F32)
    bvv = bv_tm[:].rearrange("p (a n) -> p a n", a=16)
    cvv = cv_tm[:].rearrange("p (a n) -> p a n", a=16)
    pb_d = banks[0]
    pb_t2 = banks[1]
    xc_use = None
    bc_evs = []
    dt_evs = []
    for tb in range(NTB):
        lds = []
        for kc in range(EC):
            ld = b.dma("sync", xcbv[:, kc, :],
                       xcd[kc, :, tb * TB:(tb + 1) * TB],
                       deps=[xc_use] + (xc_sts if tb == 0 else []),
                       tag=tag + "_xld", stream=f"{tag}_xl{kc % 2}")
            lds.append(ld)
        mm = pb_d.mm_acc(
            [(pb_d.tile[0:64, 0:TB], wxv[:, kc, 0:64], xcbv[:, kc, :],
              [lds[kc]]) for kc in range(EC)],
            deps=[wx_op], tag=tag + "_dmm")
        ev = pb_d.read("vector",
                       lambda e, o=dt_sb[:, tb * TB:(tb + 1) * TB],
                              p=pb_d.tile[0:64, 0:TB]: e.tensor_copy(o, p),
                       tag=tag + "_dev")
        dt_evs.append(ev)
        for tt4 in range(4):
            tt = tb * 4 + tt4
            for (off, outv) in ((64, bvv), (80, cvv)):
                mms = [(pb_t2.tile[:, 0:16],
                        xcbv[:, kc, tt4 * 128:(tt4 + 1) * 128],
                        wxv[:, kc, off:off + 16], [lds[kc]])
                       for kc in range(EC)]
                mmb = pb_t2.mm_acc(mms, deps=[wx_op], tag=tag + "_bvmm")
                evb = pb_t2.read(
                    "vector",
                    lambda e, o=outv[:, tt, :], p=pb_t2.tile[:, 0:16]:
                        e.tensor_copy(o, p), tag=tag + "_bvev")
                bc_evs.append(evb)
        xc_use = mmb
    wdt_t = seg.sb(tag + "_wdtt", [64, T], BF16)
    wdt_v = wdt_t[:].rearrange("p (c m) -> p c m", c=EC)
    wdt_op = b.dma("gpsimd", wdt_v,
                   w_dt.rearrange("p (c m) -> p c m", c=EC),
                   tag=tag + "_wdtl", stream=tag + "_wdts", wait_all=True)
    spA = [A.alloc(tag + f"_spA{i}", [128, TB], F32) for i in range(2)]
    spB = [A.alloc(tag + f"_spB{i}", [128, TB], F32) for i in range(2)]
    dstg = [A.alloc(tag + f"_dstg{i}", [128, TB], BF16) for i in range(2)]
    pb_sp = banks[2]
    dl_last = [None, None]
    dl_sts = []
    for ec in range(EC):
        for tb in range(NTB):
            si = (ec * NTB + tb) % 2
            mm = pb_sp.mm_acc(
                [(pb_sp.tile[:, 0:TB], wdt_v[:, ec, :],
                  dt_sb[:, tb * TB:(tb + 1) * TB],
                  [wdt_op, dt_evs[tb]])],
                tag=tag + "_spmm")
            sp = emit_softplus(seg, b, dstg[si][:], pb_sp.tile[:, 0:TB],
                               spA[si], spB[si], [mm, dl_last[si]],
                               tag + f"_sp{ec}_{tb}", bias=bdtc[:, ec:ec + 1],
                               bias_op=bdt_op)
            pb_sp.readers.append(sp)
            st = b.dma("sync", deltad[ec, :, tb * TB:(tb + 1) * TB],
                       dstg[si][:], deps=[sp], tag=tag + "_dlst",
                       stream=f"{tag}_dl{si}")
            dl_last[si] = st
            dl_sts.append(st)
    bld_evs = emit_nb_bcast_build(
        seg, tag, A, banks[3], banks[4],
        [(bvv, bvb_d, bc_evs, None, None),
         (cvv, cvb_d, bc_evs, None, None)], nn=16)
    acl = seg.sb(tag + "_acl", [128, EC * 16], F32)
    a_e = b.emit("scalar", lambda e: e.activation(acl[:], alogc[:], AF.Exp,
                                                  scale=-1.0),
                 deps=[alog_op], tag=tag + "_aexp")
    a_n = b.emit("vector", lambda e: e.tensor_scalar(
        acl[:], acl[:], -1.0, None, OP.mult), deps=[a_e], tag=tag + "_aneg")
    A.reset()
    dl_sb = A.alloc(tag + "_dls", [128, T], BF16)
    xc_sb = A.alloc(tag + "_xcs", [128, T], BF16)
    u2_sb = A.alloc(tag + "_u2", [128, T], BF16)
    zg_sb = A.alloc(tag + "_zg", [128, T], BF16)
    bv_sb = A.alloc(tag + "_bvs", [128, T], BF16)
    cv_sb = A.alloc(tag + "_cvs", [128, T], BF16)
    dA = A.alloc(tag + "_dA", [128, T], F32)
    wt_ = A.alloc(tag + "_w", [128, T], F32)
    ht_ = A.alloc(tag + "_h", [128, T], BF16)
    yt_ = A.alloc(tag + "_y", [128, T], F32)
    tt_ = A.alloc(tag + "_t", [128, T], BF16)
    ystg = A.alloc(tag + "_ystg", [128, T], BF16)
    dl_use = []
    xc_use2 = None
    zg_use = None
    bv_use = None
    cv_use = None
    prev_s1 = None
    prev_y1 = None
    prev_yst = None
    y_sts = []
    for ec in range(EC):
        ld_dl = b.dma("sync", dl_sb[:], deltad[ec, :, :],
                      deps=list(dl_use) + (dl_sts if ec == 0 else []),
                      tag=tag + "_lddl", stream=f"{tag}_s1")
        ld_xc = b.dma("sync", xc_sb[:], xcd[ec, :, :],
                      deps=[xc_use2] + (xc_sts if ec == 0 else []),
                      tag=tag + "_ldxc", stream=f"{tag}_s2")
        ld_zg = b.dma("sync", zg_sb[:], zd[ec, :, :],
                      deps=[zg_use] + (p1_sts if ec == 0 else []),
                      tag=tag + "_ldzg", stream=f"{tag}_s3")
        u2 = b.emit("vector", lambda e: e.tensor_tensor(
            u2_sb[:], dl_sb[:], xc_sb[:], OP.mult),
            deps=[ld_dl, ld_xc, prev_s1], tag=tag + "_u2")
        yz = b.emit("vector", lambda e: e.memset(yt_[:], 0.0),
                    deps=[prev_yst], tag=tag + "_yz")
        for n in range(16):
            ld_bv = b.dma("sync", bv_sb[:], bvb_d[n, :, :],
                          deps=[bv_use] + (bld_evs if ec == 0 else []),
                          tag=tag + "_ldbv", stream=f"{tag}_s4")
            ld_cv = b.dma("sync", cv_sb[:], cvb_d[n, :, :],
                          deps=[cv_use] + (bld_evs if ec == 0 else []),
                          tag=tag + "_ldcv", stream=f"{tag}_s5")
            e1 = b.emit("scalar",
                        lambda e, sc=acl[:, ec * 16 + n:ec * 16 + n + 1]:
                            e.activation(dA[:], dl_sb[:], AF.Exp, scale=sc),
                        deps=[a_n, ld_dl, prev_s1], tag=tag + "_dA")
            b1_ = b.emit("vector",
                         lambda e: e.tensor_tensor(wt_[:], u2_sb[:], bv_sb[:],
                                                   OP.mult),
                         deps=[u2, ld_bv, prev_s1], tag=tag + "_bm")
            s1 = b.emit("vector", lambda e: e.tensor_tensor_scan(
                ht_[:], dA[:], wt_[:], 0.0, OP.mult, OP.add),
                deps=[b1_, e1], tag=tag + "_scan")
            m1 = b.emit("vector",
                        lambda e: e.tensor_tensor(tt_[:], ht_[:], cv_sb[:],
                                                  OP.mult),
                        deps=[s1, ld_cv], tag=tag + "_cm")
            y1 = b.emit("vector", lambda e: e.tensor_tensor(
                yt_[:], yt_[:], tt_[:], OP.add),
                deps=[m1, yz, prev_y1], tag=tag + "_ya")
            bv_use = b1_
            cv_use = m1
            prev_s1 = s1
            prev_y1 = y1
        dl_use = [e1, u2]
        xc_use2 = None
        p1 = b.emit("vector",
                    lambda e, dcol=dpc[:, ec:ec + 1]:
                        e.scalar_tensor_tensor(yt_[:], xc_sb[:], dcol, yt_[:],
                                               OP.mult, OP.add),
                    deps=[dp_op, y1, ld_xc], tag=tag + "_dp")
        p2 = b.emit("vector",
                    lambda e: e.tensor_tensor(ystg[:], yt_[:], zg_sb[:],
                                              OP.mult),
                    deps=[p1, ld_zg, prev_yst], tag=tag + "_zm")
        zg_use = p2
        xc_use2 = p1
        yst = b.dma("sync", yd[ec, :, :], ystg[:], deps=[p2],
                    tag=tag + "_yst", stream=f"{tag}_s6")
        prev_yst = yst
        y_sts.append(yst)
    A.reset()
    ycb = A.alloc(tag + "_ycb", [128, EC * TB], BF16)
    ycbv = ycb[:].rearrange("p (c t) -> p c t", c=EC)
    wot = [A.alloc(tag + f"_wot{i}", [128, EC * 128], BF16) for i in range(2)]
    pb_o = banks[5]
    wo_last = [y_sts[-1], y_sts[-1]]
    y_use = None
    out_ops = []
    for tb in range(NTB):
        lds = []
        for kc in range(EC):
            ld = b.dma("sync", ycbv[:, kc, :],
                       yd[kc, :, tb * TB:(tb + 1) * TB],
                       deps=[y_use] + (y_sts if tb == 0 else []),
                       tag=tag + "_yld", stream=f"{tag}_yl{kc % 2}")
            lds.append(ld)
        for oc in range(DC):
            par = oc % 2
            wld = b.dma("gpsimd",
                        wot[par][:].rearrange("p (c m) -> p c m", c=EC),
                        w_out.rearrange("(c p) m -> p c m", p=128)
                        [:, :, oc * 128:(oc + 1) * 128],
                        deps=[wo_last[par]], tag=tag + "_wol",
                        stream=f"{tag}_wo{par}")
            wov = wot[par][:].rearrange("p (c m) -> p c m", c=EC)
            mms = [(pb_o.tile[:, 0:TB], wov[:, kc, :], ycbv[:, kc, :],
                    [lds[kc]] + ([wld] if kc == 0 else []))
                   for kc in range(EC)]
            mm = pb_o.mm_acc(mms, tag=tag + "_omm")
            wo_last[par] = mm
            y_use = mm
            rd = pb_o.read("vector",
                           lambda e, a=rv[:, oc, tb * TB:(tb + 1) * TB],
                                  p=pb_o.tile[:, 0:TB]:
                               e.tensor_tensor(a, a, p, OP.add),
                           tag=tag + "_ores")
            out_ops.append(rd)
    return [out_ops[-1]]


NE2 = 64
CAP = 512


def sub_l1_ffn_a(seg, rv, xv, w, deps):
    b = seg.b
    tag = "l1fa"
    banks = seg._shared_banks[0:6]
    if not hasattr(seg, "A"):
        seg.A = seg.make_arena("A", 60 * 1024)
    A = seg.A
    sguw = seg.din(tag + "_sguw", w["shared_gate_up_w"])
    sdw = seg.din(tag + "_sdw", w["shared_down_w"])
    gw = seg.din(tag + "_gw", w["gate_w"])
    shactd = seg.dram(tag + "_sha", [16, 128, T], BF16)
    A.reset()
    lnops = emit_layernorm(seg, rv, xv, deps, banks, eps=1e-5,
                           tag=tag + "_ln", alloc=A.alloc)
    A.reset()
    wg_t = [A.alloc(tag + f"_wg{i}", [128, DC * 128], BF16) for i in range(2)]
    wu_t = [A.alloc(tag + f"_wu{i}", [128, DC * 128], BF16) for i in range(2)]
    gs_t = [A.alloc(tag + f"_gs{i}", [128, TB], BF16) for i in range(2)]
    stg = [A.alloc(tag + f"_stg{i}", [128, TB], BF16) for i in range(3)]
    pb_g, pb_u = banks[0], banks[1]
    wlast = [lnops[-1]] * 2
    ulast = [lnops[-1]] * 2
    slast = [None] * 3
    gprev = [None, None]
    sh_sts = []
    scnt = 0
    for i in range(16):
        par = i % 2
        wld = b.dma("gpsimd", wg_t[par][:].rearrange("p (c m) -> p c m", c=DC),
                    sguw.rearrange("(c p) m -> p c m", p=128)
                    [:, :, i * 128:(i + 1) * 128],
                    deps=[wlast[par]], tag=tag + "_wgl",
                    stream=f"{tag}_wg{par}")
        uld = b.dma("gpsimd", wu_t[par][:].rearrange("p (c m) -> p c m", c=DC),
                    sguw.rearrange("(c p) m -> p c m", p=128)
                    [:, :, 2048 + i * 128:2048 + (i + 1) * 128],
                    deps=[ulast[par]], tag=tag + "_wul",
                    stream=f"{tag}_wu{par}")
        wgv = wg_t[par][:].rearrange("p (c m) -> p c m", c=DC)
        wuv = wu_t[par][:].rearrange("p (c m) -> p c m", c=DC)
        for tb in range(NTB):
            ts = tb * TB
            gmm = pb_g.mm_acc(
                [(pb_g.tile[:, 0:TB], wgv[:, kc, :], xv[:, kc, ts:ts + TB])
                 for kc in range(DC)], deps=lnops + [wld], tag=tag + "_gmm")
            umm = pb_u.mm_acc(
                [(pb_u.tile[:, 0:TB], wuv[:, kc, :], xv[:, kc, ts:ts + TB])
                 for kc in range(DC)], deps=lnops + [uld], tag=tag + "_umm")
            g_ev = pb_g.read("scalar",
                             lambda e, o=gs_t[par][:], p=pb_g.tile[:, 0:TB]:
                                 e.activation(o, p, AF.Silu),
                             deps=[gprev[par]], tag=tag + "_sil")
            si = scnt % 3
            scnt += 1
            u_ev = pb_u.read("vector",
                             lambda e, o=stg[si][:], p=pb_u.tile[:, 0:TB],
                                    g=gs_t[par][:]:
                                 e.tensor_tensor(o, p, g, OP.mult),
                             deps=[g_ev, slast[si]], tag=tag + "_act")
            gprev[par] = u_ev
            st = b.dma("sync", shactd[i, :, ts:ts + TB], stg[si][:],
                       deps=[u_ev], tag=tag + "_shst", stream=f"{tag}_sh{si}")
            slast[si] = st
            sh_sts.append(st)
        wlast[par] = gmm
        ulast[par] = umm
    sgt, sg_op = seg.const(tag + "_sgt",
                           bf(w["shared_gate_w"].reshape(DC, 128).T))
    grow = seg.sb(tag + "_grow", [1, T], F32)
    pb_r = banks[2]
    for tb in range(NTB):
        ts = tb * TB
        mm = pb_r.mm_acc(
            [(pb_r.tile[0:1, 0:TB], sgt[:, kc:kc + 1], xv[:, kc, ts:ts + TB])
             for kc in range(DC)], deps=lnops + [sg_op], tag=tag + "_grm")
        ev = pb_r.read("scalar",
                       lambda e, o=grow[0:1, ts:ts + TB],
                              p=pb_r.tile[0:1, 0:TB]:
                           e.activation(o, p, AF.Sigmoid),
                       tag=tag + "_sig")
    grow_ev = ev
    A.reset()
    shb = A.alloc(tag + "_shb", [128, EC16 * TB], BF16) if False else \
        A.alloc(tag + "_shb", [128, 16 * TB], BF16)
    shbv = shb[:].rearrange("p (c t) -> p c t", c=16)
    wd_t = [A.alloc(tag + f"_wd{i}", [128, 16 * 128], BF16) for i in range(2)]
    gb_sb = A.alloc(tag + "_gb", [128, TB], F32)
    tsc = A.alloc(tag + "_tsc", [128, TB], F32)
    pb_d = banks[3]
    pb_b = banks[4]
    sh_use = None
    wdlast = [sh_sts[-1]] * 2
    gb_last = None
    t2_prev = None
    out_ops = []
    for tb in range(NTB):
        ts = tb * TB
        lds = []
        for kc in range(16):
            ld = b.dma("sync", shbv[:, kc, :],
                       shactd[kc, :, ts:ts + TB],
                       deps=[sh_use] + (sh_sts if tb == 0 else []),
                       tag=tag + "_shld", stream=f"{tag}_sl{kc % 2}")
            lds.append(ld)
        bcast_row(seg, pb_b, grow[0:1, ts:ts + TB], TB,
                  deps=[grow_ev, gb_last], tag=tag + "_gbb")
        gb_ev = pb_b.read("vector",
                          lambda e, o=gb_sb[:], p=pb_b.tile[:, 0:TB]:
                              e.tensor_copy(o, p), tag=tag + "_gbe")
        gb_last = gb_ev
        for oc in range(DC):
            par = oc % 2
            wld = b.dma("gpsimd",
                        wd_t[par][:].rearrange("p (c m) -> p c m", c=16),
                        sdw.rearrange("(c p) m -> p c m", p=128)
                        [:, :, oc * 128:(oc + 1) * 128],
                        deps=[wdlast[par]], tag=tag + "_wdl",
                        stream=f"{tag}_wd{par}")
            wdv = wd_t[par][:].rearrange("p (c m) -> p c m", c=16)
            mms = [(pb_d.tile[:, 0:TB], wdv[:, kc, :], shbv[:, kc, :],
                    [lds[kc]] + ([wld] if kc == 0 else []))
                   for kc in range(16)]
            mm = pb_d.mm_acc(mms, tag=tag + "_dmm")
            wdlast[par] = mm
            sh_use = mm
            t1 = pb_d.read("vector",
                           lambda e, o=tsc[:], p=pb_d.tile[:, 0:TB]:
                               e.tensor_tensor(o, p, gb_sb[:], OP.mult),
                           deps=[gb_ev, t2_prev], tag=tag + "_shm")
            t2 = b.emit("vector",
                        lambda e, a=rv[:, oc, ts:ts + TB], i=tsc[:]:
                            e.tensor_tensor(a, a, i, OP.add),
                        deps=[t1], tag=tag + "_shr")
            t2_prev = t2
            out_ops.append(t2)
    gwt = seg.sb(tag + "_gwt", [128, DC * 64], BF16)
    gwv = gwt[:].rearrange("p (c m) -> p c m", c=DC)
    gw_op = b.dma("gpsimd", gwv, gw.rearrange("(c p) m -> p c m", p=128),
                  tag=tag + "_gwl", stream=tag + "_gws", wait_all=True)
    lgt = seg.sb(tag + "_lg", [128, 16 * 64], F32)
    lgv = lgt[:].rearrange("p (a b) -> p a b", a=16)
    wsb = seg.sb(tag + "_wsb", [128, 16 * 64], F32)
    wsv = wsb[:].rearrange("p (a b) -> p a b", a=16)
    scr = seg.sb(tag + "_scr", [128, 16 * 4], F32)
    scv = scr[:].rearrange("p (a b) -> p a b", a=16)
    pb_l = banks[5]
    lg_ops = []
    for tt in range(16):
        mms = [(pb_l.tile[:, 0:64], xv[:, kc, tt * 128:(tt + 1) * 128],
                gwv[:, kc, :]) for kc in range(DC)]
        mm = pb_l.mm_acc(mms, deps=lnops + [gw_op], tag=tag + "_lmm")
        ev = pb_l.read("vector",
                       lambda e, o=lgv[:, tt, :], p=pb_l.tile[:, 0:64]:
                           e.tensor_copy(o, p), tag=tag + "_lev")
        lg_ops.append(ev)
    selt = seg.sb(tag + "_sel", [128, 16 * 64], F32)
    seg._l1fa_selt = selt
    selv = selt[:].rearrange("p (a b) -> p a b", a=16)
    chains = []
    for tt in range(16):
        ch = []
        lg = lgv[:, tt, :]
        ws_ = wsv[:, tt, :]
        m1 = scv[:, tt, 0:1]
        mk = scv[:, tt, 1:2]
        dn = scv[:, tt, 2:3]
        nm1 = scv[:, tt, 3:4]

        def s1(d2, lg=lg, m1=m1, ws_=ws_, tt=tt):
            return b.emit("vector", lambda e: e.reduce_max(m1, lg, AX),
                          deps=d2 + [lg_ops[tt]], tag=tag + "_m1")
        ch.append(s1)

        def s1b(d2, m1=m1, nm1=nm1):
            return b.emit("vector", lambda e: e.tensor_scalar(
                nm1, m1, -1.0, None, OP.mult), deps=d2, tag=tag + "_nm1")
        ch.append(s1b)

        def s1c(d2, lg=lg, ws_=ws_):
            return b.emit("vector", lambda e: e.tensor_copy(ws_, lg),
                          deps=d2, tag=tag + "_cp")
        ch.append(s1c)

        mkscr = selv[:, tt, :]

        def mk_round(r):
            def sa(d2, ws_=ws_, mk=mk, m1=m1, r=r, mkscr=mkscr):
                src = m1 if r == 0 else mk
                return b.emit("vector", lambda e: e.tensor_scalar(
                    mkscr, ws_, src, None, OP.is_ge), deps=d2, tag=tag + "_ge")

            def sb_(d2, ws_=ws_, mkscr=mkscr):
                return b.emit("vector", lambda e: e.scalar_tensor_tensor(
                    ws_, mkscr, -1e30, ws_, OP.mult, OP.add),
                    deps=d2, tag=tag + "_msk")

            def sc_(d2, ws_=ws_, mk=mk):
                return b.emit("vector", lambda e: e.reduce_max(mk, ws_, AX),
                              deps=d2, tag=tag + "_mk")
            return [sa, sb_, sc_]

        for r in range(7):
            ch += mk_round(r)

        def s5(d2, lg=lg, nm1=nm1):
            return b.emit("scalar", lambda e: e.activation(
                lg, lg, AF.Exp, bias=nm1), deps=d2, tag=tag + "_exp")
        ch.append(s5)

        def s6(d2, lg=lg, ws_=ws_, mk=mk, dn=dn, nm1=nm1):
            return b.emit("scalar", lambda e: e.activation(
                dn, mk, AF.Exp, bias=nm1), deps=d2, tag=tag + "_em8")
        ch.append(s6)

        def s7(d2, lg=lg, ws_=ws_, dn=dn):
            return b.emit("vector", lambda e: e.tensor_scalar(
                ws_, lg, dn, None, OP.is_ge), deps=d2, tag=tag + "_keep")
        ch.append(s7)

        def s8(d2, lg=lg, ws_=ws_):
            return b.emit("vector", lambda e: e.tensor_tensor(
                ws_, ws_, lg, OP.mult), deps=d2, tag=tag + "_pm")
        ch.append(s8)

        def s9(d2, ws_=ws_, dn=dn):
            return b.emit("vector", lambda e: e.reduce_sum(dn, ws_, AX),
                          deps=d2, tag=tag + "_dn")
        ch.append(s9)

        def s10(d2, dn=dn):
            return b.emit("vector", lambda e: e.reciprocal(dn, dn),
                          deps=d2, tag=tag + "_rc")
        ch.append(s10)

        def s11(d2, ws_=ws_, dn=dn):
            return b.emit("vector", lambda e: e.tensor_scalar(
                ws_, ws_, dn, None, OP.mult), deps=d2, tag=tag + "_nw")
        ch.append(s11)
        chains.append(ch)
    w_ops = wavefront(chains)
    sel_ops = []
    for tt in range(16):
        op = b.emit("vector",
                    lambda e, o=selv[:, tt, :], i=wsv[:, tt, :]:
                        e.tensor_scalar(o, i, 0.0, None, OP.is_gt),
                    deps=[w_ops[tt]], tag=tag + "_sel")
        sel_ops.append(op)
    pb_c = banks[0]
    cmms = [(pb_c.tile[0:64, 0:1], selv[:, tt, :], seg.onesc[:],
             [sel_ops[tt], seg.onesc_op]) for tt in range(16)]
    cmm = pb_c.mm_acc(cmms, tag=tag + "_cnt")
    counts_sb = seg.sb(tag + "_cnts", [64, 1], F32)
    cnt_ev = pb_c.read("vector",
                       lambda e: e.tensor_copy(counts_sb[:], pb_c.tile[0:64, 0:1]),
                       tag=tag + "_cev")
    return out_ops, wsv, sel_ops + [cnt_ev] + w_ops, counts_sb, cnt_ev


def sub_l1_ffn_b(seg, rv, xv, w, deps, wsv, selv, w_deps, counts_all,
                 pmask_ap, cin_ops):
    b = seg.b
    tag = "l1fb"
    banks = seg._shared_banks[0:6]
    A = seg.A
    egu = seg.din(tag + "_egu", w["expert_gate_up_w"])
    edw = seg.din(tag + "_edw", w["expert_down_w"])
    triu_c, triu_op = seg.const("triu128", np.triu(np.ones((128, 128),
                                                           np.float32), 1))
    scrp = seg.sb(tag + "_scrp", [128, 64], F32)
    pb_a, pb_b2 = banks[0], banks[1]
    pb_a.mm_acc([(pb_a.tile[0:64, 0:1], counts_all, pmask_ap,
                  list(cin_ops))], tag=tag + "_car")
    carrycol = seg.sb(tag + "_ccol", [64, 1], F32)
    cc_ev = pb_a.read("vector",
                      lambda e: e.tensor_copy(carrycol[:], pb_a.tile[0:64, 0:1]),
                      tag=tag + "_ccev")
    carry_row = seg.sb(tag + "_crow", [1, 64], F32)
    colsum_sb = seg.sb(tag + "_cols", [64, 1], F32)
    pb_b2.mm_acc([(pb_b2.tile[0:1, 0:64], carrycol[:],
                   seg.identf[0:64, 0:64], [cc_ev, seg.identf_op])],
                 tag=tag + "_ctr")
    cr_ev = pb_b2.read("vector",
                       lambda e: e.tensor_copy(carry_row[:], pb_b2.tile[0:1, 0:64]),
                       tag=tag + "_crev")
    kept_ops = []
    prev_kept = None
    prev_cs = cr_ev
    for tt in range(16):
        pb_a.mm_acc(
            [(pb_a.tile[:, 0:64], triu_c[:], selv[:, tt, :],
              [triu_op] + list(w_deps)),
             (pb_a.tile[:, 0:64], seg._onesr[:], carry_row[:],
              [prev_cs, seg._onesr_op])],
            tag=tag + "_pos")
        k1 = pb_a.read("vector",
                       lambda e, o=scrp[:], p=pb_a.tile[:, 0:64]:
                           e.tensor_scalar(o, p, float(CAP), None, OP.is_lt),
                       deps=[prev_kept], tag=tag + "_lt")
        k2 = b.emit("vector",
                    lambda e, o=wsv[:, tt, :]:
                        e.tensor_tensor(o, o, scrp[:], OP.mult),
                    deps=[k1], tag=tag + "_keep")
        prev_kept = k2
        kept_ops.append(k2)
        pb_a.mm_acc(
            [(pb_a.tile[0:64, 0:1], selv[:, tt, :], seg.onesc[:],
              [seg.onesc_op] + list(w_deps))],
            tag=tag + "_cs")
        cs_ev = pb_a.read("vector",
                          lambda e: e.tensor_copy(colsum_sb[:],
                                                  pb_a.tile[0:64, 0:1]),
                          tag=tag + "_csev")
        pb_b2.mm_acc([(pb_b2.tile[0:1, 0:64], colsum_sb[:],
                       seg.identf[0:64, 0:64], [cs_ev, seg.identf_op]),
                      (pb_b2.tile[0:1, 0:64], seg._onesr[0:1, 0:1],
                       carry_row[:], [prev_cs])],
                     tag=tag + "_ctr2")
        prev_cs = pb_b2.read("vector",
                             lambda e: e.tensor_copy(carry_row[:],
                                                     pb_b2.tile[0:1, 0:64]),
                             tag=tag + "_crev2")
    A.reset()
    wgu_t = [A.alloc(tag + f"_wgu{i}", [128, DC * 1024], BF16)
             for i in range(2)]
    wdw_t = [A.alloc(tag + "_wdw", [128, 4 * 1024], BF16)] * 2
    act = A.alloc(tag + "_act", [128, 4 * TB], BF16)
    actv = act[:].rearrange("p (c t) -> p c t", c=4)
    gs_t = A.alloc(tag + "_gs", [128, TB], BF16)
    ts1 = A.alloc(tag + "_ts1", [128, TB], F32)
    wb_sb = A.alloc(tag + "_wb", [128, NTB * TB], F32)
    wbv = wb_sb[:].rearrange("p (a t) -> p a t", a=NTB)
    wrow = A.alloc(tag + "_wrow", [1, TB], F32)
    pb_g, pb_u, pb_d, pb_w, pb_w2 = (banks[0], banks[1], banks[2], banks[3],
                                     banks[4])
    wgu_last = [kept_ops[-1]] * 2
    wdw_last = [kept_ops[-1]]
    gs_war = None
    act_war = None
    a2_prev = None
    wb_war = None
    wrow_war = None
    out_ops = []
    for x in range(NE2):
        par = x % 2
        wld = b.dma("gpsimd", wgu_t[par][:].rearrange("p (c m) -> p c m", c=DC),
                    egu[x].rearrange("(c p) m -> p c m", p=128),
                    deps=[wgu_last[par]], tag=tag + "_wgul",
                    stream=f"{tag}_wgu{par}")
        dld = b.dma("gpsimd", wdw_t[par][:].rearrange("p (c m) -> p c m", c=4),
                    edw[x].rearrange("(c p) m -> p c m", p=128),
                    deps=list(wdw_last), tag=tag + "_wdwl",
                    stream=f"{tag}_wdw")
        wguv = wgu_t[par][:].rearrange("p (c m) -> p c m", c=DC)
        wdwv = wdw_t[par][:].rearrange("p (c m) -> p c m", c=4)
        wb_evs = []
        for tb in range(NTB):
            war = list(pb_w.readers)
            for j in range(4):
                def ftr(e, o=pb_w.tile[0:1, j * 128:(j + 1) * 128],
                        c=wsv[:, tb * 4 + j, x:x + 1], i=seg.identf[:],
                        first=(j == 0), last=(j == 3)):
                    return e.matmul(o, c, i, start=first, stop=last)
                trop = b.emit("tensor", ftr,
                              deps=(kept_ops + war + [seg.identf_op])
                              if j == 0 else [], tag=tag + "_wtr")
            pb_w.writer = trop
            pb_w.readers = []
            wr_ev = pb_w.read("vector",
                              lambda e: e.tensor_copy(wrow[:],
                                                      pb_w.tile[0:1, 0:TB]),
                              deps=[wrow_war], tag=tag + "_wrev")
            wrow_war = bcast_row(seg, pb_w2, wrow[:], TB, deps=[wr_ev],
                                 tag=tag + "_wbb")
            wb_ev = pb_w2.read("vector",
                               lambda e, o=wbv[:, tb, :],
                                      p=pb_w2.tile[:, 0:TB]:
                                   e.tensor_copy(o, p),
                               deps=[wb_war], tag=tag + "_wbe")
            wb_evs.append(wb_ev)
        for tb in range(NTB):
            ts = tb * TB
            a_evs = []
            for i in range(4):
                gmm = pb_g.mm_acc(
                    [(pb_g.tile[:, 0:TB], wguv[:, kc, i * 128:(i + 1) * 128],
                      xv[:, kc, ts:ts + TB]) for kc in range(DC)],
                    deps=list(deps) + [wld], tag=tag + "_gmm")
                umm = pb_u.mm_acc(
                    [(pb_u.tile[:, 0:TB],
                      wguv[:, kc, 512 + i * 128:512 + (i + 1) * 128],
                      xv[:, kc, ts:ts + TB]) for kc in range(DC)],
                    deps=list(deps) + [wld], tag=tag + "_umm")
                g_ev = pb_g.read("scalar",
                                 lambda e, o=gs_t[:], p=pb_g.tile[:, 0:TB]:
                                     e.activation(o, p, AF.Silu),
                                 deps=[gs_war], tag=tag + "_sil")
                u_ev = pb_u.read("vector",
                                 lambda e, o=ts1[:], p=pb_u.tile[:, 0:TB],
                                        g=gs_t[:]:
                                     e.tensor_tensor(o, p, g, OP.mult),
                                 deps=[g_ev, a2_prev], tag=tag + "_uev")
                gs_war = u_ev
                a2 = b.emit("vector",
                            lambda e, o=actv[:, i, :], wbx=wbv[:, tb, :]:
                                e.tensor_tensor(o, ts1[:], wbx, OP.mult),
                            deps=[u_ev, wb_evs[tb], act_war],
                            tag=tag + "_act")
                a2_prev = a2
                a_evs.append(a2)
            wgu_last[par] = umm
            wb_war = a_evs[-1]
            for oc in range(DC):
                mms = [(pb_d.tile[:, 0:TB], wdwv[:, kc, oc * 128:(oc + 1) * 128],
                        actv[:, kc, :], [dld] if kc == 0 else [])
                       for kc in range(4)]
                dmm = pb_d.mm_acc(mms, deps=[a_evs[-1]], tag=tag + "_dmm")
                act_war = dmm
                wdw_last = [dmm]
                rd = pb_d.read("vector",
                               lambda e, a=rv[:, oc, ts:ts + TB],
                                      p=pb_d.tile[:, 0:TB]:
                                   e.tensor_tensor(a, a, p, OP.add),
                               tag=tag + "_racc")
                out_ops.append(rd)
    return [out_ops[-1]]


def head_perm():
    perm = np.zeros(D, np.int64)
    for h in range(HEADS):
        for j in range(HD):
            perm[h * HD + j] = j * HEADS + h
    return perm


D = 1024
DC = D // 128
T = 2048
TB = 512
NTB = T // TB
HEADS = 16
HD = 64
NEG = 1.0e10


def bf(x):
    return np.asarray(x).astype(ml_dtypes.bfloat16)


class Seg:

    def __init__(self, name):
        self.nc = bass.Bass("TRN2", target_bir_lowering=False)
        self.es = ExitStack()
        self.es.__enter__()
        self.b = Builder(self.nc, self.es)
        self.name = name
        self.inputs = {}
        self.n_junk = 0
        self._const_cache = {}

    def din(self, name, arr_or_shape, dtype=None, percore=None):
        if percore is not None:
            arr0 = percore[0]
            ap = self.nc.dram_tensor(name, list(arr0.shape), np_to_bir(arr0.dtype),
                                     kind="ExternalInput").ap()
            self.inputs[name] = percore
            return ap
        arr = arr_or_shape
        ap = self.nc.dram_tensor(name, list(arr.shape), np_to_bir(arr.dtype),
                                 kind="ExternalInput").ap()
        self.inputs[name] = arr
        return ap

    def dout(self, name, shape, dtype=F32):
        return self.nc.dram_tensor(name, list(shape), dtype, kind="ExternalOutput").ap()

    def dram(self, name, shape, dtype=F32):
        return self.nc.dram_tensor(name, list(shape), dtype, kind="Internal").ap()

    def sb(self, name, shape, dtype=F32):
        return self.es.enter_context(self.nc.sbuf_tensor(name, list(shape), dtype))

    def ps(self, name, shape, dtype=F32):
        return self.es.enter_context(self.nc.psum_tensor(name, list(shape), dtype))

    def const(self, name, arr):
        key = name
        if key in self._const_cache:
            return self._const_cache[key]
        arr = np.ascontiguousarray(arr)
        ap = self.din(name, arr)
        tile = self.sb("c_" + name, arr.shape, np_to_bir(arr.dtype))
        op = self.b.dma("sync", tile[:], ap, tag="const_" + name,
                        stream="const", wait_all=True, front=True)
        self._const_cache[key] = (tile, op)
        return tile, op

    def make_arena(self, name, nbytes):
        backing = self.sb(name, [128, nbytes], U8)
        return Arena(self, backing, nbytes)

    def constcol(self, value):
        key = f"cv_{float(value)!r}".replace("-", "m").replace(".", "p")
        tile, op = self.const(key, np.full((128, 1), value, np.float32))
        return tile[:], op

    def junk(self, eng="vector"):
        if not hasattr(self, "_junktiles"):
            self._junktiles = {}
        if eng not in self._junktiles:
            self._junktiles[eng] = self.sb("junkt_" + eng, [128, 8], F32)
        jt = self._junktiles[eng]
        self.n_junk += 1
        if eng == "scalar":
            return self.b.emit(eng, lambda e: e.memzero(jt[:]), tag="junk")
        return self.b.emit(eng, lambda e: e.memset(jt[:], 0.0), tag="junk")

    def run_sim(self, core_inputs):
        from concourse.bass_interp import CoreSim
        from concourse.libnrt import get_pod_config
        self.b.check()
        self.b.finalize()
        self.es.__exit__(None, None, None)
        sim = CoreSim(self.nc, require_finite=False, require_nnan=False,
                      **get_pod_config())
        m = dict(core_inputs[0])
        for k, v in self.inputs.items():
            m[k] = np.ascontiguousarray(v[0] if isinstance(v, list) else v)
        for k, v in m.items():
            sim.tensor(k)[:] = v
        sim.simulate()
        outs = {}
        for name in self.nc.inst_names() if False else []:
            pass
        return sim

    def run(self, core_inputs, core_ids=None, **kw):
        self.b.check()
        self.b.finalize()
        self.es.__exit__(None, None, None)
        n = len(core_inputs)
        in_maps = []
        for c in range(n):
            m = dict(core_inputs[c])
            for k, v in self.inputs.items():
                if isinstance(v, list):
                    m[k] = np.ascontiguousarray(v[c])
                else:
                    m[k] = np.ascontiguousarray(v)
            in_maps.append(m)
        return run_bass_kernel_spmd(
            self.nc, in_maps, core_ids=list(range(n)) if core_ids is None else core_ids,
            **kw)


class Arena:

    def __init__(self, seg, backing, nbytes):
        self.seg = seg
        self.backing = backing
        self.nbytes = nbytes
        self.cur = 0
        self.hwm = 0
        self._n = 0

    def reset(self):
        self.cur = 0

    def alloc(self, name, shape, dtype):
        nc = self.seg.nc
        import functools, operator
        per_part = functools.reduce(operator.mul, shape[1:], 1) * mybir.dt.size(dtype)
        addr = (self.cur + 31) // 32 * 32
        self.cur = addr + per_part
        self.hwm = max(self.hwm, self.cur)
        assert self.cur <= self.nbytes, (name, self.cur, self.nbytes)
        self._n += 1
        h = nc.virtual_sbuf_tensor(f"{name}_ar{self._n}", list(shape), dtype)
        nc._tensor(h.name, h.shape, h.dtype, type="SB")
        mloc = nc.lookup_mloc(h)
        mloc.allocated = True
        base = nc.lookup_mloc(self.backing).addr
        mloc.addr = base + addr
        return h




def _std_seg(name, x_percore, extra_outs=()):
    seg = Seg(name)
    x = seg.din("x", None, percore=x_percore)
    y = seg.dout("y", [T, D])
    resid = seg.sb("resid", [128, DC * T], F32)
    rv = resid[:].rearrange("p (c t) -> p c t", c=DC)
    xn = seg.sb("xn", [128, DC * T], BF16)
    xv = xn[:].rearrange("p (c t) -> p c t", c=DC)
    setup_common(seg)
    shared = [PB(seg, f"sh{i}") for i in range(8)]
    seg._shared_banks = shared
    tin = emit_transpose_in(seg, x, rv, pbs=shared[0:2])
    return seg, x, y, rv, xv, tin, shared


def prep_weights(inputs):
    g = lambda k: np.ascontiguousarray(inputs[k]).astype(np.float32)
    perm = head_perm()
    pad = lambda a, axis: np.pad(a, [(0, 0)] * axis + [(0, HMOE - 2752)] +
                                 [(0, 0)] * (a.ndim - axis - 1))
    W = {}
    W["l0aa"] = {
        "qw": np.ascontiguousarray(g("L0_attn_a_qw")[:, perm]),
        "kw": np.ascontiguousarray(g("L0_attn_a_kw")[:, perm]),
        "vw": np.ascontiguousarray(g("L0_attn_a_vw")[:, perm]),
        "ow": np.ascontiguousarray(g("L0_attn_a_ow")[perm, :]),
    }
    W["l0ab"] = {k: g("L0_attn_b_" + k)
                 for k in ["Wq", "bq", "Wk", "bk", "Wv", "bv", "Wo", "bo"]}
    W["l0f"] = {
        "wq": g("L0_ffn_wq"), "wk": g("L0_ffn_wk"), "wv": g("L0_ffn_wv"),
        "wo": g("L0_ffn_wo"), "w_router": g("L0_ffn_w_router"),
        "wg": pad(g("L0_ffn_wg"), 2), "wu": pad(g("L0_ffn_wu"), 2),
        "wd": pad(g("L0_ffn_wd"), 1),
        "g1": g("L0_ffn_g1"), "g2": g("L0_ffn_g2"),
    }
    W["l1aa"] = {
        "norm1_w": g("L1_attn_a_norm1_w"), "norm2_w": g("L1_attn_a_norm2_w"),
        "A_log": g("L1_attn_a_A_log"), "D_param": g("L1_attn_a_D_param"),
        "scale": g("L1_attn_a_scale"),
        "WdtT": np.ascontiguousarray(g("L1_attn_a_Wdt").T),
        "bdt": g("L1_attn_a_bdt"),
        "WBT": np.ascontiguousarray(g("L1_attn_a_WB").T),
        "bB": g("L1_attn_a_bB"),
        "WCT": np.ascontiguousarray(g("L1_attn_a_WC").T),
        "bC": g("L1_attn_a_bC"),
        "W1T": np.ascontiguousarray(g("L1_attn_a_W1").T),
        "b1": g("L1_attn_a_b1"),
        "W2T": np.ascontiguousarray(g("L1_attn_a_W2").T),
        "b2": g("L1_attn_a_b2"),
    }
    W["l1ab"] = {
        "gamma": g("L1_attn_b_gamma"), "beta": g("L1_attn_b_beta"),
        "w_inT": np.ascontiguousarray(g("L1_attn_b_w_in").T),
        "conv_w": g("L1_attn_b_conv_w"), "conv_b": g("L1_attn_b_conv_b"),
        "w_xT": np.ascontiguousarray(g("L1_attn_b_w_x").T),
        "w_dtT": np.ascontiguousarray(g("L1_attn_b_w_dt").T),
        "b_dt": g("L1_attn_b_b_dt"),
        "A_log": g("L1_attn_b_A_log"), "Dp": g("L1_attn_b_Dp"),
        "w_outT": np.ascontiguousarray(g("L1_attn_b_w_out").T),
    }
    W["l1f"] = {k: g("L1_ffn_" + k)
                for k in ["gate_w", "shared_gate_w", "shared_gate_up_w",
                          "shared_down_w", "expert_gate_up_w",
                          "expert_down_w"]}
    return W


def run_seg_l0aa(xs, W):
    seg, x, y, rv, xv, tin, sh = _std_seg("s_l0aa", xs)
    outs = sub_l0_attn_a(seg, rv, xv, xv, W["l0aa"], tin)
    emit_transpose_out(seg, rv, y, outs, pbs=sh[0:2])
    r = seg.run([{} for _ in xs])
    return [np.asarray(rr["y"]) for rr in r.results]


def run_seg_l0ab(xs, W):
    seg, x, y, rv, xv, tin, sh = _std_seg("s_l0ab", xs)
    outs = sub_l0_attn_b(seg, rv, xv, W["l0ab"], tin)
    emit_transpose_out(seg, rv, y, outs, pbs=sh[0:2])
    r = seg.run([{} for _ in xs])
    return [np.asarray(rr["y"]) for rr in r.results]


def run_seg_l0f(xs, W, inst):
    seg, x, y, rv, xv, tin, sh = _std_seg(f"s_l0f{inst}", xs)
    outs = sub_l0_ffn(seg, rv, xv, W["l0f"], tin, inst)
    emit_transpose_out(seg, rv, y, outs, pbs=sh[6:8], add_dram=x)
    r = seg.run([{} for _ in xs])
    return [np.asarray(rr["y"]) for rr in r.results]


def run_seg_l1aa(xs, W):
    seg, x, y, rv, xv, tin, sh = _std_seg("s_l1aa", xs)
    outs = sub_l1_attn_a(seg, rv, xv, W["l1aa"], tin)
    emit_transpose_out(seg, rv, y, outs, pbs=sh[6:8], add_dram=x)
    r = seg.run([{} for _ in xs])
    return [np.asarray(rr["y"]) for rr in r.results]


def run_seg_l1ab(xs, W):
    seg, x, y, rv, xv, tin, sh = _std_seg("s_l1ab", xs)
    outs = sub_l1_attn_b(seg, rv, xv, W["l1ab"], tin)
    emit_transpose_out(seg, rv, y, outs, pbs=sh[6:8], add_dram=x)
    r = seg.run([{} for _ in xs])
    return [np.asarray(rr["y"]) for rr in r.results]


def run_seg_l1f_a(xs, W):
    seg, x, y, rv, xv, tin, sh = _std_seg("s_l1fa", xs)
    wout = seg.dout("wout", [T, NE2])
    xnd = seg.dout("xnd", [D, T], BF16)
    cout = seg.dout("cout", [NE2, 1])
    aops, wsv, wdeps, counts_sb, cnt_ev = sub_l1_ffn_a(seg, rv, xv,
                                                       W["l1f"], tin)
    seg.b.dma("sync", wout.rearrange("(a p) b -> p a b", p=128), wsv,
              deps=wdeps, tag="wst")
    seg.b.dma("gpsimd", xnd.rearrange("(c p) t -> p c t", p=128), xv,
              deps=tin + aops, tag="xnst")
    seg.b.dma("sync", cout, counts_sb[:], deps=[cnt_ev], tag="cst")
    emit_transpose_out(seg, rv, y, aops, pbs=sh[6:8])
    r = seg.run([{} for _ in xs])
    return ([np.asarray(rr["y"]) for rr in r.results],
            [np.asarray(rr["wout"]) for rr in r.results],
            [np.asarray(rr["xnd"]) for rr in r.results],
            [np.asarray(rr["cout"]) for rr in r.results])


def run_seg_l1f_b(y1s, wts, xns, counts_all, W):
    n = len(y1s)
    seg = Seg("s_l1fb")
    x = seg.din("x", None, percore=y1s)
    wt_in = seg.din("wt", None, percore=wts)
    xn_in = seg.din("xnin", None, percore=xns)
    cts = seg.din("cts", counts_all)
    pmask = seg.din("pmask", None,
                    percore=[np.asarray([[1.0] if r < c else [0.0]
                                         for r in range(8)], np.float32)
                             for c in range(n)])
    y = seg.dout("y", [T, D])
    resid = seg.sb("resid", [128, DC * T], F32)
    rv = resid[:].rearrange("p (c t) -> p c t", c=DC)
    xn = seg.sb("xn", [128, DC * T], BF16)
    xv = xn[:].rearrange("p (c t) -> p c t", c=DC)
    setup_common(seg)
    shared = [PB(seg, f"sh{i}") for i in range(8)]
    seg._shared_banks = shared
    if not hasattr(seg, "A"):
        seg.A = seg.make_arena("A", 60 * 1024)
    wsb = seg.sb("wsb", [128, 16 * NE2], F32)
    wsv = wsb[:].rearrange("p (a b) -> p a b", a=16)
    selt = seg.sb("selt", [128, 16 * NE2], F32)
    selv = selt[:].rearrange("p (a b) -> p a b", a=16)
    cts_t = seg.sb("cts_t", [8, NE2], F32)
    pmk_t = seg.sb("pmk_t", [8, 1], F32)
    b = seg.b
    tin = emit_transpose_in(seg, x, rv, pbs=shared[0:2])
    ldxn = b.dma("sync", xv, xn_in.rearrange("(c p) t -> p c t", p=128),
                 tag="ldxn")
    ldw = b.dma("sync", wsv, wt_in.rearrange("(a p) b -> p a b", p=128),
                tag="ldw")
    ldc = b.dma("sync", cts_t[:], cts, tag="ldc")
    ldp = b.dma("sync", pmk_t[:], pmask, tag="ldp")
    sel_ops = []
    for tt in range(16):
        op = b.emit("vector",
                    lambda e, o=selv[:, tt, :], i=wsv[:, tt, :]:
                        e.tensor_scalar(o, i, 0.0, None, OP.is_gt),
                    deps=[ldw], tag="selb")
        sel_ops.append(op)
    outs = sub_l1_ffn_b(seg, rv, xv, W["l1f"], [ldxn], wsv, selv,
                        sel_ops + [ldw], cts_t[:], pmk_t[:], [ldc, ldp])
    emit_transpose_out(seg, rv, y, outs, pbs=shared[6:8])
    r = seg.run([{} for _ in range(n)])
    return [np.asarray(rr["y"]) for rr in r.results]


def kernel(**inputs):
    W = prep_weights(inputs)
    x0 = np.ascontiguousarray(inputs["L0_attn_a_x"]).astype(np.float32)
    xs = [np.ascontiguousarray(x0[c]) for c in range(8)]
    xs = run_seg_l0aa(xs, W)
    xs = run_seg_l0f(xs, W, 0)
    xs = run_seg_l0ab(xs, W)
    xs = run_seg_l0f(xs, W, 1)
    xs = run_seg_l1aa(xs, W)
    for _ in range(2):
        y1s, wts, xns, cnts = run_seg_l1f_a(xs, W)
        counts_all = np.ascontiguousarray(
            np.stack([c.reshape(NE2) for c in cnts], 0))
        xs = run_seg_l1f_b(y1s, wts, xns, counts_all, W)
        if _ == 0:
            xs = run_seg_l1ab(xs, W)
    return np.stack(xs, 0).astype(np.float32)


def np_to_bir(dt):
    dt = np.dtype(dt)
    if dt == np.float32:
        return F32
    if dt == ml_dtypes.bfloat16:
        return BF16
    if dt == np.uint8:
        return U8
    if dt == np.int32:
        return mybir.dt.int32
    raise ValueError(dt)




class PB:
    def __init__(self, seg, name, shape=(128, 512), dtype=F32):
        self.seg = seg
        self.tile = seg.ps(name, list(shape), dtype)
        self.readers = []
        self.writer = None

    def mm_acc(self, mms, deps=(), tag=None):
        b = self.seg.b
        war = list(self.readers)
        n = len(mms)
        op = None
        for i in range(n):
            mm = mms[i]
            mdeps = list(mm[3]) if len(mm) > 3 else []
            if i == 0:
                mdeps += list(deps) + war

            def fn(e, mm=mm, first=(i == 0), last=(i == n - 1)):
                return e.matmul(mm[0], mm[1], mm[2], start=first, stop=last)

            op = b.emit("tensor", fn, deps=mdeps, tag=tag)
        self.writer = op
        self.readers = []
        return op

    def transpose(self, out_ap, in_ap, ident, deps=(), tag=None):
        b = self.seg.b
        war = list(self.readers)

        def fn(e):
            return e.transpose(out_ap, in_ap, ident)

        op = b.emit("tensor", fn, deps=list(deps) + war, tag=tag)
        self.writer = op
        self.readers = []
        return op

    def read(self, eng, fn, deps=(), tag=None):
        op = self.seg.b.emit(eng, fn, deps=list(deps) + [self.writer], tag=tag)
        self.readers.append(op)
        return op




def setup_common(seg):
    seg.identf, seg.identf_op = seg.const("identf", np.eye(128, dtype=np.float32))
    seg.identb, seg.identb_op = seg.const("identb", bf(np.eye(128)))
    seg.onesc, seg.onesc_op = seg.const("onesc", np.ones((128, 1), np.float32))
    seg._onesr, seg._onesr_op = seg.const("onesr", np.ones((1, 128), np.float32))
    seg.onesr_b, seg.onesr_b_op = seg.const("onesr_b", bf(np.ones((1, 128))))
    seg.onesc_b, seg.onesc_b_op = seg.const("onesc_b", bf(np.ones((128, 1))))


def bcast_row(seg, bank, row_ap, n, deps=(), tag="bc"):
    return bank.mm_acc([(bank.tile[:, 0:n], seg._onesr[:], row_ap)],
                       deps=list(deps) + [seg._onesr_op], tag=tag)


def emit_transpose_in(seg, x_dram, rv, dtype_sb=F32, pbs=None):
    b = seg.b
    xrow = seg.sb("ti_xrow", [128, 2 * D], F32)
    seg._ti_xrow = xrow
    if pbs is None:
        pbs = [PB(seg, "ti_p0"), PB(seg, "ti_p1")]
    evac_ops = []
    last_use = [None, None]
    for tt in range(T // 128):
        dbuf = tt % 2
        src = x_dram[tt * 128:(tt + 1) * 128, :]
        dst = xrow[:, dbuf * D:(dbuf + 1) * D]
        ld = b.dma("sync", dst, src, deps=[last_use[dbuf]], tag="ti_ld",
                   stream=f"ti{dbuf}")
        tops = []
        for dc in range(DC):
            bank = pbs[(tt * DC + dc) % 2]
            bank.transpose(bank.tile[:, 0:128], dst[:, dc * 128:(dc + 1) * 128],
                           seg.identf[:], deps=[ld, seg.identf_op], tag="ti_tr")
            ev = bank.read(
                "vector",
                lambda e, bank=bank, dc=dc, tt=tt: e.tensor_copy(
                    rv[:, dc, tt * 128:(tt + 1) * 128], bank.tile[:, 0:128]),
                tag="ti_ev")
            tops.append(ev)
            evac_ops.append(ev)
        last_use[dbuf] = tops[-1]
    return evac_ops


def emit_transpose_out(seg, rv, out_dram, deps, pbs=None, add_dram=None):
    b = seg.b
    orow = getattr(seg, "_ti_xrow", None)
    if orow is None:
        orow = seg.sb("to_orow", [128, 2 * D], F32)
    if pbs is None:
        pbs = [PB(seg, "to_p0"), PB(seg, "to_p1")]
    arow = seg.sb("to_arow", [128, 2 * D], F32) if add_dram is not None else None
    dma_ops = []
    last_st = [None, None]
    for tt in range(T // 128):
        dbuf = tt % 2
        dst = orow[:, dbuf * D:(dbuf + 1) * D]
        evs = []
        adep = None
        if add_dram is not None:
            adst = arow[:, dbuf * D:(dbuf + 1) * D]
            adep = b.dma("sync", adst, add_dram[tt * 128:(tt + 1) * 128, :],
                         deps=[last_st[dbuf]], tag="to_ald",
                         stream=f"toa{dbuf}")
        for dc in range(DC):
            bank = pbs[(tt * DC + dc) % 2]
            bank.transpose(bank.tile[:, 0:128], rv[:, dc, tt * 128:(tt + 1) * 128],
                           seg.identf[:],
                           deps=list(deps) + [seg.identf_op, last_st[dbuf]],
                           tag="to_tr")
            if add_dram is not None:
                ev = bank.read(
                    "vector",
                    lambda e, bank=bank, dap=dst[:, dc * 128:(dc + 1) * 128],
                           aap=adst[:, dc * 128:(dc + 1) * 128]:
                        e.tensor_tensor(dap, aap, bank.tile[:, 0:128], OP.add),
                    deps=[adep], tag="to_ev")
            else:
                ev = bank.read(
                    "vector",
                    lambda e, bank=bank, dap=dst[:, dc * 128:(dc + 1) * 128]:
                        e.tensor_copy(dap, bank.tile[:, 0:128]),
                    tag="to_ev")
            evs.append(ev)
        st = b.dma("sync", out_dram[tt * 128:(tt + 1) * 128, :], dst,
                   deps=evs, tag="to_st", stream=f"to{dbuf}")
        last_st[dbuf] = st
        dma_ops.append(st)
    return dma_ops


def emit_layernorm(seg, rv, xn, deps, banks, eps=1e-5, tag="ln", rms=False,
                   gain=None, gain_op=None, beta=None, beta_op=None,
                   out2=None, alloc=None):
    b = seg.b
    alloc = alloc or seg.sb
    pb_s, pb_q, pb_m, pb_r = banks[0], banks[1], banks[2], banks[3]
    sq = [alloc(tag + f"_sq{i}", [128, TB], F32) for i in range(DC)]
    stats = alloc(tag + "_st", [1, 6 * TB], F32)
    sv = stats[:].rearrange("p (k t) -> p k t", k=6)
    tmp = [alloc(tag + f"_tmp{i}", [128, TB], F32) for i in range(2)]
    out_ops = []
    deps = list(deps)
    prev_mms = []
    for tb in range(NTB):
        ts = tb * TB
        te = ts + TB
        sum_ev = None
        if not rms:
            pb_s.mm_acc(
                [(pb_s.tile[0:1, 0:TB], seg.onesc[:], rv[:, dc, ts:te])
                 for dc in range(DC)],
                deps=deps + [seg.onesc_op], tag=tag + "_sum")
            sum_ev = pb_s.read(
                "vector",
                lambda e: e.tensor_copy(sv[:, 0, :], pb_s.tile[0:1, 0:TB]),
                tag=tag + "_sev")
        qmms = []
        for dc in range(DC):
            sq_ap = sq[dc][:]
            src_ap = rv[:, dc, ts:te]
            aop = b.emit(
                "scalar",
                lambda e, o=sq_ap, i=src_ap: e.activation(o, i, AF.Square),
                deps=list(deps) + prev_mms,
                tag=tag + "_sq")
            qmms.append((pb_q.tile[0:1, 0:TB], seg.onesc[:], sq_ap,
                         [aop, seg.onesc_op]))
        qg = pb_q.mm_acc(qmms, tag=tag + "_qsum")
        prev_mms = [qg]
        sq_ev = pb_q.read(
            "vector", lambda e: e.tensor_copy(sv[:, 1, :], pb_q.tile[0:1, 0:TB]),
            tag=tag + "_qev")
        if rms:
            op3 = b.emit("vector", lambda e: e.tensor_scalar(
                sv[:, 4, :], sv[:, 1, :], 1.0 / D, None, OP.mult),
                deps=[sq_ev], tag=tag + "_ms")
            mean_op = None
        else:
            op1 = b.emit("vector", lambda e: e.tensor_scalar(
                sv[:, 2, :], sv[:, 0, :], 1.0 / D, None, OP.mult),
                deps=[sum_ev, sq_ev], tag=tag + "_mean")
            op2 = b.emit("vector", lambda e: e.tensor_tensor(
                sv[:, 3, :], sv[:, 2, :], sv[:, 2, :], OP.mult),
                deps=[op1], tag=tag + "_msq")
            op3 = b.emit("vector", lambda e: e.scalar_tensor_tensor(
                sv[:, 4, :], sv[:, 1, :], 1.0 / D, sv[:, 3, :],
                OP.mult, OP.subtract), deps=[op2], tag=tag + "_var")
            mean_op = op1
        epsc, epsc_op = seg.constcol(eps)
        op4 = b.emit("scalar", lambda e: e.activation(
            sv[:, 5, :], sv[:, 4, :], AF.Ln, bias=epsc[0:1, 0:1]),
            deps=[op3, epsc_op], tag=tag + "_lnv")
        op5 = b.emit("scalar", lambda e: e.activation(
            sv[:, 5, :], sv[:, 5, :], AF.Exp, scale=-0.5),
            deps=[op4], tag=tag + "_expv")
        if not rms:
            bcast_row(seg, pb_m, sv[:, 2, :], TB, deps=[mean_op], tag=tag + "_mb")
        bcast_row(seg, pb_r, sv[:, 5, :], TB, deps=[op5], tag=tag + "_rb")
        chains = []
        tmp_war = getattr(seg, "_ln_tmp_war_" + tag, {})
        setattr(seg, "_ln_tmp_war_" + tag, tmp_war)
        for dc in range(DC):
            ch = []
            xn_ap = xn[:, dc, ts:te]
            rv_ap = rv[:, dc, ts:te]
            rb_ap = pb_r.tile[:, 0:TB]
            mb_ap = pb_m.tile[:, 0:TB]
            tmp_ap = tmp[dc % 2][:]
            g_ap = gain[:, dc:dc + 1] if gain is not None else None
            be_ap = beta[:, dc:dc + 1] if beta is not None else None
            if rms:
                def t2(deps2, xn_ap=xn_ap, rv_ap=rv_ap, rb_ap=rb_ap, g_ap=g_ap):
                    if g_ap is not None:
                        return pb_r.read(
                            "vector", lambda e: e.scalar_tensor_tensor(
                                xn_ap, rb_ap, g_ap, rv_ap, OP.mult, OP.mult),
                            deps=deps2 + deps + ([gain_op] if gain_op else []),
                            tag=tag + "_g")
                    return pb_r.read(
                        "vector", lambda e: e.tensor_tensor(
                            xn_ap, rv_ap, rb_ap, OP.mult),
                        deps=deps2 + deps, tag=tag + "_mul")
                ch.append(t2)
            else:
                def t1(deps2, tmp_ap=tmp_ap, rv_ap=rv_ap, mb_ap=mb_ap, dc=dc):
                    return pb_m.read(
                        "vector", lambda e: e.tensor_tensor(
                            tmp_ap, rv_ap, mb_ap, OP.subtract),
                        deps=deps2 + deps + [tmp_war.get(dc % 2)],
                        tag=tag + "_sub")
                ch.append(t1)

                def t2(deps2, xn_ap=xn_ap, tmp_ap=tmp_ap, rb_ap=rb_ap,
                       g_ap=g_ap, dc=dc):
                    if g_ap is not None:
                        op = pb_r.read(
                            "vector", lambda e: e.scalar_tensor_tensor(
                                xn_ap, rb_ap, g_ap, tmp_ap, OP.mult, OP.mult),
                            deps=deps2 + ([gain_op] if gain_op else []),
                            tag=tag + "_g")
                    else:
                        op = pb_r.read(
                            "vector", lambda e: e.tensor_tensor(
                                xn_ap, tmp_ap, rb_ap, OP.mult),
                            deps=deps2, tag=tag + "_mul")
                    tmp_war[dc % 2] = op
                    return op
                ch.append(t2)
            if beta is not None:
                def t3(deps2, xn_ap=xn_ap, be_ap=be_ap):
                    return b.emit(
                        "vector", lambda e: e.tensor_scalar(
                            xn_ap, xn_ap, be_ap, None, OP.add),
                        deps=deps2 + ([beta_op] if beta_op else []),
                        tag=tag + "_beta")
                ch.append(t3)
            if out2 is not None:
                def t4(deps2, o2_ap=out2[:, dc, ts:te], xn_ap=xn_ap):
                    return b.emit(
                        "vector", lambda e: e.tensor_copy(o2_ap, xn_ap),
                        deps=deps2, tag=tag + "_o2")
                ch.append(t4)
            chains.append(ch)
        out_ops += wavefront(chains)
    return out_ops


def load_weight_bf16(seg, name, w_ap, din, dout, deps=(), tile=None,
                     stream=None):
    kc = din // 128
    if tile is None:
        tile = seg.sb("w_" + name, [128, kc * dout], BF16)
    view = tile[:].rearrange("p (c m) -> p c m", c=kc)[:, :, 0:dout]
    src = w_ap.rearrange("(c p) m -> p c m", p=128)
    op = seg.b.dma("gpsimd", view, src, deps=deps, tag="wld_" + name,
                   stream=stream or ("w_" + name),
                   wait_all=(stream is None))
    return view, op


def emit_proj(seg, tag, wview, w_op, xv, nkc, dout, out_fn, deps, banks,
              extra_mms=None, tbs=None):
    evs = []
    nb = len(banks)
    i = 0
    for tb in (range(NTB) if tbs is None else tbs):
        for oc in range(dout // 128):
            pb = banks[i % nb]
            i += 1
            mms = [(pb.tile[:, 0:TB], wview[:, kc, oc * 128:(oc + 1) * 128],
                    xv[:, kc, tb * TB:(tb + 1) * TB]) for kc in range(nkc)]
            if extra_mms is not None:
                mms += extra_mms(oc, tb)
            mm = pb.mm_acc(mms, deps=list(deps) + [w_op], tag=tag + "_mm")
            evs.append(out_fn(oc, tb, pb, mm))
    return evs


def evac_to_dram(seg, tag, dram_fm, dtype=BF16, bias=None, bias_op=None,
                 nstage=3, alloc=None):
    alloc = alloc or seg.sb
    stages = [alloc(f"{tag}_stg{i}", [128, TB], dtype) for i in range(nstage)]
    state = {"i": 0, "last_st": [None] * nstage}

    def out_fn(oc, tb, pb, mm):
        si = state["i"] % nstage
        state["i"] += 1
        stg = stages[si]
        if bias is not None:
            ev = pb.read("vector",
                         lambda e, o=stg[:], p=pb.tile[:, 0:TB],
                                b_=bias[:, oc:oc + 1]:
                             e.tensor_scalar(o, p, b_, None, OP.add),
                         deps=[state["last_st"][si]] +
                              ([bias_op] if bias_op else []),
                         tag=tag + "_ev")
        else:
            ev = pb.read("vector",
                         lambda e, o=stg[:], p=pb.tile[:, 0:TB]:
                             e.tensor_copy(o, p),
                         deps=[state["last_st"][si]], tag=tag + "_ev")
        st = seg.b.dma("sync",
                       dram_fm[oc * 128:(oc + 1) * 128,
                               tb * TB:(tb + 1) * TB],
                       stg[:], deps=[ev], tag=tag + "_st",
                       stream=f"{tag}_st{si}")
        state["last_st"][si] = st
        return st

    return out_fn


def emit_proj_tok(seg, tag, wview, w_op, xv, nkc, dout, dram_tok, deps,
                  banks, bias=None, bias_op=None, alloc=None):
    evs = []
    alloc = alloc or seg.sb
    nb = len(banks)
    stages = [alloc(f"{tag}_stg{i}", [128, 512], BF16) for i in range(3)]
    state = {"i": 0, "last_st": [None] * 3}
    nvc = dout // 512
    for tt in range(T // 128):
        for vc in range(nvc):
            pb = banks[(tt * nvc + vc) % nb]
            mms = [(pb.tile[:, 0:512], xv[:, kc, tt * 128:(tt + 1) * 128],
                    wview[:, kc, vc * 512:(vc + 1) * 512]) for kc in range(nkc)]
            if bias is not None:
                row, row_op = bias
                mms.append((pb.tile[:, 0:512], seg.onesr_b[:],
                            row[0:1, vc * 512:(vc + 1) * 512],
                            [row_op, seg.onesr_b_op]))
            mm = pb.mm_acc(mms, deps=list(deps) + [w_op], tag=tag + "_mm")
            si = state["i"] % 3
            state["i"] += 1
            stg = stages[si]
            ev = pb.read("vector",
                         lambda e, o=stg[:], p=pb.tile[:, 0:512]:
                             e.tensor_copy(o, p),
                         deps=[state["last_st"][si]], tag=tag + "_ev")
            st = seg.b.dma("sync",
                           dram_tok[tt * 128:(tt + 1) * 128,
                                    vc * 512:(vc + 1) * 512],
                           stg[:], deps=[ev], tag=tag + "_st",
                           stream=f"{tag}_st{si}")
            state["last_st"][si] = st
            evs.append(st)
    return evs


def emit_attention(seg, tag, q_dram, k_dram, v_dram, o_dram, deps,
                   causal, scale, alibi_slopes=None, pbs=None, alloc=None):
    b = seg.b
    alloc = alloc or seg.sb
    nh = HEADS
    qh = [alloc(f"{tag}_q{i}", [64, T], BF16) for i in range(2)]
    kh = [alloc(f"{tag}_k{i}", [64, T], BF16) for i in range(2)]
    vh = [alloc(f"{tag}_v{i}", [128, 16 * HD], BF16) for i in range(2)]
    pt = alloc(f"{tag}_pt", [128, 16 * TB], BF16)
    ptv = pt[:].rearrange("p (k q) -> p k q", k=16)
    mkb = alloc(f"{tag}_mkb", [128, 8], F32)
    m1c = alloc(f"{tag}_m1c", [128, 4], F32)
    mb_sb = alloc(f"{tag}_mb", [128, TB], F32)
    rs_sb = alloc(f"{tag}_rs", [64, TB], F32)
    ts_sb = [alloc(f"{tag}_ts{i}", [128, TB], F32) for i in range(2)]
    o_stg = [alloc(f"{tag}_os{i}", [64, TB], BF16) for i in range(2)]
    srow = alloc(f"{tag}_srow", [1, TB], F32)
    rrow = alloc(f"{tag}_rrow", [1, TB], F32)
    if pbs is None:
        pbs = [PB(seg, f"{tag}_pb{i}") for i in range(6)]
    pA = pbs[0:2]
    pRow = pbs[2]
    pSum = pbs[3]
    pO = pbs[4]
    pR = pbs[5]
    if causal:
        mask01 = []
        for r in range(4):
            p_ = np.arange(128)[:, None]
            j_ = np.arange(TB)[None, :]
            mask01.append((j_ >= (r * 128 + p_)).astype(np.float32))
        maskc, maskc_op = seg.const(tag + "_mask", bf(np.stack(mask01, 1).reshape(128, 4 * TB)))
        maskv = maskc[:].rearrange("p (r q) -> p r q", r=4)
    if alibi_slopes is not None:
        cols = np.zeros((128, nh, 16), np.float32)
        madd = np.zeros((128, nh), np.float32)
        for h in range(nh):
            for kt in range(16):
                cols[:, h, kt] = (1.0 / scale) * alibi_slopes[h] * (kt * 128 + np.arange(128))
            madd[:, h] = (1.0 / scale) * alibi_slopes[h] * (T - 1)
        acols, acols_op = seg.const(tag + "_acols", cols.reshape(128, nh * 16))
        acolsv = acols[:].rearrange("p (h k) -> p h k", h=nh)
        amadd, amadd_op = seg.const(tag + "_amadd", madd)
    last_q = [None, None]
    last_k = [None, None]
    last_v = [None, None]
    last_ost = [None, None]
    out_ops = []
    for h in range(nh):
        par = h % 2
        ldq = b.dma("sync", qh[par][:], q_dram[h * 64:(h + 1) * 64, :],
                    deps=[last_q[par]] + list(deps), tag=tag + "_ldq",
                    stream=f"{tag}_q{par}")
        ldk = b.dma("sync", kh[par][:], k_dram[h * 64:(h + 1) * 64, :],
                    deps=[last_k[par]] + list(deps), tag=tag + "_ldk",
                    stream=f"{tag}_k{par}")
        ldv = b.dma("gpsimd",
                    vh[par][:].rearrange("p (k v) -> p k v", k=16),
                    v_dram[:, h * 64:(h + 1) * 64].rearrange(
                        "(k p) v -> p k v", p=128),
                    deps=[last_v[par]] + list(deps), tag=tag + "_ldv",
                    stream=f"{tag}_v{par}")
        Q, K, V = qh[par], kh[par], vh[par]
        q_uses, k_uses, v_uses = [], [], []
        pt_last_read = {}
        for qb in range(NTB):
            nkt = min(16, 4 * qb + 4) if causal else 16
            chains = []
            for qtl in range(4):
                qt = qb * 4 + qtl
                nkb = min(NTB, qt // 4 + 1) if causal else NTB
                ch = []
                red_ops = []

                def stA(deps2, qt=qt, nkb=nkb, qtl=qtl, red_ops=red_ops):
                    last = None
                    for kb in range(nkb):
                        bank = pA[kb % 2]
                        mm = bank.mm_acc(
                            [(bank.tile[:, 0:TB],
                              Q[:, qt * 128:(qt + 1) * 128],
                              K[:, kb * TB:(kb + 1) * TB])],
                            deps=[ldq, ldk], tag=tag + "_mmA")
                        q_uses.append(mm)
                        k_uses.append(mm)
                        red = bank.read(
                            "vector",
                            lambda e, o=mkb[:, (qtl % 2) * 4 + kb:
                                            (qtl % 2) * 4 + kb + 1],
                                   p=bank.tile[:, 0:TB]:
                                e.tensor_reduce(o, p, AX, OP.max),
                            deps=deps2, tag=tag + "_redA")
                        red_ops.append(red)
                        last = red
                    return last

                ch.append(stA)

                def stComb(deps2, qt=qt, nkb=nkb, qtl=qtl):
                    o = m1c[:, qtl:qtl + 1]
                    i = mkb[:, (qtl % 2) * 4:(qtl % 2) * 4 + nkb]
                    return b.emit(
                        "vector",
                        lambda e, o=o, i=i: e.tensor_reduce(o, i, AX, OP.max),
                        deps=deps2, tag=tag + "_comb")

                ch.append(stComb)
                if alibi_slopes is not None:
                    def stAl(deps2, qtl=qtl, h=h):
                        o = m1c[:, qtl:qtl + 1]
                        return b.emit(
                            "vector",
                            lambda e, o=o, a=amadd[:, h:h + 1]:
                                e.tensor_scalar(o, o, a, None, OP.add),
                            deps=deps2 + [amadd_op], tag=tag + "_al")
                    ch.append(stAl)

                def stTr(deps2, qtl=qtl):
                    return _acc_row(seg, pRow, qtl, m1c[:, qtl:qtl + 1],
                                    deps2)

                ch.append(stTr)
                chains.append(ch)
            wavefront(chains)
            mrow_ev = pRow.read(
                "vector", lambda e: e.tensor_copy(srow[:], pRow.tile[0:1, 0:TB]),
                tag=tag + "_mrowev")
            bcast_row(seg, pR, srow[:], TB, deps=[mrow_ev], tag=tag + "_mbc")
            mb_ev = pR.read(
                "vector", lambda e: e.tensor_copy(mb_sb[:], pR.tile[:, 0:TB]),
                tag=tag + "_mbev")
            sum_started = [False]
            exp_ops = []
            for kt in range(nkt):
                bank = pA[kt % 2]
                mm = bank.mm_acc(
                    [(bank.tile[:, 0:TB], K[:, kt * 128:(kt + 1) * 128],
                      Q[:, qb * TB:(qb + 1) * TB])],
                    deps=[ldq, ldk], tag=tag + "_mmB")
                q_uses.append(mm)
                k_uses.append(mm)
                tsb = ts_sb[kt % 2]
                if alibi_slopes is not None:
                    stt = bank.read(
                        "vector",
                        lambda e, o=tsb[:], p=bank.tile[:, 0:TB],
                               a=acolsv[:, h, kt:kt + 1]:
                            e.scalar_tensor_tensor(o, p, a, mb_sb[:],
                                                   OP.add, OP.subtract),
                        deps=[mb_ev, acols_op], tag=tag + "_stt")
                else:
                    stt = bank.read(
                        "vector",
                        lambda e, o=tsb[:], p=bank.tile[:, 0:TB]:
                            e.tensor_tensor(o, p, mb_sb[:], OP.subtract),
                        deps=[mb_ev], tag=tag + "_stt")
                ptap = ptv[:, kt, :]
                eop = b.emit(
                    "scalar",
                    lambda e, o=ptap, i=tsb[:]: e.activation(
                        o, i, AF.Exp, scale=float(scale)),
                    deps=[stt, pt_last_read.get(kt)], tag=tag + "_exp")
                last_pt_w = eop
                if causal and kt >= 4 * qb:
                    r = kt - 4 * qb
                    mop = b.emit(
                        "vector",
                        lambda e, o=ptap, m=maskv[:, r, :]: e.tensor_tensor(
                            o, o, m, OP.mult),
                        deps=[eop, maskc_op], tag=tag + "_msk")
                    last_pt_w = mop
                exp_ops.append((kt, last_pt_w))
                def fnsum(e, o=pSum.tile[0:1, 0:TB], w=seg.onesc_b[:],
                          i=ptap, first=(not sum_started[0]),
                          last=(kt == nkt - 1)):
                    return e.matmul(o, w, i, start=first, stop=last)
                war = list(pSum.readers) if not sum_started[0] else []
                smm = b.emit("tensor", fnsum,
                             deps=[last_pt_w, seg.onesc_b_op] + war,
                             tag=tag + "_sum")
                pSum.writer = smm
                if not sum_started[0]:
                    pSum.readers = []
                    sum_started[0] = True
            sum_ev = pSum.read(
                "vector", lambda e: e.tensor_copy(rrow[:], pSum.tile[0:1, 0:TB]),
                tag=tag + "_sev")
            rec = b.emit("vector", lambda e: e.reciprocal(rrow[:], rrow[:]),
                         deps=[sum_ev], tag=tag + "_rec")
            bcast_row(seg, pR, rrow[:], TB, deps=[rec], tag=tag + "_rbc")
            rs_ev = pR.read(
                "vector", lambda e: e.tensor_copy(rs_sb[:], pR.tile[0:64, 0:TB]),
                tag=tag + "_rsev")
            mms = []
            for kt in range(nkt):
                ktdep = exp_ops[kt][1]
                mms.append((pO.tile[0:64, 0:TB], vh[par][:].rearrange(
                    "p (k v) -> p k v", k=16)[:, kt, :], ptv[:, kt, :],
                    [ktdep, ldv]))
            pvop = pO.mm_acc(mms, tag=tag + "_pv")
            for kt in range(nkt):
                pt_last_read[kt] = pvop
            v_uses.append(pvop)
            q_uses.append(pvop)
            opar = (h * NTB + qb) % 2
            oev = pO.read(
                "vector",
                lambda e, o=o_stg[opar][:], p=pO.tile[0:64, 0:TB]:
                    e.tensor_tensor(o, p, rs_sb[:], OP.mult),
                deps=[rs_ev, last_ost[opar]], tag=tag + "_oev")
            pR.readers.append(oev)
            ost = b.dma("sync", o_dram[h * 64:(h + 1) * 64,
                                       qb * TB:(qb + 1) * TB],
                        o_stg[opar][:], deps=[oev], tag=tag + "_ost",
                        stream=f"{tag}_o{opar}")
            last_ost[opar] = ost
            out_ops.append(ost)
        last_q[par] = q_uses[-1]
        last_k[par] = k_uses[-1]
        last_v[par] = v_uses[-1]
    return out_ops


def _acc_row(seg, pRow, qtl, col_ap, deps2):
    war = list(pRow.readers) if qtl == 0 else []
    if qtl == 0:
        pRow.readers = []

    def fn(e, o=pRow.tile[0:1, qtl * 128:(qtl + 1) * 128], c=col_ap,
           i=seg.identf[:], first=(qtl == 0), last=(qtl == 3)):
        return e.matmul(o, c, i, start=first, stop=last)
    op = seg.b.emit("tensor", fn, deps=deps2 + war + [seg.identf_op],
                    tag="mtr_acc")
    pRow.writer = op
    return op


def wavefront(chains):
    n = len(chains)
    if n == 0:
        return []
    nst = max(len(c) for c in chains)
    last = [None] * n
    if n == 1:
        for s, fn in enumerate(chains[0]):
            last[0] = fn([last[0]] if last[0] else [])
        return last
    for w in range(n + nst - 1):
        for c in range(min(w, n - 1), -1, -1):
            s = w - c
            if s < len(chains[c]):
                last[c] = chains[c][s]([last[c]] if last[c] is not None else [])
    return last


# version 06
# speedup vs baseline: 1.0011x; 1.0011x over previous
import math
import numpy as np
import ml_dtypes

import concourse.bass as bass
import concourse.mybir as mybir
from concourse.alu_op_type import AluOpType
from concourse.bass_utils import run_bass_kernel_spmd
from contextlib import ExitStack

F32 = mybir.dt.float32
BF16 = mybir.dt.bfloat16
U8 = mybir.dt.uint8
AF = mybir.ActivationFunctionType
AX = mybir.AxisListType.X
OP = AluOpType

ENGINES = ("tensor", "vector", "scalar", "gpsimd", "sync")
DMA_QUEUES = ("sync", "gpsimd", "scalar")


class Op:
    __slots__ = ("eng", "idx", "fn", "deps", "step", "inc_val", "marked",
                 "tag", "sem_key")

    def __init__(self, eng, idx, fn, deps, step, tag, sem_key):
        self.eng = eng
        self.idx = idx
        self.fn = fn
        self.deps = deps
        self.step = step
        self.inc_val = None
        self.marked = False
        self.tag = tag
        self.sem_key = sem_key


class Builder:

    def __init__(self, nc, es):
        self.nc = nc
        self.es = es
        self.ops = {e: [] for e in ENGINES}
        self.pre = {e: [] for e in ENGINES}
        self.sems = {}
        self.n_emitted = 0
        self.streams = {}
        self._frozen_pre = False

    def stream(self, key, queue, wait_all=False):
        if key in self.streams:
            assert self.streams[key]["queue"] == queue, key
            return key
        self.streams[key] = {"wait_all": wait_all, "queue": queue, "total": 0}
        return key

    def emit(self, eng, fn, deps=(), dma=False, tag=None, sem_key=None,
             front=False):
        deps = [d for d in deps if d is not None]
        for d in deps:
            assert isinstance(d, Op)
            d.marked = True
        if sem_key is None:
            sem_key = eng
        if front:
            assert not deps and dma and not self._frozen_pre
            op = Op(eng, -10000 + len(self.pre[eng]), fn, deps, 16, tag,
                    sem_key)
            op.marked = True
            self.streams[sem_key]["total"] += 16
            self.pre[eng].append(op)
            self.n_emitted += 1
            return op
        op = Op(eng, len(self.ops[eng]), fn, deps, 16 if dma else 1, tag,
                sem_key)
        if dma:
            op.marked = True
            self.streams[sem_key]["total"] += 16
        self.ops[eng].append(op)
        self.n_emitted += 1
        return op

    def dma(self, queue, out, in_, deps=(), tag=None, stream=None,
            wait_all=False, front=False, **kw):
        assert queue in DMA_QUEUES
        if stream is None:
            is_store = type(out.tensor).__name__ == "DRamTensorHandle"
            if is_store:
                stream = queue + "_st"
                wait_all = False
            else:
                stream = queue + "_ld"
                wait_all = True
        self.stream(stream, queue, wait_all)
        info = self.streams[stream]
        deps = list(deps)
        if not info["wait_all"] and info.get("last_op") is not None:
            deps.append(info["last_op"])
        op = self.emit(
            queue, lambda e: e.dma_start(out=out, in_=in_, **kw), deps,
            dma=True, tag=tag, sem_key=stream, front=front)
        info["last_op"] = op
        return op

    def finalize(self):
        nc = self.nc
        keys = list(ENGINES) + list(self.streams)
        for k in keys:
            self.sems[k] = self.es.enter_context(nc.semaphore("sem_" + k))
        counts = {k: 0 for k in keys}
        for e in ENGINES:
            for op in self.pre[e] + self.ops[e]:
                if op.marked:
                    counts[op.sem_key] += op.step
                    op.inc_val = counts[op.sem_key]
        block = self.es.enter_context(nc.Block())
        for e in ENGINES:
            self._emit_engine(block, e)

    def check(self):
        counts = {}
        for e in ENGINES:
            for op in self.pre[e] + self.ops[e]:
                if op.marked or op.step == 16:
                    counts[op.sem_key] = counts.get(op.sem_key, 0) + op.step
                    op.inc_val = counts[op.sem_key]
        ptr = {e: 0 for e in ENGINES}
        allops = {e: self.pre[e] + self.ops[e] for e in ENGINES}
        sem = {}
        progress = True
        while progress:
            progress = False
            for e in ENGINES:
                while ptr[e] < len(allops[e]):
                    op = allops[e][ptr[e]]
                    ok = True
                    for d in op.deps:
                        if d.sem_key == e:
                            continue
                        v = self._wait_val(d)
                        if sem.get(d.sem_key, 0) < v:
                            ok = False
                            break
                    if not ok:
                        break
                    if op.marked:
                        sem[op.sem_key] = sem.get(op.sem_key, 0) + op.step
                    ptr[e] += 1
                    progress = True
        stuck = {e: (ptr[e], len(allops[e])) for e in ENGINES
                 if ptr[e] < len(allops[e])}
        if stuck:
            msg = []
            for e, (p, n) in stuck.items():
                op = allops[e][p]
                blocked = []
                for d in op.deps:
                    if d.sem_key != e and sem.get(d.sem_key, 0) < self._wait_val(d):
                        blocked.append((d.sem_key, self._wait_val(d),
                                        sem.get(d.sem_key, 0), d.tag))
                msg.append(f"{e}@{p}/{n} tag={op.tag} blocked_on={blocked}")
            raise RuntimeError("DEADLOCK:\n" + "\n".join(msg))

    def _wait_val(self, d):
        if d.sem_key in self.streams and self.streams[d.sem_key]["wait_all"]:
            return self.streams[d.sem_key]["total"]
        return d.inc_val

    def _emit_engine(self, block, eng):
        ops = self.pre[eng] + self.ops[eng]
        sems = self.sems
        if not ops and not any(s["queue"] == eng for s in self.streams.values()):
            return
        drains = [(k, s["total"]) for k, s in self.streams.items()
                  if s["queue"] == eng and s["total"]]

        def body(engobj):
            waited = {}
            for op in ops:
                for d in op.deps:
                    if d.eng == eng and d.sem_key == eng:
                        assert d.idx < op.idx
                    assert d.marked and d.inc_val is not None, (d.tag, op.tag)
                    v = self._wait_val(d)
                    if waited.get(d.sem_key, 0) < v:
                        engobj.wait_ge(sems[d.sem_key], v)
                        waited[d.sem_key] = v
                inst = op.fn(engobj)
                if op.marked:
                    assert inst is not None, (eng, op.idx, op.tag)
                    inst.then_inc(sems[op.sem_key], op.step)
            for k, tot in drains:
                engobj.wait_ge(sems[k], tot)

        getattr(block, eng)(body)




def sub_l0_attn_a(seg, rv, xn, xv, w, deps):
    b = seg.b
    tag = "l0aa"
    banks = getattr(seg, "_shared_banks", None)
    if banks is None:
        banks = [PB(seg, tag + f"_b{i}") for i in range(8)]
    banks = banks[0:6]
    if not hasattr(seg, "A"):
        seg.A = seg.make_arena("A", 54 * 1024)
    A = seg.A
    A.reset()
    lnops = emit_layernorm(seg, rv, xv, deps, banks, eps=1e-5,
                           tag=tag + "_ln", alloc=A.alloc)
    qd = seg.dram(tag + "_q", [D, T], BF16)
    kd = seg.dram(tag + "_k", [D, T], BF16)
    vd = seg.dram(tag + "_v", [T, D], BF16)
    od = seg.dram(tag + "_o", [D, T], BF16)
    wq = seg.din(tag + "_wq", w["qw"])
    wk = seg.din(tag + "_wk", w["kw"])
    wv = seg.din(tag + "_wv", w["vw"])
    wo = seg.din(tag + "_wo", w["ow"])
    wbuf = getattr(seg, "_wbuf", None)
    if wbuf is None:
        wbuf = seg.sb("wbuf", [128, DC * 1024], BF16)
        seg._wbuf = wbuf
    A.reset()
    wqv, wq_op = load_weight_bf16(seg, tag + "q", wq, D, D, tile=wbuf,
                                  deps=getattr(seg, "_wbuf_war", []),
                                  stream="wbuf")
    qevs = emit_proj(seg, tag + "q", wqv, wq_op, xv, DC, D,
                     evac_to_dram(seg, tag + "qe", qd, alloc=A.alloc),
                     lnops, banks)
    wkv, wk_op = load_weight_bf16(seg, tag + "k", wk, D, D, tile=wbuf,
                                  deps=qevs, stream=tag + "_w")
    kevs = emit_proj(seg, tag + "k", wkv, wk_op, xv, DC, D,
                     evac_to_dram(seg, tag + "ke", kd, alloc=A.alloc),
                     lnops, banks)
    wvv, wv_op = load_weight_bf16(seg, tag + "v", wv, D, D, tile=wbuf,
                                  deps=kevs, stream="wbuf")
    vevs = emit_proj_tok(seg, tag + "v", wvv, wv_op, xv, DC, D, vd, lnops,
                         banks, alloc=A.alloc)
    A.reset()
    aouts = emit_attention(seg, tag + "_at", qd, kd, vd, od,
                           qevs + kevs + vevs, causal=True, scale=1.0 / 32.0,
                           pbs=banks, alloc=A.alloc)
    oldv = b.dma("sync", xv, od.rearrange("(c p) t -> p c t", p=128),
                 deps=aouts, tag=tag + "_oldv")
    wov, wo_op = load_weight_bf16(seg, tag + "o", wo, D, D, tile=wbuf,
                                  deps=vevs, stream="wbuf")

    def resadd(oc, tb, pb, mm):
        ap = rv[:, oc, tb * TB:(tb + 1) * TB]
        return pb.read(
            "vector",
            lambda e, a=ap, p=pb.tile[:, 0:TB]: e.tensor_tensor(
                a, a, p, OP.add),
            tag=tag + "_res")

    oevs = emit_proj(seg, tag + "o", wov, wo_op, xv, DC, D, resadd,
                     [oldv], banks)
    seg._wbuf_war = oevs
    return oevs


def sub_l0_attn_b(seg, rv, xv, w, deps):
    b = seg.b
    tag = "l0ab"
    banks = seg._shared_banks[0:6]
    if not hasattr(seg, "A"):
        seg.A = seg.make_arena("A", 54 * 1024)
    A = seg.A
    A.reset()
    lnops = emit_layernorm(seg, rv, xv, deps, banks, eps=1e-5,
                           tag=tag + "_ln", alloc=A.alloc)
    qd = seg.dram(tag + "_q", [D, T], BF16)
    kd = seg.dram(tag + "_k", [D, T], BF16)
    vd = seg.dram(tag + "_v", [T, D], BF16)
    od = seg.dram(tag + "_o", [D, T], BF16)
    wq = seg.din(tag + "_wq", w["Wq"])
    wk = seg.din(tag + "_wk", w["Wk"])
    wv = seg.din(tag + "_wv", w["Wv"])
    wo = seg.din(tag + "_wo", w["Wo"])
    bqc, bq_op = seg.const(tag + "_bq", w["bq"].reshape(DC, 128).T.copy())
    bkc, bk_op = seg.const(tag + "_bk", w["bk"].reshape(DC, 128).T.copy())
    bvc, bv_op = seg.const(tag + "_bv", bf(w["bv"].reshape(1, D)))
    boc, bo_op = seg.const(tag + "_bo", w["bo"].reshape(DC, 128).T.copy())
    wbuf = getattr(seg, "_wbuf", None)
    if wbuf is None:
        wbuf = seg.sb("wbuf", [128, DC * 1024], BF16)
        seg._wbuf = wbuf
    A.reset()
    wqv, wq_op = load_weight_bf16(seg, tag + "q", wq, D, D, tile=wbuf,
                                  deps=getattr(seg, "_wbuf_war", []),
                                  stream="wbuf")
    qevs = emit_proj(seg, tag + "q", wqv, wq_op, xv, DC, D,
                     evac_to_dram(seg, tag + "qe", qd, bias=bqc,
                                  bias_op=bq_op, alloc=A.alloc),
                     lnops, banks)
    wkv, wk_op = load_weight_bf16(seg, tag + "k", wk, D, D, tile=wbuf,
                                  deps=qevs, stream="wbuf")
    kevs = emit_proj(seg, tag + "k", wkv, wk_op, xv, DC, D,
                     evac_to_dram(seg, tag + "ke", kd, bias=bkc,
                                  bias_op=bk_op, alloc=A.alloc),
                     lnops, banks)
    wvv, wv_op = load_weight_bf16(seg, tag + "v", wv, D, D, tile=wbuf,
                                  deps=kevs, stream="wbuf")
    vevs = emit_proj_tok(seg, tag + "v", wvv, wv_op, xv, DC, D, vd, lnops,
                         banks, bias=(bvc, bv_op), alloc=A.alloc)
    xh = (2.0 ** 8) ** (1.0 / HEADS)
    slopes = np.asarray([1.0 / xh ** (i + 1) for i in range(HEADS)], np.float32)
    A.reset()
    aouts = emit_attention(seg, tag + "_at", qd, kd, vd, od,
                           qevs + kevs + vevs, causal=False, scale=1.0 / 8.0,
                           alibi_slopes=slopes, pbs=banks, alloc=A.alloc)
    oldv = b.dma("sync", xv, od.rearrange("(c p) t -> p c t", p=128),
                 deps=aouts, tag=tag + "_oldv")
    wov, wo_op = load_weight_bf16(seg, tag + "o", wo, D, D, tile=wbuf,
                                  deps=vevs, stream="wbuf")

    def resadd(oc, tb, pb, mm):
        ap = rv[:, oc, tb * TB:(tb + 1) * TB]
        return pb.read(
            "vector",
            lambda e, a=ap, p=pb.tile[:, 0:TB], bo_=boc[:, oc:oc + 1]:
                e.scalar_tensor_tensor(a, p, bo_, a, OP.add, OP.add),
            deps=[bo_op], tag=tag + "_res")

    oevs = emit_proj(seg, tag + "o", wov, wo_op, xv, DC, D, resadd,
                     [oldv], banks)
    seg._wbuf_war = oevs
    return oevs


HMOE = 2816
HCM = HMOE // 128


def rope_tables():
    inv = 1.0 / 10000.0 ** (np.arange(0, HD, 2, dtype=np.float32) / HD)
    t = np.arange(T, dtype=np.float32)
    fr = np.einsum("i,j->ij", t, inv)
    emb = np.concatenate([fr, fr], -1)
    cos = np.cos(emb).astype(np.float32).T
    sin = np.sin(emb).astype(np.float32).T
    return bf(np.tile(cos, (2, 1))), bf(np.tile(sin, (2, 1)))


def rot_half_mat():
    R = np.zeros((HD, HD), np.float32)
    for i in range(32):
        R[i, 32 + i] = -1.0
        R[32 + i, i] = 1.0
    R2 = np.zeros((128, 128), np.float32)
    R2[0:64, 0:64] = R
    R2[64:128, 64:128] = R
    return R2.T.copy()


def evac_rope_to_dram(seg, tag, dram_fm, rotbank, r2t, r2t_op, cos2, sin2,
                      cs_op, alloc=None):
    alloc = alloc or seg.sb
    stages = [alloc(f"{tag}_stg{i}", [128, TB], BF16) for i in range(3)]
    t1 = alloc(f"{tag}_t1", [128, TB], F32)
    t2 = alloc(f"{tag}_t2", [128, TB], F32)
    state = {"i": 0, "last_st": [None] * 3}

    def out_fn(oc, tb, pb, mm):
        si = state["i"] % 3
        state["i"] += 1
        stg = stages[si]
        qe = pb.read("vector",
                     lambda e, o=stg[:], p=pb.tile[:, 0:TB]: e.tensor_copy(o, p),
                     deps=[state["last_st"][si]], tag=tag + "_qe")
        rot = rotbank.mm_acc([(rotbank.tile[:, 0:TB], r2t[:], stg[:])],
                             deps=[qe, r2t_op], tag=tag + "_rot")
        tc = seg.b.emit("vector",
                        lambda e, o=t2[:], i=stg[:],
                               c=cos2[:, tb * TB:(tb + 1) * TB]:
                            e.tensor_tensor(o, i, c, OP.mult),
                        deps=[qe] + list(cs_op), tag=tag + "_tc")
        ta = rotbank.read("vector",
                          lambda e, o=t1[:], p=rotbank.tile[:, 0:TB],
                                 s=sin2[:, tb * TB:(tb + 1) * TB]:
                              e.tensor_tensor(o, p, s, OP.mult),
                          tag=tag + "_ta")
        td = seg.b.emit("vector",
                        lambda e, o=stg[:], a=t1[:], c=t2[:]:
                            e.tensor_tensor(o, a, c, OP.add),
                        deps=[ta, tc], tag=tag + "_td")
        st = seg.b.dma("sync",
                       dram_fm[oc * 128:(oc + 1) * 128, tb * TB:(tb + 1) * TB],
                       stg[:], deps=[td], tag=tag + "_st",
                       stream=f"{tag}_st{si}")
        state["last_st"][si] = st
        return st

    return out_fn


def sub_l0_ffn(seg, rv, xv, w, deps, inst):
    b = seg.b
    tag = f"l0f{inst}"
    banks = seg._shared_banks[0:6]
    if not hasattr(seg, "A"):
        seg.A = seg.make_arena("A", 54 * 1024)
    A = seg.A
    g1c, g1_op = seg.const("l0f_g1", w["g1"].reshape(DC, 128).T.copy())
    g2c, g2_op = seg.const("l0f_g2", w["g2"].reshape(DC, 128).T.copy())
    A.reset()
    lnops = emit_layernorm(seg, rv, xv, deps, banks, eps=1e-5,
                           tag=tag + "_ln", out2=rv, alloc=A.alloc)
    A.reset()
    rmops = emit_layernorm(seg, rv, xv, lnops, banks, eps=1e-6,
                           tag=tag + "_r1", rms=True, gain=g1c, gain_op=g1_op,
                           alloc=A.alloc)
    qd = seg.dram(tag + "_q", [D, T], BF16)
    kd = seg.dram(tag + "_k", [D, T], BF16)
    vd = seg.dram(tag + "_v", [T, D], BF16)
    od = seg.dram(tag + "_o", [D, T], BF16)
    wq = seg.din("l0f_wq", w["wq"])
    wk = seg.din("l0f_wk", w["wk"])
    wv = seg.din("l0f_wv", w["wv"])
    wo = seg.din("l0f_wo", w["wo"])
    cosA, sinA = rope_tables()
    cos2, cos_op = seg.const("rope_cos", cosA)
    sin2, sin_op = seg.const("rope_sin", sinA)
    r2t, r2t_op = seg.const("rope_r2t", bf(rot_half_mat()))
    cs_op = cos_op
    wbuf = getattr(seg, "_wbuf", None)
    if wbuf is None:
        wbuf = seg.sb("wbuf", [128, DC * 1024], BF16)
        seg._wbuf = wbuf
    rotb = banks[2]
    A.reset()
    rope_fn = evac_rope_to_dram(seg, tag + "qe", qd, rotb, r2t, r2t_op,
                                cos2, sin2, [cos_op, sin_op], alloc=A.alloc)
    wqv, wq_op = load_weight_bf16(seg, tag + "q", wq, D, D, tile=wbuf,
                                  deps=getattr(seg, "_wbuf_war", []),
                                  stream="wbuf")
    qevs = emit_proj(seg, tag + "q", wqv, wq_op, xv, DC, D,
                     rope_fn, rmops, banks[0:2])
    rope_fn2 = evac_rope_to_dram(seg, tag + "ke", kd, rotb, r2t, r2t_op,
                                 cos2, sin2, [cos_op, sin_op], alloc=A.alloc)
    wkv, wk_op = load_weight_bf16(seg, tag + "k", wk, D, D, tile=wbuf,
                                  deps=qevs, stream="wbuf")
    kevs = emit_proj(seg, tag + "k", wkv, wk_op, xv, DC, D,
                     rope_fn2, rmops, banks[0:2])
    wvv, wv_op = load_weight_bf16(seg, tag + "v", wv, D, D, tile=wbuf,
                                  deps=kevs, stream="wbuf")
    vevs = emit_proj_tok(seg, tag + "v", wvv, wv_op, xv, DC, D, vd, rmops,
                         banks, alloc=A.alloc)
    A.reset()
    aouts = emit_attention(seg, tag + "_at", qd, kd, vd, od,
                           qevs + kevs + vevs, causal=True, scale=1.0 / 8.0,
                           pbs=banks, alloc=A.alloc)
    oldv = b.dma("sync", xv, od.rearrange("(c p) t -> p c t", p=128),
                 deps=aouts, tag=tag + "_oldv")
    wov, wo_op = load_weight_bf16(seg, tag + "o", wo, D, D, tile=wbuf,
                                  deps=vevs, stream="wbuf")

    def resadd(oc, tb, pb, mm):
        ap = rv[:, oc, tb * TB:(tb + 1) * TB]
        return pb.read(
            "vector",
            lambda e, a=ap, p=pb.tile[:, 0:TB]: e.tensor_tensor(
                a, a, p, OP.add),
            tag=tag + "_res")

    hops = emit_proj(seg, tag + "o", wov, wo_op, xv, DC, D, resadd,
                     [oldv], banks)
    seg._wbuf_war = hops
    A.reset()
    r2ops = emit_layernorm(seg, rv, xv, hops, banks, eps=1e-6,
                           tag=tag + "_r2", rms=True, gain=g2c, gain_op=g2_op,
                           alloc=A.alloc)
    A.reset()
    moeops = emit_l0_moe(seg, tag, rv, xv, w, r2ops, banks, alloc=A.alloc,
                         phase_dep=r2ops[-1])
    return moeops


def emit_l0_moe(seg, tag, rv, xv, w, deps, banks, alloc=None, phase_dep=None):
    b = seg.b
    alloc = alloc or seg.sb
    wr = seg.din("l0f_wr", w["w_router"])
    wg = seg.din("l0f_wg", w["wg"])
    wu = seg.din("l0f_wu", w["wu"])
    wd = seg.din("l0f_wd", w["wd"])
    NE = 8
    wrt = seg.sb(tag + "_wrt", [128, DC * 8], BF16)
    wrv = wrt[:].rearrange("p (c m) -> p c m", c=DC)
    wr_op = b.dma("gpsimd", wrv, wr.rearrange("(c p) m -> p c m", p=128),
                  tag=tag + "_wr", stream=tag + "_wrs", wait_all=True)
    logit = alloc(tag + "_lg", [128, 16 * 8], F32)
    lgv = logit[:].rearrange("p (a b) -> p a b", a=16)
    pb_l = banks[0]
    lg_ops = []
    for tt in range(16):
        mms = [(pb_l.tile[:, 0:8], xv[:, kc, tt * 128:(tt + 1) * 128],
                wrv[:, kc, :]) for kc in range(DC)]
        mm = pb_l.mm_acc(mms, deps=list(deps) + [wr_op], tag=tag + "_lmm")
        ev = pb_l.read("vector",
                       lambda e, o=lgv[:, tt, :], p=pb_l.tile[:, 0:8]:
                           e.tensor_copy(o, p),
                       tag=tag + "_lev")
        lg_ops.append(ev)
    comb = alloc(tag + "_cb", [128, 16 * 8], F32)
    cbv = comb[:].rearrange("p (a b) -> p a b", a=16)
    scr = alloc(tag + "_sc", [128, 16 * 4], F32)
    scv = scr[:].rearrange("p (a b) -> p a b", a=16)
    chains = []
    for tt in range(16):
        ch = []
        lg = lgv[:, tt, :]
        cb = cbv[:, tt, :]
        m1 = scv[:, tt, 0:1]
        m2 = scv[:, tt, 1:2]
        dn = scv[:, tt, 2:3]

        nm1 = scv[:, tt, 3:4]

        def s1(d2, lg=lg, m1=m1, tt=tt):
            return b.emit("vector", lambda e: e.reduce_max(m1, lg, AX),
                          deps=d2 + [lg_ops[tt]], tag=tag + "_m1")
        def s1b(d2, m1=m1, nm1=nm1):
            return b.emit("vector", lambda e: e.tensor_scalar(
                nm1, m1, -1.0, None, OP.mult), deps=d2, tag=tag + "_nm1")
        def s2(d2, lg=lg, m1=m1, cb=cb):
            return b.emit("vector", lambda e: e.tensor_scalar(
                cb, lg, m1, None, OP.is_ge),
                deps=d2, tag=tag + "_am")
        def s2b(d2, lg=lg, cb=cb):
            return b.emit("vector", lambda e: e.scalar_tensor_tensor(
                cb, cb, -1e30, lg, OP.mult, OP.add),
                deps=d2, tag=tag + "_msk")
        def s3(d2, cb=cb, m2=m2):
            return b.emit("vector", lambda e: e.reduce_max(m2, cb, AX),
                          deps=d2, tag=tag + "_m2")
        def s4(d2, lg=lg, m2=m2, cb=cb):
            return b.emit("vector", lambda e: e.tensor_scalar(
                cb, lg, m2, None, OP.is_ge),
                deps=d2, tag=tag + "_keep")
        def s5(d2, lg=lg, nm1=nm1):
            return b.emit("scalar", lambda e: e.activation(
                lg, lg, AF.Exp, bias=nm1),
                deps=d2, tag=tag + "_exp")
        def s6(d2, lg=lg, cb=cb):
            return b.emit("vector", lambda e: e.tensor_tensor(
                cb, cb, lg, OP.mult), deps=d2, tag=tag + "_pm")
        def s7(d2, cb=cb, dn=dn):
            return b.emit("vector", lambda e: e.reduce_sum(dn, cb, AX),
                          deps=d2, tag=tag + "_dn")
        def s8(d2, dn=dn):
            return b.emit("vector", lambda e: e.reciprocal(dn, dn),
                          deps=d2, tag=tag + "_rc")
        def s9(d2, cb=cb, dn=dn):
            return b.emit("vector", lambda e: e.tensor_scalar(
                cb, cb, dn, None, OP.mult), deps=d2, tag=tag + "_nw")
        ch += [s1, s1b, s2, s2b, s3, s4, s5, s6, s7, s8, s9]
        chains.append(ch)
    comb_ops = wavefront(chains)
    pb_t = banks[1]
    crow_sb = alloc(tag + "_crs", [1, TB], F32)
    wgt = [alloc(tag + f"_wgt{i}", [128, DC * 128], BF16) for i in range(2)]
    wut = [alloc(tag + f"_wut{i}", [128, DC * 128], BF16) for i in range(2)]
    wdt = [alloc(tag + f"_wdt{i}", [128, HCM * 128], BF16) for i in range(2)]
    act = alloc(tag + "_act", [128, HCM * TB], BF16)
    actv = act[:].rearrange("p (c t) -> p c t", c=HCM)
    cbsb = alloc(tag + "_cbs", [128, TB], F32)
    gs = [alloc(tag + f"_gs{i}", [128, TB], BF16) for i in range(2)]
    pb_g, pb_u, pb_c, pb_d = banks[2], banks[3], banks[4], banks[5]
    wg_last = [phase_dep, phase_dep]
    wu_last = [phase_dep, phase_dep]
    wd_last = [phase_dep, phase_dep]
    act_war = None
    cb_last = None
    rd = None
    for x in range(NE):
        for tb in range(NTB):
            ts = tb * TB
            crow_war = list(pb_t.readers)
            for j in range(4):
                def ftr(e, o=pb_t.tile[0:1, j * 128:(j + 1) * 128],
                        c=cbv[:, tb * 4 + j, x:x + 1], i=seg.identf[:],
                        first=(j == 0), last=(j == 3)):
                    return e.matmul(o, c, i, start=first, stop=last)
                trop = b.emit("tensor", ftr,
                              deps=(comb_ops + crow_war + [seg.identf_op])
                              if j == 0 else [], tag=tag + "_ctr")
            pb_t.writer = trop
            pb_t.readers = []
            crev = pb_t.read("vector",
                             lambda e, o=crow_sb[:], p=pb_t.tile[0:1, 0:TB]:
                                 e.tensor_copy(o, p),
                             deps=[cb_last], tag=tag + "_crev")
            bcast_row(seg, pb_c, crow_sb[:], TB,
                      deps=[crev], tag=tag + "_cbb")
            cb_ev = pb_c.read("vector",
                              lambda e, o=cbsb[:], p=pb_c.tile[:, 0:TB]:
                                  e.tensor_copy(o, p),
                              tag=tag + "_cbe")
            cb_last = cb_ev
            pend = []
            ev_ops = []

            def flush(pend=pend, ev_ops=ev_ops):
                hcp, g_evp, u_evp = pend.pop(0)
                ev = b.emit("vector",
                            lambda e, o=actv[:, hcp, :], g=gs[hcp % 2][:]:
                                e.tensor_tensor(o, o, g, OP.mult),
                            deps=[u_evp, g_evp], tag=tag + "_act")
                ev_ops.append(ev)

            for hc in range(HCM):
                par = hc % 2
                wg_ld = b.dma("gpsimd",
                              wgt[par][:].rearrange("p (c m) -> p c m", c=DC),
                              wg[x].rearrange("(c p) m -> p c m", p=128)
                              [:, :, hc * 128:(hc + 1) * 128],
                              deps=[wg_last[par]], tag=tag + "_wgl",
                              stream=f"{tag}_wg{par}")
                wu_ld = b.dma("gpsimd",
                              wut[par][:].rearrange("p (c m) -> p c m", c=DC),
                              wu[x].rearrange("(c p) m -> p c m", p=128)
                              [:, :, hc * 128:(hc + 1) * 128],
                              deps=[wu_last[par]], tag=tag + "_wul",
                              stream=f"{tag}_wu{par}")
                wgv = wgt[par][:].rearrange("p (c m) -> p c m", c=DC)
                wuv = wut[par][:].rearrange("p (c m) -> p c m", c=DC)
                gmm = pb_g.mm_acc(
                    [(pb_g.tile[:, 0:TB], wgv[:, kc, :],
                      xv[:, kc, ts:ts + TB]) for kc in range(DC)],
                    deps=list(deps) + [wg_ld], tag=tag + "_gmm")
                umm = pb_u.mm_acc(
                    [(pb_u.tile[:, 0:TB], wuv[:, kc, :],
                      xv[:, kc, ts:ts + TB]) for kc in range(DC)],
                    deps=list(deps) + [wu_ld], tag=tag + "_umm")
                wg_last[par] = gmm
                wu_last[par] = umm
                gs_war = ev_ops[hc - 2] if hc >= 2 else None
                g_ev = pb_g.read("scalar",
                                 lambda e, o=gs[par][:], p=pb_g.tile[:, 0:TB]:
                                     e.activation(o, p, AF.Silu),
                                 deps=[gs_war], tag=tag + "_silu")
                u_ev = pb_u.read("vector",
                                 lambda e, o=actv[:, hc, :],
                                        p=pb_u.tile[:, 0:TB]:
                                     e.tensor_tensor(o, p, cbsb[:], OP.mult),
                                 deps=[cb_ev, act_war],
                                 tag=tag + "_ue")
                pend.append((hc, g_ev, u_ev))
                if len(pend) > 1:
                    flush()
            flush()
            ev_last = ev_ops[-1]
            for dc in range(DC):
                par = dc % 2
                wd_ld = b.dma("gpsimd",
                              wdt[par][:].rearrange("p (c m) -> p c m", c=HCM),
                              wd[x].rearrange("(c p) m -> p c m", p=128)
                              [:, :, dc * 128:(dc + 1) * 128],
                              deps=[wd_last[par]], tag=tag + "_wdl",
                              stream=f"{tag}_wd{par}")
                wdv = wdt[par][:].rearrange("p (c m) -> p c m", c=HCM)
                mms = []
                for hc in range(HCM):
                    mdeps = [wd_ld] if hc == 0 else []
                    mms.append((pb_d.tile[:, 0:TB], wdv[:, hc, :],
                                actv[:, hc, :], mdeps))
                dmm = pb_d.mm_acc(mms, deps=[ev_last], tag=tag + "_dmm")
                wd_last[par] = dmm
                act_war = dmm
                rd = pb_d.read("vector",
                               lambda e, a=rv[:, dc, ts:ts + TB],
                                      p=pb_d.tile[:, 0:TB]:
                                   e.tensor_tensor(a, a, p, OP.add),
                               tag=tag + "_racc")
    return [rd]


def emit_softplus(seg, b, out_ap, in_ap, scr1, scr2, deps, tag, bias=None,
                  bias_op=None):
    import os
    one, one_op = seg.constcol(1.0)
    bdeps = [bias_op] if bias_op else []
    if os.environ.get("SOFTPASS"):
        return b.emit("vector", lambda e: e.tensor_copy(out_ap, in_ap),
                      deps=list(deps), tag=tag + "_passcp")
    a1 = b.emit("scalar", lambda e: e.activation(
        scr1[:], in_ap, AF.Abs, bias=bias if bias is not None else 0.0),
        deps=list(deps) + bdeps, tag=tag + "_abs")
    a2 = b.emit("scalar", lambda e: e.activation(scr1[:], scr1[:], AF.Exp,
                                                 scale=-1.0),
                deps=[a1], tag=tag + "_exp")
    a3 = b.emit("scalar", lambda e: e.activation(scr1[:], scr1[:], AF.Ln,
                                                 bias=one[0:128, 0:1]),
                deps=[a2, one_op], tag=tag + "_ln")
    if bias is not None:
        a4 = b.emit("vector", lambda e: e.tensor_scalar(
            scr2[:], in_ap, bias, 0.0, OP.add, OP.max),
            deps=list(deps) + bdeps + [a1], tag=tag + "_max")
    else:
        a4 = b.emit("vector", lambda e: e.tensor_scalar(
            scr2[:], in_ap, 0.0, None, OP.max),
            deps=list(deps) + [a1], tag=tag + "_max")
    a5 = b.emit("vector", lambda e: e.tensor_tensor(out_ap, scr1[:], scr2[:],
                                                    OP.add),
                deps=[a3, a4], tag=tag + "_add")
    return a5


def emit_nb_bcast_build(seg, tag, A, pb_row, pb_bc, sources, nn=16):
    b = seg.b
    brow = A.alloc(tag + "_brow", [1, TB], F32)
    bstg = [A.alloc(tag + f"_bstg{i}", [128, TB], BF16) for i in range(2)]
    last_bst = [None, None]
    bld_evs = []
    cnt = 0
    for n in range(nn):
        for (srcv, dramv, dep_evs, scale_cols, sc_op) in sources:
            for tb in range(NTB):
                war = list(pb_row.readers)
                for j in range(4):
                    def ftr(e, o=pb_row.tile[0:1, j * 128:(j + 1) * 128],
                            c=srcv[:, tb * 4 + j, n:n + 1], i=seg.identf[:],
                            first=(j == 0), last=(j == 3)):
                        return e.matmul(o, c, i, start=first, stop=last)
                    trop = b.emit("tensor", ftr,
                                  deps=(dep_evs + war + [seg.identf_op])
                                  if j == 0 else [], tag=tag + "_btr")
                pb_row.writer = trop
                pb_row.readers = []
                crev = pb_row.read("vector",
                                   lambda e, o=brow[:], p=pb_row.tile[0:1, 0:TB]:
                                       e.tensor_copy(o, p),
                                   tag=tag + "_brev")
                bcast_row(seg, pb_bc, brow[:], TB, deps=[crev],
                          tag=tag + "_bbc")
                si = cnt % 2
                cnt += 1
                if scale_cols is not None:
                    ev = pb_bc.read(
                        "vector",
                        lambda e, o=bstg[si][:], p=pb_bc.tile[:, 0:TB],
                               sc=scale_cols[:, n:n + 1]:
                            e.tensor_scalar(o, p, sc, None, OP.mult),
                        deps=[sc_op, last_bst[si]], tag=tag + "_bsc")
                else:
                    ev = pb_bc.read(
                        "vector",
                        lambda e, o=bstg[si][:], p=pb_bc.tile[:, 0:TB]:
                            e.tensor_copy(o, p),
                        deps=[last_bst[si]], tag=tag + "_bsc")
                st = b.dma("sync", dramv[n, :, tb * TB:(tb + 1) * TB],
                           bstg[si][:], deps=[ev], tag=tag + "_bst",
                           stream=f"{tag}_bb{si}")
                last_bst[si] = st
                bld_evs.append(st)
    return bld_evs


def sub_l1_attn_a(seg, rv, xv, w, deps, upto=99):
    b = seg.b
    tag = "l1aa"
    banks = seg._shared_banks[0:6]
    if not hasattr(seg, "A"):
        seg.A = seg.make_arena("A", 54 * 1024)
    A = seg.A
    n1c, n1_op = seg.const(tag + "_n1", w["norm1_w"].reshape(DC, 128).T.copy())
    n2c, n2_op = seg.const(tag + "_n2", w["norm2_w"].reshape(DC, 128).T.copy())
    Dpc, Dp_op = seg.const(tag + "_dp", w["D_param"].reshape(DC, 128).T.copy())
    scc, sc_op = seg.const(tag + "_sc", w["scale"].reshape(DC, 128).T.copy())
    bdtc, bdt_op = seg.const(tag + "_bdt", w["bdt"].reshape(DC, 128).T.copy())
    b1c, b1_op = seg.const(tag + "_b1", w["b1"].reshape(32, 128).T.copy())
    b2c, b2_op = seg.const(tag + "_b2", w["b2"].reshape(DC, 128).T.copy())
    bBr, bB_op = seg.const(tag + "_bB", bf(w["bB"].reshape(1, 16)))
    bCr, bC_op = seg.const(tag + "_bC", bf(w["bC"].reshape(1, 16)))
    wdt = seg.din(tag + "_wdtT", w["WdtT"])
    wbT = seg.din(tag + "_wbT", w["WBT"])
    wcT = seg.din(tag + "_wcT", w["WCT"])
    w1T = seg.din(tag + "_w1T", w["W1T"])
    w2T = seg.din(tag + "_w2T", w["W2T"])

    dtd = seg.dram(tag + "_dt", [DC, 128, T], BF16)
    ud = seg.dram(tag + "_u", [DC, 128, T], BF16)
    bvb_d = seg.dram(tag + "_bvb", [16, 128, T], BF16)
    cvb_d = seg.dram(tag + "_cvb", [16, 128, T], BF16)
    A.reset()
    lnops = emit_layernorm(seg, rv, xv, deps, banks, eps=1e-5,
                           tag=tag + "_ln", out2=rv, alloc=A.alloc)
    A.reset()
    rmops = emit_layernorm(seg, rv, xv, lnops, banks, eps=1e-6,
                           tag=tag + "_r1", rms=True, gain=n1c, gain_op=n1_op,
                           alloc=A.alloc)
    if upto <= 0:
        return rmops
    A.reset()
    wbuf = getattr(seg, "_wbuf", None)
    if wbuf is None:
        wbuf = seg.sb("wbuf", [128, DC * 1024], BF16)
        seg._wbuf = wbuf
    wv1, wv1_op = load_weight_bf16(seg, tag + "dt", wdt, D, D, tile=wbuf,
                                   deps=getattr(seg, "_wbuf_war", []),
                                   stream="wbuf")
    spA = [A.alloc(tag + f"_spA{i}", [128, TB], F32) for i in range(2)]
    spB = [A.alloc(tag + f"_spB{i}", [128, TB], F32) for i in range(2)]
    rct = [A.alloc(tag + f"_rct{i}", [128, TB], F32) for i in range(2)]
    dtt = [A.alloc(tag + f"_dtt{i}", [128, TB], F32) for i in range(2)]
    stg = [A.alloc(tag + f"_stg{i}", [128, TB], BF16) for i in range(2)]
    stg2 = [A.alloc(tag + f"_stg2{i}", [128, TB], BF16) for i in range(2)]
    last_std = {}

    def dt_evac(oc, tb, pb, mm):
        si = (oc * NTB + tb) % 2
        import os
        if os.environ.get("SIMPLE_DT"):
            d1 = last_std.get(("a", si))
            c0 = pb.read("vector", lambda e, o=stg[si][:], p=pb.tile[:, 0:TB]:
                         e.tensor_copy(o, p), deps=[d1], tag=tag + "_sdc")
            st1 = b.dma("sync", dtd[oc, :, tb * TB:(tb + 1) * TB], stg[si][:],
                        deps=[c0], tag=tag + "_dtst", stream=f"{tag}_dt{si}")
            st2 = b.dma("sync", ud[oc, :, tb * TB:(tb + 1) * TB], stg[si][:],
                        deps=[c0], tag=tag + "_ust", stream=f"{tag}_dt{si}")
            last_std[("a", si)] = st2
            return st2
        d1 = last_std.get(("a", si))
        d2 = last_std.get(("b", si))
        sp = emit_softplus(
            seg, b, dtt[si][:],
            pb.tile[:, 0:TB], spA[si], spB[si],
            [mm, d1, d2], tag + "_sp", bias=bdtc[:, oc:oc + 1],
            bias_op=bdt_op)
        pb.readers.append(sp)
        c1 = b.emit("vector", lambda e, o=stg[si][:], i=dtt[si][:]:
                    e.tensor_copy(o, i), deps=[sp, d1], tag=tag + "_dtc")
        st1 = b.dma("sync", dtd[oc, :, tb * TB:(tb + 1) * TB], stg[si][:],
                    deps=[c1], tag=tag + "_dtst", stream=f"{tag}_dt{si}")
        import os
        if os.environ.get("NORECIP"):
            st2b = b.dma("sync", ud[oc, :, tb * TB:(tb + 1) * TB], stg[si][:],
                         deps=[c1], tag=tag + "_ust", stream=f"{tag}_dt{si}")
            last_std[("a", si)] = st1
            last_std[("b", si)] = st2b
            return st2b
        r1 = b.emit("vector", lambda e, o=rct[si][:], i=dtt[si][:]:
                    e.reciprocal(o, i), deps=[c1], tag=tag + "_rcp")
        u1 = b.emit("vector",
                    lambda e, o=stg2[si][:], a=rct[si][:],
                           x_=xv[:, oc, tb * TB:(tb + 1) * TB]:
                        e.tensor_tensor(o, a, x_, OP.mult),
                    deps=[r1, d2], tag=tag + "_umul")
        st2 = b.dma("sync", ud[oc, :, tb * TB:(tb + 1) * TB], stg2[si][:],
                    deps=[u1], tag=tag + "_ust", stream=f"{tag}_dt{si}")
        last_std[("a", si)] = st1
        last_std[("b", si)] = st2
        return st2

    devs = emit_proj(seg, tag + "dt", wv1, wv1_op, xv, DC, D, dt_evac,
                     rmops, banks[0:2])
    seg._wbuf_war = devs
    if upto <= 1:
        return devs + rmops
    A.reset()
    wbT_t = seg.sb(tag + "_wbTt", [128, DC * 16], BF16)
    wcT_t = seg.sb(tag + "_wcTt", [128, DC * 16], BF16)
    wbv = wbT_t[:].rearrange("p (c m) -> p c m", c=DC)
    wcv = wcT_t[:].rearrange("p (c m) -> p c m", c=DC)
    wb_op = b.dma("gpsimd", wbv, wbT.rearrange("(c p) m -> p c m", p=128),
                  deps=list(deps), tag=tag + "_wbl", stream=tag + "_wbs")
    wc_op = b.dma("gpsimd", wcv, wcT.rearrange("(c p) m -> p c m", p=128),
                  tag=tag + "_wcl", stream=tag + "_wbs")
    acol_sb = seg.sb(tag + "_acols", [128, 16 * 2], F32)
    arow = seg.sb(tag + "_arow", [1, 16 * 2], F32)
    pb_r = banks[2]
    alogc, alog_op = seg.const(tag + "_alog", w["A_log"].reshape(1, 16).copy())
    a_e = b.emit("scalar", lambda e: e.activation(
        arow[0:1, 0:16], alogc[0:1, 0:16], AF.Exp),
        deps=[alog_op], tag=tag + "_aexp")
    a_n = b.emit("vector", lambda e: e.tensor_scalar(
        arow[0:1, 0:16], arow[0:1, 0:16], -1.0, None, OP.mult),
        deps=[a_e], tag=tag + "_aneg")
    a_r = b.emit("vector", lambda e: e.reciprocal(
        arow[0:1, 16:32], arow[0:1, 0:16]), deps=[a_n], tag=tag + "_arcp")
    bcast_row(seg, pb_r, arow[0:1, 0:32], 32, deps=[a_r], tag=tag + "_abc")
    a_ev = pb_r.read("vector", lambda e: e.tensor_copy(
        acol_sb[:], pb_r.tile[:, 0:32]), tag=tag + "_aev")
    acols = acol_sb[:].rearrange("p (k n) -> p k n", k=2)
    bv_tm = A.alloc(tag + "_bvtm", [128, 16 * 16], F32)
    cv_tm = A.alloc(tag + "_cvtm", [128, 16 * 16], F32)
    bvv = bv_tm[:].rearrange("p (a n) -> p a n", a=16)
    cvv = cv_tm[:].rearrange("p (a n) -> p a n", a=16)
    pb_b = banks[0]
    bc_evs = []
    for tt in range(16):
        for (wvx, rowc, row_op, outv) in ((wbv, bBr, bB_op, bvv),
                                          (wcv, bCr, bC_op, cvv)):
            mms = [(pb_b.tile[:, 0:16], xv[:, kc, tt * 128:(tt + 1) * 128],
                    wvx[:, kc, :]) for kc in range(DC)]
            mms.append((pb_b.tile[:, 0:16], seg.onesr_b[:], rowc[0:1, 0:16],
                        [row_op, seg.onesr_b_op]))
            mm = pb_b.mm_acc(mms, deps=rmops + [wb_op, wc_op],
                             tag=tag + "_bcmm")
            ev = pb_b.read("vector",
                           lambda e, o=outv[:, tt, :], p=pb_b.tile[:, 0:16]:
                               e.tensor_copy(o, p), tag=tag + "_bcev")
            bc_evs.append(ev)
    bld_evs = emit_nb_bcast_build(
        seg, tag, A, banks[3], banks[4],
        [(bvv, bvb_d, bc_evs, acols[:, 1, 0:16], a_ev),
         (cvv, cvb_d, bc_evs, None, None)], nn=16)
    if upto <= 2:
        return bld_evs + [a_ev]
    A.reset()
    dt_sb = [A.alloc(tag + "_dts", [128, T], BF16)] * 2
    u_sb = [A.alloc(tag + "_us", [128, T], BF16)] * 2
    bv_sb = A.alloc(tag + "_bvs", [128, T], BF16)
    cv_sb = A.alloc(tag + "_cvs", [128, T], BF16)
    dA = A.alloc(tag + "_dA", [128, T], F32)
    wt_ = A.alloc(tag + "_w", [128, T], F32)
    ht_ = A.alloc(tag + "_h", [128, T], F32)
    yt_ = A.alloc(tag + "_y", [128, T], F32)
    tt_ = A.alloc(tag + "_t", [128, T], BF16)
    last_dt_use = [[], []]
    last_bv_use = None
    last_cv_use = None
    prev_s1 = None
    prev_y1 = None
    prev_p2 = None
    rv_out_ops = []
    for dc in range(DC):
        dpar = 0
        ld_dt = b.dma("sync", dt_sb[dpar][:], dtd[dc, :, :],
                      deps=list(last_dt_use[dpar]) + (devs if dc < 2 else []),
                      tag=tag + "_lddt", stream=f"{tag}_sdt{dpar}")
        ld_u = b.dma("sync", u_sb[dpar][:], ud[dc, :, :],
                     deps=list(last_dt_use[dpar]) + (devs if dc < 2 else []),
                     tag=tag + "_ldu", stream=f"{tag}_sdt{dpar}")
        yz = b.emit("vector", lambda e: e.memset(yt_[:], 0.0),
                    deps=[prev_p2], tag=tag + "_yz")
        for n in range(16):
            ld_bv = b.dma("sync", bv_sb[:], bvb_d[n, :, :],
                          deps=[last_bv_use] +
                               (bld_evs if dc == 0 else []),
                          tag=tag + "_ldbv", stream=f"{tag}_sbv")
            ld_cv = b.dma("sync", cv_sb[:], cvb_d[n, :, :],
                          deps=[last_cv_use] +
                               (bld_evs if dc == 0 else []),
                          tag=tag + "_ldcv", stream=f"{tag}_scv")
            e1 = b.emit("scalar",
                        lambda e, sc=acols[:, 0, n:n + 1], i=dt_sb[dpar][:]:
                            e.activation(dA[:], i, AF.Exp, scale=sc),
                        deps=[a_ev, ld_dt, prev_s1], tag=tag + "_dA")
            w1_ = b.emit("vector",
                         lambda e, u=u_sb[dpar][:]: e.scalar_tensor_tensor(
                             wt_[:], dA[:], -1.0, u, OP.add, OP.mult),
                         deps=[e1, ld_u], tag=tag + "_wm")
            b1_ = b.emit("vector",
                         lambda e, bvx=bv_sb[:]: e.tensor_tensor(
                             wt_[:], wt_[:], bvx, OP.mult),
                         deps=[w1_, ld_bv], tag=tag + "_bm")
            s1 = b.emit("vector", lambda e: e.tensor_tensor_scan(
                ht_[:], dA[:], wt_[:], 0.0, OP.mult, OP.add),
                deps=[b1_], tag=tag + "_scan")
            m1 = b.emit("vector",
                        lambda e, cvx=cv_sb[:]: e.tensor_tensor(
                            tt_[:], ht_[:], cvx, OP.mult),
                        deps=[s1, ld_cv], tag=tag + "_cm")
            y1 = b.emit("vector", lambda e: e.tensor_tensor(
                yt_[:], yt_[:], tt_[:], OP.add),
                deps=[m1, yz, prev_y1], tag=tag + "_ya")
            prev_y1 = y1
            last_bv_use = b1_
            last_cv_use = m1
            prev_s1 = s1
            last_e1 = e1
            last_w1 = w1_
        last_dt_use[dpar] = [last_e1, last_w1]
        p1 = b.emit("vector",
                    lambda e, dcol=Dpc[:, dc:dc + 1], x_=xv[:, dc, :]:
                        e.scalar_tensor_tensor(yt_[:], x_, dcol, yt_[:],
                                               OP.mult, OP.add),
                    deps=[Dp_op], tag=tag + "_dp")
        p2 = b.emit("vector",
                    lambda e, scol=scc[:, dc:dc + 1], r_=rv[:, dc, :]:
                        e.scalar_tensor_tensor(r_, yt_[:], scol, r_,
                                               OP.mult, OP.add),
                    deps=[p1, sc_op], tag=tag + "_sc")
        prev_p2 = p2
        rv_out_ops.append(p2)
    if upto <= 3:
        return rv_out_ops
    A.reset()
    r2ops = emit_layernorm(seg, rv, xv, rv_out_ops, banks, eps=1e-6,
                           tag=tag + "_r2", rms=True, gain=n2c, gain_op=n2_op,
                           alloc=A.alloc)
    A.reset()
    w1t_ = [A.alloc(tag + f"_w1t{i}", [128, DC * 128], BF16) for i in range(2)]
    w2t_ = [A.alloc(tag + "_w2t", [128, 32 * 128], BF16)] * 2
    h_sb = A.alloc(tag + "_h", [128, 32 * TB], BF16)
    hv = h_sb[:].rearrange("p (c t) -> p c t", c=32)
    gsq = A.alloc(tag + "_gsq", [128, TB], F32)
    gcu = A.alloc(tag + "_gcu", [128, TB], F32)
    gth = A.alloc(tag + "_gth", [128, TB], F32)
    pb_h, pb_o = banks[2], banks[3]
    c_g = float(np.sqrt(2.0 / np.pi))
    w1_last = [r2ops[-1], r2ops[-1]]
    w2_last = [r2ops[-1], r2ops[-1]]
    h_war = None
    for tb in range(NTB):
        ts = tb * TB
        h_evs = []
        for hc in range(32):
            par = hc % 2
            w1ld = b.dma("gpsimd",
                         w1t_[par][:].rearrange("p (c m) -> p c m", c=DC),
                         w1T.rearrange("(c p) m -> p c m", p=128)
                         [:, :, hc * 128:(hc + 1) * 128],
                         deps=[w1_last[par]], tag=tag + "_w1l",
                         stream=f"{tag}_w1{par}")
            w1v = w1t_[par][:].rearrange("p (c m) -> p c m", c=DC)
            mm = pb_h.mm_acc(
                [(pb_h.tile[:, 0:TB], w1v[:, kc, :], xv[:, kc, ts:ts + TB])
                 for kc in range(DC)],
                deps=r2ops + [w1ld], tag=tag + "_hmm")
            w1_last[par] = mm
            g1 = pb_h.read("scalar",
                           lambda e, o=gsq[:], p=pb_h.tile[:, 0:TB],
                                  bc=b1c[:, hc:hc + 1]:
                               e.activation(o, p, AF.Square, bias=bc),
                           deps=[b1_op], tag=tag + "_gsq")
            g2 = pb_h.read("vector",
                           lambda e, o=gcu[:], p=pb_h.tile[:, 0:TB],
                                  bc=b1c[:, hc:hc + 1]:
                               e.tensor_scalar(o, p, bc, None, OP.add),
                           deps=[b1_op, g1], tag=tag + "_gs")
            g3 = b.emit("vector",
                        lambda e: e.scalar_tensor_tensor(
                            gth[:], gsq[:], 0.044715, gcu[:], OP.mult,
                            OP.mult),
                        deps=[g1, g2], tag=tag + "_g3")
            g4 = b.emit("vector", lambda e: e.tensor_tensor(
                gth[:], gth[:], gcu[:], OP.add), deps=[g3], tag=tag + "_g4")
            g5 = b.emit("scalar", lambda e: e.activation(
                gth[:], gth[:], AF.Tanh, scale=c_g), deps=[g4],
                tag=tag + "_g5")
            g6 = b.emit("vector", lambda e: e.scalar_tensor_tensor(
                gth[:], gth[:], 1.0, gcu[:], OP.add, OP.mult),
                deps=[g5], tag=tag + "_g6")
            g8 = b.emit("vector",
                        lambda e, o=hv[:, hc, 0:TB]: e.tensor_scalar(
                            o, gth[:], 0.5, None, OP.mult),
                        deps=[g6, h_war], tag=tag + "_g8")
            h_evs.append(g8)
        for dc in range(DC):
            par = 0
            w2ld = b.dma("gpsimd",
                         w2t_[par][:].rearrange("p (c m) -> p c m", c=32),
                         w2T.rearrange("(c p) m -> p c m", p=128)
                         [:, :, dc * 128:(dc + 1) * 128],
                         deps=[w2_last[par]], tag=tag + "_w2l",
                         stream=f"{tag}_w2{par}")
            w2v = w2t_[par][:].rearrange("p (c m) -> p c m", c=32)
            mms = []
            for hc in range(32):
                mdeps = [w2ld] if hc == 0 else []
                mms.append((pb_o.tile[:, 0:TB], w2v[:, hc, :], hv[:, hc, 0:TB],
                            mdeps))
            dmm = pb_o.mm_acc(mms, deps=[h_evs[-1]], tag=tag + "_omm")
            w2_last[par] = dmm
            h_war = dmm
            rd = pb_o.read("vector",
                           lambda e, a=rv[:, dc, ts:ts + TB],
                                  p=pb_o.tile[:, 0:TB],
                                  bc=b2c[:, dc:dc + 1]:
                               e.scalar_tensor_tensor(a, p, bc, a,
                                                      OP.add, OP.add),
                           deps=[b2_op], tag=tag + "_ores")
            rv_out_ops.append(rd)
    return [rv_out_ops[-1]]


def sub_l1_attn_b(seg, rv, xv, w, deps, salloc=None):
    b = seg.b
    tag = "l1ab"
    salloc = salloc or seg.sb
    banks = seg._shared_banks[0:6]
    if not hasattr(seg, "A"):
        seg.A = seg.make_arena("A", 54 * 1024)
    A = seg.A
    EC = 16
    gmc, gm_op = seg.const(tag + "_gm", w["gamma"].reshape(DC, 128).T.copy())
    btc, bt_op = seg.const(tag + "_bt", w["beta"].reshape(DC, 128).T.copy())
    cwc, cw_op = seg.const(tag + "_cw", w["conv_w"].reshape(EC, 128, 4)
                           .transpose(1, 0, 2).reshape(128, EC * 4).copy())
    cbc, cb_op = seg.const(tag + "_cb", w["conv_b"].reshape(EC, 128).T.copy())
    bdtc, bdt_op = seg.const(tag + "_bdt", w["b_dt"].reshape(EC, 128).T.copy())
    dpc, dp_op = seg.const(tag + "_dpc", w["Dp"].reshape(EC, 128).T.copy())
    alogc, alog_op = seg.const(tag + "_alog", w["A_log"].reshape(EC, 128, 16)
                               .transpose(1, 0, 2).reshape(128, EC * 16).copy())
    w_in = seg.din(tag + "_winT", w["w_inT"])
    w_x = seg.din(tag + "_wxT", w["w_xT"])
    w_dt = seg.din(tag + "_wdtT", w["w_dtT"])
    w_out = seg.din(tag + "_woutT", w["w_outT"])
    xpd = seg.dram(tag + "_xp", [EC, 128, T], BF16)
    zd = seg.dram(tag + "_z", [EC, 128, T], BF16)
    xcd = seg.dram(tag + "_xc", [EC, 128, T], BF16)
    deltad = seg.dram(tag + "_dl", [EC, 128, T], BF16)
    yd = seg.dram(tag + "_y", [EC, 128, T], BF16)
    bvb_d = seg.dram(tag + "_bvb", [16, 128, T], BF16)
    cvb_d = seg.dram(tag + "_cvb", [16, 128, T], BF16)
    A.reset()
    lnops = emit_layernorm(seg, rv, xv, deps, banks, eps=1e-5,
                           tag=tag + "_ln", out2=rv, alloc=A.alloc)
    A.reset()
    l2ops = emit_layernorm(seg, rv, xv, lnops, banks, eps=1e-5,
                           tag=tag + "_l2", gain=gmc, gain_op=gm_op,
                           beta=btc, beta_op=bt_op, alloc=A.alloc)
    A.reset()
    wst = [A.alloc(tag + f"_wst{i}", [128, DC * 128], BF16) for i in range(2)]
    pstg = [A.alloc(tag + f"_pstg{i}", [128, TB], BF16) for i in range(3)]
    pb_p = banks[0]
    w_last = [l2ops[-1], l2ops[-1]]
    st_last = [None] * 3
    p1_sts = []
    scnt = 0
    for oc in range(32):
        par = oc % 2
        wld = b.dma("gpsimd", wst[par][:].rearrange("p (c m) -> p c m", c=DC),
                    w_in.rearrange("(c p) m -> p c m", p=128)
                    [:, :, oc * 128:(oc + 1) * 128],
                    deps=[w_last[par]], tag=tag + "_wl",
                    stream=f"{tag}_w{par}")
        wv_ = wst[par][:].rearrange("p (c m) -> p c m", c=DC)
        for tb in range(NTB):
            mm = pb_p.mm_acc(
                [(pb_p.tile[:, 0:TB], wv_[:, kc, :],
                  xv[:, kc, tb * TB:(tb + 1) * TB]) for kc in range(DC)],
                deps=l2ops + [wld], tag=tag + "_pmm")
            si = scnt % 3
            scnt += 1
            if oc < EC:
                ev = pb_p.read("vector",
                               lambda e, o=pstg[si][:], p=pb_p.tile[:, 0:TB]:
                                   e.tensor_copy(o, p),
                               deps=[st_last[si]], tag=tag + "_pev")
                st = b.dma("sync", xpd[oc, :, tb * TB:(tb + 1) * TB],
                           pstg[si][:], deps=[ev], tag=tag + "_pst",
                           stream=f"{tag}_p{si}")
            else:
                ev = pb_p.read("scalar",
                               lambda e, o=pstg[si][:], p=pb_p.tile[:, 0:TB]:
                                   e.activation(o, p, AF.Silu),
                               deps=[st_last[si]], tag=tag + "_zev")
                st = b.dma("sync", zd[oc - EC, :, tb * TB:(tb + 1) * TB],
                           pstg[si][:], deps=[ev], tag=tag + "_zst",
                           stream=f"{tag}_p{si}")
            st_last[si] = st
            p1_sts.append(st)
        w_last[par] = mm
    A.reset()
    cin = [A.alloc(tag + f"_cin{i}", [128, T + 4], BF16) for i in range(2)]
    cacc = A.alloc(tag + "_cacc", [128, T], F32)
    cstg = [A.alloc(tag + f"_cstg{i}", [128, T], BF16) for i in range(2)]
    cz_ops = [None, None]
    cst_last = [None, None]
    xc_sts = []
    for ec in range(EC):
        par = ec % 2
        z0 = b.emit("vector",
                    lambda e, o=cin[par][:, 0:4]: e.memset(o, 0.0),
                    deps=[cz_ops[par]], tag=tag + "_czero")
        ld = b.dma("sync", cin[par][:, 4:4 + T], xpd[ec, :, :],
                   deps=[cz_ops[par]] + (p1_sts if ec < 2 else []),
                   tag=tag + "_cld", stream=f"{tag}_c{par}")
        c1 = b.emit("vector",
                    lambda e, o=cacc[:], i=cin[par][:, 1:1 + T],
                           sc=cwc[:, ec * 4 + 0:ec * 4 + 1]:
                        e.tensor_scalar(o, i, sc, None, OP.mult),
                    deps=[z0, ld, cw_op, cz_ops[par]], tag=tag + "_c1")
        for j in range(1, 4):
            c1 = b.emit("vector",
                        lambda e, o=cacc[:], i=cin[par][:, 1 + j:1 + j + T],
                               sc=cwc[:, ec * 4 + j:ec * 4 + j + 1]:
                            e.scalar_tensor_tensor(o, i, sc, o, OP.mult,
                                                   OP.add),
                        deps=[c1], tag=tag + "_cj")
        sil = b.emit("scalar",
                     lambda e, o=cstg[par][:], i=cacc[:],
                            bc=cbc[:, ec:ec + 1]:
                         e.activation(o, i, AF.Silu, bias=bc),
                     deps=[c1, cb_op, cst_last[par]], tag=tag + "_csil")
        cz_ops[par] = sil
        st = b.dma("sync", xcd[ec, :, :], cstg[par][:], deps=[sil],
                   tag=tag + "_cst", stream=f"{tag}_cs{par}")
        cst_last[par] = st
        xc_sts.append(st)
    A.reset()
    wxt = salloc(tag + "_wxt", [128, EC * 96], BF16)
    wxv = wxt[:].rearrange("p (c m) -> p c m", c=EC)
    wx_op = b.dma("gpsimd", wxv, w_x.rearrange("(c p) m -> p c m", p=128),
                  deps=list(deps), tag=tag + "_wxl", stream=tag + "_wxs")
    xcb = A.alloc(tag + "_xcb", [128, EC * TB], BF16)
    xcbv = xcb[:].rearrange("p (c t) -> p c t", c=EC)
    dt_sb = salloc(tag + "_dtr", [64, T], BF16)
    bv_tm = A.alloc(tag + "_bvtm", [128, 16 * 16], F32)
    cv_tm = A.alloc(tag + "_cvtm", [128, 16 * 16], F32)
    bvv = bv_tm[:].rearrange("p (a n) -> p a n", a=16)
    cvv = cv_tm[:].rearrange("p (a n) -> p a n", a=16)
    pb_d = banks[0]
    pb_t2 = banks[1]
    xc_use = None
    bc_evs = []
    dt_evs = []
    for tb in range(NTB):
        lds = []
        for kc in range(EC):
            ld = b.dma("sync", xcbv[:, kc, :],
                       xcd[kc, :, tb * TB:(tb + 1) * TB],
                       deps=[xc_use] + (xc_sts if tb == 0 else []),
                       tag=tag + "_xld", stream=f"{tag}_xl{kc % 2}")
            lds.append(ld)
        mm = pb_d.mm_acc(
            [(pb_d.tile[0:64, 0:TB], wxv[:, kc, 0:64], xcbv[:, kc, :],
              [lds[kc]]) for kc in range(EC)],
            deps=[wx_op], tag=tag + "_dmm")
        ev = pb_d.read("vector",
                       lambda e, o=dt_sb[:, tb * TB:(tb + 1) * TB],
                              p=pb_d.tile[0:64, 0:TB]: e.tensor_copy(o, p),
                       tag=tag + "_dev")
        dt_evs.append(ev)
        for tt4 in range(4):
            tt = tb * 4 + tt4
            for (off, outv) in ((64, bvv), (80, cvv)):
                mms = [(pb_t2.tile[:, 0:16],
                        xcbv[:, kc, tt4 * 128:(tt4 + 1) * 128],
                        wxv[:, kc, off:off + 16], [lds[kc]])
                       for kc in range(EC)]
                mmb = pb_t2.mm_acc(mms, deps=[wx_op], tag=tag + "_bvmm")
                evb = pb_t2.read(
                    "vector",
                    lambda e, o=outv[:, tt, :], p=pb_t2.tile[:, 0:16]:
                        e.tensor_copy(o, p), tag=tag + "_bvev")
                bc_evs.append(evb)
        xc_use = mmb
    wdt_t = salloc(tag + "_wdtt", [64, T], BF16)
    wdt_v = wdt_t[:].rearrange("p (c m) -> p c m", c=EC)
    wdt_op = b.dma("gpsimd", wdt_v,
                   w_dt.rearrange("p (c m) -> p c m", c=EC),
                   deps=list(deps), tag=tag + "_wdtl", stream=tag + "_wdts")
    spA = [A.alloc(tag + f"_spA{i}", [128, TB], F32) for i in range(2)]
    spB = [A.alloc(tag + f"_spB{i}", [128, TB], F32) for i in range(2)]
    dstg = [A.alloc(tag + f"_dstg{i}", [128, TB], BF16) for i in range(2)]
    pb_sp = banks[2]
    dl_last = [None, None]
    dl_sts = []
    for ec in range(EC):
        for tb in range(NTB):
            si = (ec * NTB + tb) % 2
            mm = pb_sp.mm_acc(
                [(pb_sp.tile[:, 0:TB], wdt_v[:, ec, :],
                  dt_sb[:, tb * TB:(tb + 1) * TB],
                  [wdt_op, dt_evs[tb]])],
                tag=tag + "_spmm")
            sp = emit_softplus(seg, b, dstg[si][:], pb_sp.tile[:, 0:TB],
                               spA[si], spB[si], [mm, dl_last[si]],
                               tag + f"_sp{ec}_{tb}", bias=bdtc[:, ec:ec + 1],
                               bias_op=bdt_op)
            pb_sp.readers.append(sp)
            st = b.dma("sync", deltad[ec, :, tb * TB:(tb + 1) * TB],
                       dstg[si][:], deps=[sp], tag=tag + "_dlst",
                       stream=f"{tag}_dl{si}")
            dl_last[si] = st
            dl_sts.append(st)
    bld_evs = emit_nb_bcast_build(
        seg, tag, A, banks[3], banks[4],
        [(bvv, bvb_d, bc_evs, None, None),
         (cvv, cvb_d, bc_evs, None, None)], nn=16)
    acl = salloc(tag + "_acl", [128, EC * 16], F32)
    a_e = b.emit("scalar", lambda e: e.activation(acl[:], alogc[:], AF.Exp,
                                                  scale=-1.0),
                 deps=[alog_op], tag=tag + "_aexp")
    a_n = b.emit("vector", lambda e: e.tensor_scalar(
        acl[:], acl[:], -1.0, None, OP.mult), deps=[a_e], tag=tag + "_aneg")
    A.reset()
    dl_sb = A.alloc(tag + "_dls", [128, T], BF16)
    xc_sb = A.alloc(tag + "_xcs", [128, T], BF16)
    u2_sb = A.alloc(tag + "_u2", [128, T], BF16)
    zg_sb = A.alloc(tag + "_zg", [128, T], BF16)
    bv_sb = A.alloc(tag + "_bvs", [128, T], BF16)
    cv_sb = A.alloc(tag + "_cvs", [128, T], BF16)
    dA = A.alloc(tag + "_dA", [128, T], F32)
    wt_ = A.alloc(tag + "_w", [128, T], BF16)
    ht_ = A.alloc(tag + "_h", [128, T], BF16)
    yt_ = A.alloc(tag + "_y", [128, T], F32)
    tt_ = A.alloc(tag + "_t", [128, T], BF16)
    ystg = tt_
    dl_use = []
    xc_use2 = None
    zg_use = None
    bv_use = None
    cv_use = None
    prev_s1 = None
    prev_y1 = None
    prev_yst = None
    y_sts = []
    for ec in range(EC):
        ld_dl = b.dma("sync", dl_sb[:], deltad[ec, :, :],
                      deps=list(dl_use) + (dl_sts if ec == 0 else []),
                      tag=tag + "_lddl", stream=f"{tag}_s1")
        ld_xc = b.dma("sync", xc_sb[:], xcd[ec, :, :],
                      deps=[xc_use2] + (xc_sts if ec == 0 else []),
                      tag=tag + "_ldxc", stream=f"{tag}_s2")
        ld_zg = b.dma("sync", zg_sb[:], zd[ec, :, :],
                      deps=[zg_use] + (p1_sts if ec == 0 else []),
                      tag=tag + "_ldzg", stream=f"{tag}_s3")
        u2 = b.emit("vector", lambda e: e.tensor_tensor(
            u2_sb[:], dl_sb[:], xc_sb[:], OP.mult),
            deps=[ld_dl, ld_xc, prev_s1], tag=tag + "_u2")
        yz = b.emit("vector", lambda e: e.memset(yt_[:], 0.0),
                    deps=[prev_yst], tag=tag + "_yz")
        for n in range(16):
            ld_bv = b.dma("sync", bv_sb[:], bvb_d[n, :, :],
                          deps=[bv_use] + (bld_evs if ec == 0 else []),
                          tag=tag + "_ldbv", stream=f"{tag}_s4")
            ld_cv = b.dma("sync", cv_sb[:], cvb_d[n, :, :],
                          deps=[cv_use] + (bld_evs if ec == 0 else []),
                          tag=tag + "_ldcv", stream=f"{tag}_s5")
            e1 = b.emit("scalar",
                        lambda e, sc=acl[:, ec * 16 + n:ec * 16 + n + 1]:
                            e.activation(dA[:], dl_sb[:], AF.Exp, scale=sc),
                        deps=[a_n, ld_dl, prev_s1], tag=tag + "_dA")
            b1_ = b.emit("vector",
                         lambda e: e.tensor_tensor(wt_[:], u2_sb[:], bv_sb[:],
                                                   OP.mult),
                         deps=[u2, ld_bv, prev_s1], tag=tag + "_bm")
            s1 = b.emit("vector", lambda e: e.tensor_tensor_scan(
                ht_[:], dA[:], wt_[:], 0.0, OP.mult, OP.add),
                deps=[b1_, e1], tag=tag + "_scan")
            m1 = b.emit("vector",
                        lambda e: e.tensor_tensor(tt_[:], ht_[:], cv_sb[:],
                                                  OP.mult),
                        deps=[s1, ld_cv, prev_yst], tag=tag + "_cm")
            y1 = b.emit("vector", lambda e: e.tensor_tensor(
                yt_[:], yt_[:], tt_[:], OP.add),
                deps=[m1, yz, prev_y1], tag=tag + "_ya")
            bv_use = b1_
            cv_use = m1
            prev_s1 = s1
            prev_y1 = y1
        dl_use = [e1, u2]
        xc_use2 = None
        p1 = b.emit("vector",
                    lambda e, dcol=dpc[:, ec:ec + 1]:
                        e.scalar_tensor_tensor(yt_[:], xc_sb[:], dcol, yt_[:],
                                               OP.mult, OP.add),
                    deps=[dp_op, y1, ld_xc], tag=tag + "_dp")
        p2 = b.emit("vector",
                    lambda e: e.tensor_tensor(ystg[:], yt_[:], zg_sb[:],
                                              OP.mult),
                    deps=[p1, ld_zg, prev_yst], tag=tag + "_zm")
        zg_use = p2
        xc_use2 = p1
        yst = b.dma("sync", yd[ec, :, :], ystg[:], deps=[p2],
                    tag=tag + "_yst", stream=f"{tag}_s6")
        prev_yst = yst
        y_sts.append(yst)
    A.reset()
    ycb = A.alloc(tag + "_ycb", [128, EC * TB], BF16)
    ycbv = ycb[:].rearrange("p (c t) -> p c t", c=EC)
    wot = [A.alloc(tag + f"_wot{i}", [128, EC * 128], BF16) for i in range(2)]
    pb_o = banks[5]
    wo_last = [y_sts[-1], y_sts[-1]]
    y_use = None
    out_ops = []
    for tb in range(NTB):
        lds = []
        for kc in range(EC):
            ld = b.dma("sync", ycbv[:, kc, :],
                       yd[kc, :, tb * TB:(tb + 1) * TB],
                       deps=[y_use] + (y_sts if tb == 0 else []),
                       tag=tag + "_yld", stream=f"{tag}_yl{kc % 2}")
            lds.append(ld)
        for oc in range(DC):
            par = oc % 2
            wld = b.dma("gpsimd",
                        wot[par][:].rearrange("p (c m) -> p c m", c=EC),
                        w_out.rearrange("(c p) m -> p c m", p=128)
                        [:, :, oc * 128:(oc + 1) * 128],
                        deps=[wo_last[par]], tag=tag + "_wol",
                        stream=f"{tag}_wo{par}")
            wov = wot[par][:].rearrange("p (c m) -> p c m", c=EC)
            mms = [(pb_o.tile[:, 0:TB], wov[:, kc, :], ycbv[:, kc, :],
                    [lds[kc]] + ([wld] if kc == 0 else []))
                   for kc in range(EC)]
            mm = pb_o.mm_acc(mms, tag=tag + "_omm")
            wo_last[par] = mm
            y_use = mm
            rd = pb_o.read("vector",
                           lambda e, a=rv[:, oc, tb * TB:(tb + 1) * TB],
                                  p=pb_o.tile[:, 0:TB]:
                               e.tensor_tensor(a, a, p, OP.add),
                           tag=tag + "_ores")
            out_ops.append(rd)
    return [out_ops[-1]]


NE2 = 64
CAP = 512


def sub_l1_ffn_a(seg, rv, xv, w, deps, inst=0, salloc=None):
    b = seg.b
    tag = f"l1fa{inst}"
    salloc = salloc or seg.sb
    banks = seg._shared_banks[0:6]
    if not hasattr(seg, "A"):
        seg.A = seg.make_arena("A", 54 * 1024)
    A = seg.A
    sguw = seg.din(tag + "_sguw", w["shared_gate_up_w"])
    sdw = seg.din(tag + "_sdw", w["shared_down_w"])
    gw = seg.din(tag + "_gw", w["gate_w"])
    shactd = seg.dram(tag + "_sha", [16, 128, T], BF16)
    A.reset()
    lnops = emit_layernorm(seg, rv, xv, deps, banks, eps=1e-5,
                           tag=tag + "_ln", alloc=A.alloc)
    A.reset()
    wg_t = [A.alloc(tag + f"_wg{i}", [128, DC * 128], BF16) for i in range(2)]
    wu_t = [A.alloc(tag + f"_wu{i}", [128, DC * 128], BF16) for i in range(2)]
    gs_t = [A.alloc(tag + f"_gs{i}", [128, TB], BF16) for i in range(2)]
    stg = [A.alloc(tag + f"_stg{i}", [128, TB], BF16) for i in range(3)]
    pb_g, pb_u = banks[0], banks[1]
    wlast = [lnops[-1]] * 2
    ulast = [lnops[-1]] * 2
    slast = [None] * 3
    gprev = [None, None]
    sh_sts = []
    scnt = 0
    for i in range(16):
        par = i % 2
        wld = b.dma("gpsimd", wg_t[par][:].rearrange("p (c m) -> p c m", c=DC),
                    sguw.rearrange("(c p) m -> p c m", p=128)
                    [:, :, i * 128:(i + 1) * 128],
                    deps=[wlast[par]], tag=tag + "_wgl",
                    stream=f"{tag}_wg{par}")
        uld = b.dma("gpsimd", wu_t[par][:].rearrange("p (c m) -> p c m", c=DC),
                    sguw.rearrange("(c p) m -> p c m", p=128)
                    [:, :, 2048 + i * 128:2048 + (i + 1) * 128],
                    deps=[ulast[par]], tag=tag + "_wul",
                    stream=f"{tag}_wu{par}")
        wgv = wg_t[par][:].rearrange("p (c m) -> p c m", c=DC)
        wuv = wu_t[par][:].rearrange("p (c m) -> p c m", c=DC)
        for tb in range(NTB):
            ts = tb * TB
            gmm = pb_g.mm_acc(
                [(pb_g.tile[:, 0:TB], wgv[:, kc, :], xv[:, kc, ts:ts + TB])
                 for kc in range(DC)], deps=lnops + [wld], tag=tag + "_gmm")
            umm = pb_u.mm_acc(
                [(pb_u.tile[:, 0:TB], wuv[:, kc, :], xv[:, kc, ts:ts + TB])
                 for kc in range(DC)], deps=lnops + [uld], tag=tag + "_umm")
            g_ev = pb_g.read("scalar",
                             lambda e, o=gs_t[par][:], p=pb_g.tile[:, 0:TB]:
                                 e.activation(o, p, AF.Silu),
                             deps=[gprev[par]], tag=tag + "_sil")
            si = scnt % 3
            scnt += 1
            u_ev = pb_u.read("vector",
                             lambda e, o=stg[si][:], p=pb_u.tile[:, 0:TB],
                                    g=gs_t[par][:]:
                                 e.tensor_tensor(o, p, g, OP.mult),
                             deps=[g_ev, slast[si]], tag=tag + "_act")
            gprev[par] = u_ev
            st = b.dma("sync", shactd[i, :, ts:ts + TB], stg[si][:],
                       deps=[u_ev], tag=tag + "_shst", stream=f"{tag}_sh{si}")
            slast[si] = st
            sh_sts.append(st)
        wlast[par] = gmm
        ulast[par] = umm
    sgt, sg_op = seg.const("l1f_sgt",
                           bf(w["shared_gate_w"].reshape(DC, 128).T))
    grow = salloc(tag + "_grow", [1, T], BF16)
    pb_r = banks[2]
    for tb in range(NTB):
        ts = tb * TB
        mm = pb_r.mm_acc(
            [(pb_r.tile[0:1, 0:TB], sgt[:, kc:kc + 1], xv[:, kc, ts:ts + TB])
             for kc in range(DC)], deps=lnops + [sg_op], tag=tag + "_grm")
        ev = pb_r.read("scalar",
                       lambda e, o=grow[0:1, ts:ts + TB],
                              p=pb_r.tile[0:1, 0:TB]:
                           e.activation(o, p, AF.Sigmoid),
                       tag=tag + "_sig")
    grow_ev = ev
    A.reset()
    shb = A.alloc(tag + "_shb", [128, EC16 * TB], BF16) if False else \
        A.alloc(tag + "_shb", [128, 16 * TB], BF16)
    shbv = shb[:].rearrange("p (c t) -> p c t", c=16)
    wd_t = [A.alloc(tag + f"_wd{i}", [128, 16 * 128], BF16) for i in range(2)]
    gb_sb = A.alloc(tag + "_gb", [128, TB], F32)
    tsc = A.alloc(tag + "_tsc", [128, TB], F32)
    pb_d = banks[3]
    pb_b = banks[4]
    sh_use = None
    wdlast = [sh_sts[-1]] * 2
    gb_last = None
    t2_prev = None
    out_ops = []
    for tb in range(NTB):
        ts = tb * TB
        lds = []
        for kc in range(16):
            ld = b.dma("sync", shbv[:, kc, :],
                       shactd[kc, :, ts:ts + TB],
                       deps=[sh_use] + (sh_sts if tb == 0 else []),
                       tag=tag + "_shld", stream=f"{tag}_sl{kc % 2}")
            lds.append(ld)
        bcast_row(seg, pb_b, grow[0:1, ts:ts + TB], TB,
                  deps=[grow_ev, gb_last], tag=tag + "_gbb")
        gb_ev = pb_b.read("vector",
                          lambda e, o=gb_sb[:], p=pb_b.tile[:, 0:TB]:
                              e.tensor_copy(o, p), tag=tag + "_gbe")
        gb_last = gb_ev
        for oc in range(DC):
            par = oc % 2
            wld = b.dma("gpsimd",
                        wd_t[par][:].rearrange("p (c m) -> p c m", c=16),
                        sdw.rearrange("(c p) m -> p c m", p=128)
                        [:, :, oc * 128:(oc + 1) * 128],
                        deps=[wdlast[par]], tag=tag + "_wdl",
                        stream=f"{tag}_wd{par}")
            wdv = wd_t[par][:].rearrange("p (c m) -> p c m", c=16)
            mms = [(pb_d.tile[:, 0:TB], wdv[:, kc, :], shbv[:, kc, :],
                    [lds[kc]] + ([wld] if kc == 0 else []))
                   for kc in range(16)]
            mm = pb_d.mm_acc(mms, tag=tag + "_dmm")
            wdlast[par] = mm
            sh_use = mm
            t1 = pb_d.read("vector",
                           lambda e, o=tsc[:], p=pb_d.tile[:, 0:TB]:
                               e.tensor_tensor(o, p, gb_sb[:], OP.mult),
                           deps=[gb_ev, t2_prev], tag=tag + "_shm")
            t2 = b.emit("vector",
                        lambda e, a=rv[:, oc, ts:ts + TB], i=tsc[:]:
                            e.tensor_tensor(a, a, i, OP.add),
                        deps=[t1], tag=tag + "_shr")
            t2_prev = t2
            out_ops.append(t2)
    gwt = salloc(tag + "_gwt", [128, DC * 64], BF16)
    gwv = gwt[:].rearrange("p (c m) -> p c m", c=DC)
    gw_op = b.dma("gpsimd", gwv, gw.rearrange("(c p) m -> p c m", p=128),
                  deps=list(deps), tag=tag + "_gwl", stream=tag + "_gws")
    lgt = salloc(tag + "_lg", [128, 16 * 64], F32)
    lgv = lgt[:].rearrange("p (a b) -> p a b", a=16)
    wsb = salloc(tag + "_wsb", [128, 16 * 64], F32)
    wsv = wsb[:].rearrange("p (a b) -> p a b", a=16)
    scr = salloc(tag + "_scr", [128, 16 * 4], F32)
    scv = scr[:].rearrange("p (a b) -> p a b", a=16)
    pb_l = banks[5]
    lg_ops = []
    for tt in range(16):
        mms = [(pb_l.tile[:, 0:64], xv[:, kc, tt * 128:(tt + 1) * 128],
                gwv[:, kc, :]) for kc in range(DC)]
        mm = pb_l.mm_acc(mms, deps=lnops + [gw_op], tag=tag + "_lmm")
        ev = pb_l.read("vector",
                       lambda e, o=lgv[:, tt, :], p=pb_l.tile[:, 0:64]:
                           e.tensor_copy(o, p), tag=tag + "_lev")
        lg_ops.append(ev)
    selt = salloc(tag + "_sel", [128, 16 * 64], F32)
    seg._l1fa_selt = selt
    selv = selt[:].rearrange("p (a b) -> p a b", a=16)
    chains = []
    for tt in range(16):
        ch = []
        lg = lgv[:, tt, :]
        ws_ = wsv[:, tt, :]
        m1 = scv[:, tt, 0:1]
        mk = scv[:, tt, 1:2]
        dn = scv[:, tt, 2:3]
        nm1 = scv[:, tt, 3:4]

        def s1(d2, lg=lg, m1=m1, ws_=ws_, tt=tt):
            return b.emit("vector", lambda e: e.reduce_max(m1, lg, AX),
                          deps=d2 + [lg_ops[tt]], tag=tag + "_m1")
        ch.append(s1)

        def s1b(d2, m1=m1, nm1=nm1):
            return b.emit("vector", lambda e: e.tensor_scalar(
                nm1, m1, -1.0, None, OP.mult), deps=d2, tag=tag + "_nm1")
        ch.append(s1b)

        def s1c(d2, lg=lg, ws_=ws_):
            return b.emit("vector", lambda e: e.tensor_copy(ws_, lg),
                          deps=d2, tag=tag + "_cp")
        ch.append(s1c)

        mkscr = selv[:, tt, :]

        def mk_round(r):
            def sa(d2, ws_=ws_, mk=mk, m1=m1, r=r, mkscr=mkscr):
                src = m1 if r == 0 else mk
                return b.emit("vector", lambda e: e.tensor_scalar(
                    mkscr, ws_, src, None, OP.is_ge), deps=d2, tag=tag + "_ge")

            def sb_(d2, ws_=ws_, mkscr=mkscr):
                return b.emit("vector", lambda e: e.scalar_tensor_tensor(
                    ws_, mkscr, -1e30, ws_, OP.mult, OP.add),
                    deps=d2, tag=tag + "_msk")

            def sc_(d2, ws_=ws_, mk=mk):
                return b.emit("vector", lambda e: e.reduce_max(mk, ws_, AX),
                              deps=d2, tag=tag + "_mk")
            return [sa, sb_, sc_]

        for r in range(7):
            ch += mk_round(r)

        def s5(d2, lg=lg, nm1=nm1):
            return b.emit("scalar", lambda e: e.activation(
                lg, lg, AF.Exp, bias=nm1), deps=d2, tag=tag + "_exp")
        ch.append(s5)

        def s6(d2, lg=lg, ws_=ws_, mk=mk, dn=dn, nm1=nm1):
            return b.emit("scalar", lambda e: e.activation(
                dn, mk, AF.Exp, bias=nm1), deps=d2, tag=tag + "_em8")
        ch.append(s6)

        def s7(d2, lg=lg, ws_=ws_, dn=dn):
            return b.emit("vector", lambda e: e.tensor_scalar(
                ws_, lg, dn, None, OP.is_ge), deps=d2, tag=tag + "_keep")
        ch.append(s7)

        def s8(d2, lg=lg, ws_=ws_):
            return b.emit("vector", lambda e: e.tensor_tensor(
                ws_, ws_, lg, OP.mult), deps=d2, tag=tag + "_pm")
        ch.append(s8)

        def s9(d2, ws_=ws_, dn=dn):
            return b.emit("vector", lambda e: e.reduce_sum(dn, ws_, AX),
                          deps=d2, tag=tag + "_dn")
        ch.append(s9)

        def s10(d2, dn=dn):
            return b.emit("vector", lambda e: e.reciprocal(dn, dn),
                          deps=d2, tag=tag + "_rc")
        ch.append(s10)

        def s11(d2, ws_=ws_, dn=dn):
            return b.emit("vector", lambda e: e.tensor_scalar(
                ws_, ws_, dn, None, OP.mult), deps=d2, tag=tag + "_nw")
        ch.append(s11)
        chains.append(ch)
    w_ops = wavefront(chains)
    sel_ops = []
    for tt in range(16):
        op = b.emit("vector",
                    lambda e, o=selv[:, tt, :], i=wsv[:, tt, :]:
                        e.tensor_scalar(o, i, 0.0, None, OP.is_gt),
                    deps=[w_ops[tt]], tag=tag + "_sel")
        sel_ops.append(op)
    pb_c = banks[0]
    cmms = [(pb_c.tile[0:64, 0:1], selv[:, tt, :], seg.onesc[:],
             [sel_ops[tt], seg.onesc_op]) for tt in range(16)]
    cmm = pb_c.mm_acc(cmms, tag=tag + "_cnt")
    counts_sb = salloc(tag + "_cnts", [64, 1], F32)
    cnt_ev = pb_c.read("vector",
                       lambda e: e.tensor_copy(counts_sb[:], pb_c.tile[0:64, 0:1]),
                       tag=tag + "_cev")
    return out_ops, wsv, selv, sel_ops + [cnt_ev] + w_ops, counts_sb, cnt_ev


def sub_l1_ffn_b(seg, rv, xv, w, deps, wsv, selv, w_deps, counts_all,
                 pmask_ap, cin_ops, inst=0, salloc=None):
    b = seg.b
    tag = f"l1fb{inst}"
    salloc = salloc or seg.sb
    banks = seg._shared_banks[0:6]
    A = seg.A
    egu = seg.din(tag + "_egu", w["expert_gate_up_w"])
    edw = seg.din(tag + "_edw", w["expert_down_w"])
    triu_c, triu_op = seg.const("triu128", np.triu(np.ones((128, 128),
                                                           np.float32), 1))
    scrp = salloc(tag + "_scrp", [128, 64], F32)
    pb_a, pb_b2 = banks[0], banks[1]
    pb_a.mm_acc([(pb_a.tile[0:64, 0:1], counts_all, pmask_ap,
                  list(cin_ops))], tag=tag + "_car")
    carrycol = salloc(tag + "_ccol", [64, 1], F32)
    cc_ev = pb_a.read("vector",
                      lambda e: e.tensor_copy(carrycol[:], pb_a.tile[0:64, 0:1]),
                      tag=tag + "_ccev")
    carry_row = salloc(tag + "_crow", [1, 64], F32)
    colsum_sb = salloc(tag + "_cols", [64, 1], F32)
    pb_b2.mm_acc([(pb_b2.tile[0:1, 0:64], carrycol[:],
                   seg.identf[0:64, 0:64], [cc_ev, seg.identf_op])],
                 tag=tag + "_ctr")
    cr_ev = pb_b2.read("vector",
                       lambda e: e.tensor_copy(carry_row[:], pb_b2.tile[0:1, 0:64]),
                       tag=tag + "_crev")
    kept_ops = []
    prev_kept = None
    prev_cs = cr_ev
    for tt in range(16):
        pb_a.mm_acc(
            [(pb_a.tile[:, 0:64], triu_c[:], selv[:, tt, :],
              [triu_op] + list(w_deps)),
             (pb_a.tile[:, 0:64], seg._onesr[:], carry_row[:],
              [prev_cs, seg._onesr_op])],
            tag=tag + "_pos")
        k1 = pb_a.read("vector",
                       lambda e, o=scrp[:], p=pb_a.tile[:, 0:64]:
                           e.tensor_scalar(o, p, float(CAP), None, OP.is_lt),
                       deps=[prev_kept], tag=tag + "_lt")
        k2 = b.emit("vector",
                    lambda e, o=wsv[:, tt, :]:
                        e.tensor_tensor(o, o, scrp[:], OP.mult),
                    deps=[k1], tag=tag + "_keep")
        prev_kept = k2
        kept_ops.append(k2)
        pb_a.mm_acc(
            [(pb_a.tile[0:64, 0:1], selv[:, tt, :], seg.onesc[:],
              [seg.onesc_op] + list(w_deps))],
            tag=tag + "_cs")
        cs_ev = pb_a.read("vector",
                          lambda e: e.tensor_copy(colsum_sb[:],
                                                  pb_a.tile[0:64, 0:1]),
                          tag=tag + "_csev")
        pb_b2.mm_acc([(pb_b2.tile[0:1, 0:64], colsum_sb[:],
                       seg.identf[0:64, 0:64], [cs_ev, seg.identf_op]),
                      (pb_b2.tile[0:1, 0:64], seg._onesr[0:1, 0:1],
                       carry_row[:], [prev_cs])],
                     tag=tag + "_ctr2")
        prev_cs = pb_b2.read("vector",
                             lambda e: e.tensor_copy(carry_row[:],
                                                     pb_b2.tile[0:1, 0:64]),
                             tag=tag + "_crev2")
    A.reset()
    wgu_t = [A.alloc(tag + "_wgu", [128, DC * 1024], BF16)] * 2
    wdw_t = [A.alloc(tag + "_wdw", [128, 4 * 1024], BF16)] * 2
    act = A.alloc(tag + "_act", [128, 4 * TB], BF16)
    actv = act[:].rearrange("p (c t) -> p c t", c=4)
    gs_t = A.alloc(tag + "_gs", [128, TB], BF16)
    ts1 = A.alloc(tag + "_ts1", [128, TB], F32)
    wb_sb = A.alloc(tag + "_wb", [128, NTB * TB], BF16)
    wbv = wb_sb[:].rearrange("p (a t) -> p a t", a=NTB)
    wrow = A.alloc(tag + "_wrow", [1, TB], F32)
    pb_g, pb_u, pb_d, pb_w, pb_w2 = (banks[0], banks[1], banks[2], banks[3],
                                     banks[4])
    wgu_last = [kept_ops[-1]]
    wdw_last = [kept_ops[-1]]
    gs_war = None
    act_war = None
    a2_prev = None
    wb_war = None
    wrow_war = None
    out_ops = []
    for x in range(NE2):
        par = x % 2
        wld = b.dma("gpsimd", wgu_t[par][:].rearrange("p (c m) -> p c m", c=DC),
                    egu[x].rearrange("(c p) m -> p c m", p=128),
                    deps=list(wgu_last), tag=tag + "_wgul",
                    stream=f"{tag}_wgu")
        dld = b.dma("gpsimd", wdw_t[par][:].rearrange("p (c m) -> p c m", c=4),
                    edw[x].rearrange("(c p) m -> p c m", p=128),
                    deps=list(wdw_last), tag=tag + "_wdwl",
                    stream=f"{tag}_wdw")
        wguv = wgu_t[par][:].rearrange("p (c m) -> p c m", c=DC)
        wdwv = wdw_t[par][:].rearrange("p (c m) -> p c m", c=4)
        wb_evs = []
        for tb in range(NTB):
            war = list(pb_w.readers)
            for j in range(4):
                def ftr(e, o=pb_w.tile[0:1, j * 128:(j + 1) * 128],
                        c=wsv[:, tb * 4 + j, x:x + 1], i=seg.identf[:],
                        first=(j == 0), last=(j == 3)):
                    return e.matmul(o, c, i, start=first, stop=last)
                trop = b.emit("tensor", ftr,
                              deps=(kept_ops + war + [seg.identf_op])
                              if j == 0 else [], tag=tag + "_wtr")
            pb_w.writer = trop
            pb_w.readers = []
            wr_ev = pb_w.read("vector",
                              lambda e: e.tensor_copy(wrow[:],
                                                      pb_w.tile[0:1, 0:TB]),
                              deps=[wrow_war], tag=tag + "_wrev")
            wrow_war = bcast_row(seg, pb_w2, wrow[:], TB, deps=[wr_ev],
                                 tag=tag + "_wbb")
            wb_ev = pb_w2.read("vector",
                               lambda e, o=wbv[:, tb, :],
                                      p=pb_w2.tile[:, 0:TB]:
                                   e.tensor_copy(o, p),
                               deps=[wb_war], tag=tag + "_wbe")
            wb_evs.append(wb_ev)
        for tb in range(NTB):
            ts = tb * TB
            a_evs = []
            for i in range(4):
                gmm = pb_g.mm_acc(
                    [(pb_g.tile[:, 0:TB], wguv[:, kc, i * 128:(i + 1) * 128],
                      xv[:, kc, ts:ts + TB]) for kc in range(DC)],
                    deps=list(deps) + [wld], tag=tag + "_gmm")
                umm = pb_u.mm_acc(
                    [(pb_u.tile[:, 0:TB],
                      wguv[:, kc, 512 + i * 128:512 + (i + 1) * 128],
                      xv[:, kc, ts:ts + TB]) for kc in range(DC)],
                    deps=list(deps) + [wld], tag=tag + "_umm")
                g_ev = pb_g.read("scalar",
                                 lambda e, o=gs_t[:], p=pb_g.tile[:, 0:TB]:
                                     e.activation(o, p, AF.Silu),
                                 deps=[gs_war], tag=tag + "_sil")
                u_ev = pb_u.read("vector",
                                 lambda e, o=ts1[:], p=pb_u.tile[:, 0:TB],
                                        g=gs_t[:]:
                                     e.tensor_tensor(o, p, g, OP.mult),
                                 deps=[g_ev, a2_prev], tag=tag + "_uev")
                gs_war = u_ev
                a2 = b.emit("vector",
                            lambda e, o=actv[:, i, :], wbx=wbv[:, tb, :]:
                                e.tensor_tensor(o, ts1[:], wbx, OP.mult),
                            deps=[u_ev, wb_evs[tb], act_war],
                            tag=tag + "_act")
                a2_prev = a2
                a_evs.append(a2)
            wgu_last = [umm]
            wb_war = a_evs[-1]
            for oc in range(DC):
                mms = [(pb_d.tile[:, 0:TB], wdwv[:, kc, oc * 128:(oc + 1) * 128],
                        actv[:, kc, :], [dld] if kc == 0 else [])
                       for kc in range(4)]
                dmm = pb_d.mm_acc(mms, deps=[a_evs[-1]], tag=tag + "_dmm")
                act_war = dmm
                wdw_last = [dmm]
                rd = pb_d.read("vector",
                               lambda e, a=rv[:, oc, ts:ts + TB],
                                      p=pb_d.tile[:, 0:TB]:
                                   e.tensor_tensor(a, a, p, OP.add),
                               tag=tag + "_racc")
                out_ops.append(rd)
    return [out_ops[-1]]


def head_perm():
    perm = np.zeros(D, np.int64)
    for h in range(HEADS):
        for j in range(HD):
            perm[h * HD + j] = j * HEADS + h
    return perm


D = 1024
DC = D // 128
T = 2048
TB = 512
NTB = T // TB
HEADS = 16
HD = 64
NEG = 1.0e10


def bf(x):
    return np.asarray(x).astype(ml_dtypes.bfloat16)


class Seg:

    def __init__(self, name):
        self.nc = bass.Bass("TRN2", target_bir_lowering=False)
        self.es = ExitStack()
        self.es.__enter__()
        self.b = Builder(self.nc, self.es)
        self.name = name
        self.inputs = {}
        self.n_junk = 0
        self._const_cache = {}

    def din(self, name, arr_or_shape, dtype=None, percore=None):
        if not hasattr(self, "_din_cache"):
            self._din_cache = {}
        if name in self._din_cache:
            return self._din_cache[name]
        if percore is not None:
            arr0 = percore[0]
            ap = self.nc.dram_tensor(name, list(arr0.shape), np_to_bir(arr0.dtype),
                                     kind="ExternalInput").ap()
            self.inputs[name] = percore
            self._din_cache[name] = ap
            return ap
        arr = arr_or_shape
        ap = self.nc.dram_tensor(name, list(arr.shape), np_to_bir(arr.dtype),
                                 kind="ExternalInput").ap()
        self.inputs[name] = arr
        self._din_cache[name] = ap
        return ap

    def dout(self, name, shape, dtype=F32):
        return self.nc.dram_tensor(name, list(shape), dtype, kind="ExternalOutput").ap()

    def dram(self, name, shape, dtype=F32):
        return self.nc.dram_tensor(name, list(shape), dtype, kind="Internal").ap()

    def sb(self, name, shape, dtype=F32):
        return self.es.enter_context(self.nc.sbuf_tensor(name, list(shape), dtype))

    def ps(self, name, shape, dtype=F32):
        return self.es.enter_context(self.nc.psum_tensor(name, list(shape), dtype))

    def const(self, name, arr):
        key = name
        if key in self._const_cache:
            return self._const_cache[key]
        arr = np.ascontiguousarray(arr)
        ap = self.din(name, arr)
        tile = self.sb("c_" + name, arr.shape, np_to_bir(arr.dtype))
        op = self.b.dma("sync", tile[:], ap, tag="const_" + name,
                        stream="const", wait_all=True, front=True)
        self._const_cache[key] = (tile, op)
        return tile, op

    def make_arena(self, name, nbytes):
        backing = self.sb(name, [128, nbytes], U8)
        return Arena(self, backing, nbytes)

    def constcol(self, value):
        key = f"cv_{float(value)!r}".replace("-", "m").replace(".", "p")
        tile, op = self.const(key, np.full((128, 1), value, np.float32))
        return tile[:], op

    def junk(self, eng="vector"):
        if not hasattr(self, "_junktiles"):
            self._junktiles = {}
        if eng not in self._junktiles:
            self._junktiles[eng] = self.sb("junkt_" + eng, [128, 8], F32)
        jt = self._junktiles[eng]
        self.n_junk += 1
        if eng == "scalar":
            return self.b.emit(eng, lambda e: e.memzero(jt[:]), tag="junk")
        return self.b.emit(eng, lambda e: e.memset(jt[:], 0.0), tag="junk")

    def run_sim(self, core_inputs):
        from concourse.bass_interp import CoreSim
        from concourse.libnrt import get_pod_config
        self.b.check()
        self.b.finalize()
        self.es.__exit__(None, None, None)
        sim = CoreSim(self.nc, require_finite=False, require_nnan=False,
                      **get_pod_config())
        m = dict(core_inputs[0])
        for k, v in self.inputs.items():
            m[k] = np.ascontiguousarray(v[0] if isinstance(v, list) else v)
        for k, v in m.items():
            sim.tensor(k)[:] = v
        sim.simulate()
        outs = {}
        for name in self.nc.inst_names() if False else []:
            pass
        return sim

    def run(self, core_inputs, core_ids=None, **kw):
        self.b.check()
        self.b.finalize()
        self.es.__exit__(None, None, None)
        n = len(core_inputs)
        in_maps = []
        for c in range(n):
            m = dict(core_inputs[c])
            for k, v in self.inputs.items():
                if isinstance(v, list):
                    m[k] = np.ascontiguousarray(v[c])
                else:
                    m[k] = np.ascontiguousarray(v)
            in_maps.append(m)
        return run_bass_kernel_spmd(
            self.nc, in_maps, core_ids=list(range(n)) if core_ids is None else core_ids,
            **kw)


class Arena:

    def __init__(self, seg, backing, nbytes):
        self.seg = seg
        self.backing = backing
        self.nbytes = nbytes
        self.cur = 0
        self.hwm = 0
        self._n = 0

    def reset(self):
        self.cur = 0

    def alloc(self, name, shape, dtype):
        nc = self.seg.nc
        import functools, operator
        per_part = functools.reduce(operator.mul, shape[1:], 1) * mybir.dt.size(dtype)
        addr = (self.cur + 31) // 32 * 32
        self.cur = addr + per_part
        self.hwm = max(self.hwm, self.cur)
        assert self.cur <= self.nbytes, (name, self.cur, self.nbytes)
        self._n += 1
        h = nc.virtual_sbuf_tensor(f"{name}_ar{self._n}", list(shape), dtype)
        nc._tensor(h.name, h.shape, h.dtype, type="SB")
        mloc = nc.lookup_mloc(h)
        mloc.allocated = True
        base = nc.lookup_mloc(self.backing).addr
        mloc.addr = base + addr
        return h




def _std_seg(name, x_percore, extra_outs=()):
    seg = Seg(name)
    x = seg.din("x", None, percore=x_percore)
    y = seg.dout("y", [T, D])
    resid = seg.sb("resid", [128, DC * T], F32)
    rv = resid[:].rearrange("p (c t) -> p c t", c=DC)
    xn = seg.sb("xn", [128, DC * T], BF16)
    xv = xn[:].rearrange("p (c t) -> p c t", c=DC)
    setup_common(seg)
    shared = [PB(seg, f"sh{i}") for i in range(8)]
    seg._shared_banks = shared
    tin = emit_transpose_in(seg, x, rv, pbs=shared[0:2])
    return seg, x, y, rv, xv, tin, shared


def prep_weights(inputs):
    g = lambda k: np.ascontiguousarray(inputs[k]).astype(np.float32)
    perm = head_perm()
    pad = lambda a, axis: np.pad(a, [(0, 0)] * axis + [(0, HMOE - 2752)] +
                                 [(0, 0)] * (a.ndim - axis - 1))
    W = {}
    W["l0aa"] = {
        "qw": np.ascontiguousarray(g("L0_attn_a_qw")[:, perm]),
        "kw": np.ascontiguousarray(g("L0_attn_a_kw")[:, perm]),
        "vw": np.ascontiguousarray(g("L0_attn_a_vw")[:, perm]),
        "ow": np.ascontiguousarray(g("L0_attn_a_ow")[perm, :]),
    }
    W["l0ab"] = {k: g("L0_attn_b_" + k)
                 for k in ["Wq", "bq", "Wk", "bk", "Wv", "bv", "Wo", "bo"]}
    W["l0f"] = {
        "wq": g("L0_ffn_wq"), "wk": g("L0_ffn_wk"), "wv": g("L0_ffn_wv"),
        "wo": g("L0_ffn_wo"), "w_router": g("L0_ffn_w_router"),
        "wg": pad(g("L0_ffn_wg"), 2), "wu": pad(g("L0_ffn_wu"), 2),
        "wd": pad(g("L0_ffn_wd"), 1),
        "g1": g("L0_ffn_g1"), "g2": g("L0_ffn_g2"),
    }
    W["l1aa"] = {
        "norm1_w": g("L1_attn_a_norm1_w"), "norm2_w": g("L1_attn_a_norm2_w"),
        "A_log": g("L1_attn_a_A_log"), "D_param": g("L1_attn_a_D_param"),
        "scale": g("L1_attn_a_scale"),
        "WdtT": np.ascontiguousarray(g("L1_attn_a_Wdt").T),
        "bdt": g("L1_attn_a_bdt"),
        "WBT": np.ascontiguousarray(g("L1_attn_a_WB").T),
        "bB": g("L1_attn_a_bB"),
        "WCT": np.ascontiguousarray(g("L1_attn_a_WC").T),
        "bC": g("L1_attn_a_bC"),
        "W1T": np.ascontiguousarray(g("L1_attn_a_W1").T),
        "b1": g("L1_attn_a_b1"),
        "W2T": np.ascontiguousarray(g("L1_attn_a_W2").T),
        "b2": g("L1_attn_a_b2"),
    }
    W["l1ab"] = {
        "gamma": g("L1_attn_b_gamma"), "beta": g("L1_attn_b_beta"),
        "w_inT": np.ascontiguousarray(g("L1_attn_b_w_in").T),
        "conv_w": g("L1_attn_b_conv_w"), "conv_b": g("L1_attn_b_conv_b"),
        "w_xT": np.ascontiguousarray(g("L1_attn_b_w_x").T),
        "w_dtT": np.ascontiguousarray(g("L1_attn_b_w_dt").T),
        "b_dt": g("L1_attn_b_b_dt"),
        "A_log": g("L1_attn_b_A_log"), "Dp": g("L1_attn_b_Dp"),
        "w_outT": np.ascontiguousarray(g("L1_attn_b_w_out").T),
    }
    W["l1f"] = {k: g("L1_ffn_" + k)
                for k in ["gate_w", "shared_gate_w", "shared_gate_up_w",
                          "shared_down_w", "expert_gate_up_w",
                          "expert_down_w"]}
    return W


def run_seg_l0aa(xs, W):
    seg, x, y, rv, xv, tin, sh = _std_seg("s_l0aa", xs)
    outs = sub_l0_attn_a(seg, rv, xv, xv, W["l0aa"], tin)
    emit_transpose_out(seg, rv, y, outs, pbs=sh[0:2])
    r = seg.run([{} for _ in xs])
    return [np.asarray(rr["y"]) for rr in r.results]


def run_seg_l0ab(xs, W):
    seg, x, y, rv, xv, tin, sh = _std_seg("s_l0ab", xs)
    outs = sub_l0_attn_b(seg, rv, xv, W["l0ab"], tin)
    emit_transpose_out(seg, rv, y, outs, pbs=sh[0:2])
    r = seg.run([{} for _ in xs])
    return [np.asarray(rr["y"]) for rr in r.results]


def run_seg_l0f(xs, W, inst):
    seg, x, y, rv, xv, tin, sh = _std_seg(f"s_l0f{inst}", xs)
    outs = sub_l0_ffn(seg, rv, xv, W["l0f"], tin, inst)
    emit_transpose_out(seg, rv, y, outs, pbs=sh[6:8], add_dram=x)
    r = seg.run([{} for _ in xs])
    return [np.asarray(rr["y"]) for rr in r.results]


def run_seg_l1aa(xs, W):
    seg, x, y, rv, xv, tin, sh = _std_seg("s_l1aa", xs)
    outs = sub_l1_attn_a(seg, rv, xv, W["l1aa"], tin)
    emit_transpose_out(seg, rv, y, outs, pbs=sh[6:8], add_dram=x)
    r = seg.run([{} for _ in xs])
    return [np.asarray(rr["y"]) for rr in r.results]


def run_seg_l1ab(xs, W):
    seg, x, y, rv, xv, tin, sh = _std_seg("s_l1ab", xs)
    outs = sub_l1_attn_b(seg, rv, xv, W["l1ab"], tin)
    emit_transpose_out(seg, rv, y, outs, pbs=sh[6:8], add_dram=x)
    r = seg.run([{} for _ in xs])
    return [np.asarray(rr["y"]) for rr in r.results]


def run_seg_l1f_a(xs, W):
    seg, x, y, rv, xv, tin, sh = _std_seg("s_l1fa", xs)
    wout = seg.dout("wout", [T, NE2])
    xnd = seg.dout("xnd", [D, T], BF16)
    cout = seg.dout("cout", [NE2, 1])
    aops, wsv, selv_, wdeps, counts_sb, cnt_ev = sub_l1_ffn_a(seg, rv, xv,
                                                              W["l1f"], tin)
    seg.b.dma("sync", wout.rearrange("(a p) b -> p a b", p=128), wsv,
              deps=wdeps, tag="wst")
    seg.b.dma("gpsimd", xnd.rearrange("(c p) t -> p c t", p=128), xv,
              deps=tin + aops, tag="xnst")
    seg.b.dma("sync", cout, counts_sb[:], deps=[cnt_ev], tag="cst")
    emit_transpose_out(seg, rv, y, aops, pbs=sh[6:8])
    r = seg.run([{} for _ in xs])
    return ([np.asarray(rr["y"]) for rr in r.results],
            [np.asarray(rr["wout"]) for rr in r.results],
            [np.asarray(rr["xnd"]) for rr in r.results],
            [np.asarray(rr["cout"]) for rr in r.results])


def run_seg_l1f_b(y1s, wts, xns, counts_all, W):
    n = len(y1s)
    seg = Seg("s_l1fb")
    x = seg.din("x", None, percore=y1s)
    wt_in = seg.din("wt", None, percore=wts)
    xn_in = seg.din("xnin", None, percore=xns)
    cts = seg.din("cts", counts_all)
    pmask = seg.din("pmask", None,
                    percore=[np.asarray([[1.0] if r < c else [0.0]
                                         for r in range(8)], np.float32)
                             for c in range(n)])
    y = seg.dout("y", [T, D])
    resid = seg.sb("resid", [128, DC * T], F32)
    rv = resid[:].rearrange("p (c t) -> p c t", c=DC)
    xn = seg.sb("xn", [128, DC * T], BF16)
    xv = xn[:].rearrange("p (c t) -> p c t", c=DC)
    setup_common(seg)
    shared = [PB(seg, f"sh{i}") for i in range(8)]
    seg._shared_banks = shared
    if not hasattr(seg, "A"):
        seg.A = seg.make_arena("A", 54 * 1024)
    wsb = seg.sb("wsb", [128, 16 * NE2], F32)
    wsv = wsb[:].rearrange("p (a b) -> p a b", a=16)
    selt = seg.sb("selt", [128, 16 * NE2], F32)
    selv = selt[:].rearrange("p (a b) -> p a b", a=16)
    cts_t = seg.sb("cts_t", [8, NE2], F32)
    pmk_t = seg.sb("pmk_t", [8, 1], F32)
    b = seg.b
    tin = emit_transpose_in(seg, x, rv, pbs=shared[0:2])
    ldxn = b.dma("sync", xv, xn_in.rearrange("(c p) t -> p c t", p=128),
                 tag="ldxn")
    ldw = b.dma("sync", wsv, wt_in.rearrange("(a p) b -> p a b", p=128),
                tag="ldw")
    ldc = b.dma("sync", cts_t[:], cts, tag="ldc")
    ldp = b.dma("sync", pmk_t[:], pmask, tag="ldp")
    sel_ops = []
    for tt in range(16):
        op = b.emit("vector",
                    lambda e, o=selv[:, tt, :], i=wsv[:, tt, :]:
                        e.tensor_scalar(o, i, 0.0, None, OP.is_gt),
                    deps=[ldw], tag="selb")
        sel_ops.append(op)
    outs = sub_l1_ffn_b(seg, rv, xv, W["l1f"], [ldxn], wsv, selv,
                        sel_ops + [ldw], cts_t[:], pmk_t[:], [ldc, ldp])
    emit_transpose_out(seg, rv, y, outs, pbs=shared[6:8])
    r = seg.run([{} for _ in range(n)])
    return [np.asarray(rr["y"]) for rr in r.results]


def run_fused(xs, W):
    seg = Seg("fused")
    b = None
    x = seg.din("x", None, percore=xs)
    y = seg.dout("y", [T, D])
    resid = seg.sb("resid", [128, DC * T], F32)
    rv = resid[:].rearrange("p (c t) -> p c t", c=DC)
    xn = seg.sb("xn", [128, DC * T], BF16)
    xv = xn[:].rearrange("p (c t) -> p c t", c=DC)
    setup_common(seg)
    shared = [PB(seg, f"sh{i}") for i in range(8)]
    seg._shared_banks = shared
    seg.A = seg.make_arena("A", 54 * 1024)
    seg.B = seg.make_arena("B", 21 * 1024)
    B = seg.B
    b = seg.b
    xsave = seg.dram("xsave", [128, DC * T], F32)
    B.reset()
    ops = emit_transpose_in(seg, x, rv, pbs=shared[0:2], alloc=B.alloc)

    def save_rv(ops, tag):
        return [b.dma("sync", xsave, resid[:], deps=ops, tag=tag + "_sv",
                      stream="xsv")]

    def addback(ops, tag):
        xs_v = xsave.rearrange("p (c t) -> p c t", c=DC)
        tile = B.alloc(tag + "_ab", [128, 2 * T], F32)
        out = []
        prev = [None, None]
        for dc in range(DC):
            par = dc % 2
            ap = tile[:, par * T:(par + 1) * T]
            ld = b.dma("sync", ap, xs_v[:, dc, :],
                       deps=ops + [prev[par]], tag=tag + "_abl",
                       stream=f"xab{par}")
            ad = b.emit("vector",
                        lambda e, a=rv[:, dc, :], i=ap: e.tensor_tensor(
                            a, a, i, OP.add),
                        deps=[ld] + ops, tag=tag + "_aba")
            prev[par] = ad
            out.append(ad)
        return out

    B.reset()
    ops = sub_l0_attn_a(seg, rv, xv, xv, W["l0aa"], ops)
    B.reset()
    sv = save_rv(ops, "f0")
    ops = sub_l0_ffn(seg, rv, xv, W["l0f"], ops + sv, 0)
    ops = addback(ops, "f0")
    B.reset()
    ops = sub_l0_attn_b(seg, rv, xv, W["l0ab"], ops)
    B.reset()
    sv = save_rv(ops, "f1")
    ops = sub_l0_ffn(seg, rv, xv, W["l0f"], ops + sv, 1)
    ops = addback(ops, "f1")
    B.reset()
    sv = save_rv(ops, "aa")
    ops = sub_l1_attn_a(seg, rv, xv, W["l1aa"], ops + sv)
    ops = addback(ops, "aa")
    pmask = seg.din("pmask", None,
                    percore=[np.asarray([[1.0] if r < c else [0.0]
                                         for r in range(8)], np.float32)
                             for c in range(8)])
    pmk_t = seg.sb("pmk_t", [8, 1], F32)
    ldp = b.dma("sync", pmk_t[:], pmask, tag="ldp")
    for inst in range(2):
        B.reset()
        aops, wsv, selv, wdeps, counts_sb, cnt_ev = sub_l1_ffn_a(
            seg, rv, xv, W["l1f"], ops, inst=inst, salloc=B.alloc)
        tagc = f"cc{inst}"
        cin = seg.dram(tagc + "_in", [64, 1], F32)
        cout = seg.dram(tagc + "_out", [8 * 64, 1], F32)
        cst = b.dma("sync", cin, counts_sb[:], deps=[cnt_ev],
                    tag=tagc + "_st", stream=tagc + "_s1")
        b.stream(tagc + "_cc", "gpsimd")
        ccop = b.emit(
            "gpsimd",
            lambda e, ci=cin, co=cout: e.collective_compute(
                "AllGather", OP.bypass,
                replica_groups=[list(range(8))], ins=[ci], outs=[co]),
            deps=[cst], dma=True, tag=tagc, sem_key=tagc + "_cc")
        cts_t = B.alloc(tagc + "_cts", [8, NE2], F32)
        cld = b.dma("sync", cts_t[:],
                    cout.rearrange("(r c) o -> r (c o)", c=NE2),
                    deps=[ccop], tag=tagc + "_ld", stream=tagc + "_s2")
        ops = sub_l1_ffn_b(seg, rv, xv, W["l1f"], aops, wsv, selv, wdeps,
                           cts_t[:], pmk_t[:], [cld, ldp], inst=inst,
                           salloc=B.alloc)
        if inst == 0:
            B.reset()
            sv = save_rv(ops, "ab")
            ops = sub_l1_attn_b(seg, rv, xv, W["l1ab"], ops + sv,
                                salloc=B.alloc)
            ops = addback(ops, "ab")
    B.reset()
    emit_transpose_out(seg, rv, y, ops, pbs=shared[6:8], alloc=B.alloc)
    r = seg.run([{} for _ in xs])
    return [np.asarray(rr["y"]) for rr in r.results]


def kernel(**inputs):
    W = prep_weights(inputs)
    x0 = np.ascontiguousarray(inputs["L0_attn_a_x"]).astype(np.float32)
    xs = [np.ascontiguousarray(x0[c]) for c in range(8)]
    import os
    if not os.environ.get("K_UNFUSED"):
        try:
            ys = run_fused(xs, W)
            return np.stack(ys, 0).astype(np.float32)
        except Exception as e:
            import traceback
            print("FUSED PATH FAILED; falling back:", e)
            traceback.print_exc()
    xs = run_seg_l0aa(xs, W)
    xs = run_seg_l0f(xs, W, 0)
    xs = run_seg_l0ab(xs, W)
    xs = run_seg_l0f(xs, W, 1)
    xs = run_seg_l1aa(xs, W)
    for _ in range(2):
        y1s, wts, xns, cnts = run_seg_l1f_a(xs, W)
        counts_all = np.ascontiguousarray(
            np.stack([c.reshape(NE2) for c in cnts], 0))
        xs = run_seg_l1f_b(y1s, wts, xns, counts_all, W)
        if _ == 0:
            xs = run_seg_l1ab(xs, W)
    return np.stack(xs, 0).astype(np.float32)


def np_to_bir(dt):
    dt = np.dtype(dt)
    if dt == np.float32:
        return F32
    if dt == ml_dtypes.bfloat16:
        return BF16
    if dt == np.uint8:
        return U8
    if dt == np.int32:
        return mybir.dt.int32
    raise ValueError(dt)




class PB:
    def __init__(self, seg, name, shape=(128, 512), dtype=F32):
        self.seg = seg
        self.tile = seg.ps(name, list(shape), dtype)
        self.readers = []
        self.writer = None

    def mm_acc(self, mms, deps=(), tag=None):
        b = self.seg.b
        war = list(self.readers)
        n = len(mms)
        op = None
        for i in range(n):
            mm = mms[i]
            mdeps = list(mm[3]) if len(mm) > 3 else []
            if i == 0:
                mdeps += list(deps) + war

            def fn(e, mm=mm, first=(i == 0), last=(i == n - 1)):
                return e.matmul(mm[0], mm[1], mm[2], start=first, stop=last)

            op = b.emit("tensor", fn, deps=mdeps, tag=tag)
        self.writer = op
        self.readers = []
        return op

    def transpose(self, out_ap, in_ap, ident, deps=(), tag=None):
        b = self.seg.b
        war = list(self.readers)

        def fn(e):
            return e.transpose(out_ap, in_ap, ident)

        op = b.emit("tensor", fn, deps=list(deps) + war, tag=tag)
        self.writer = op
        self.readers = []
        return op

    def read(self, eng, fn, deps=(), tag=None):
        op = self.seg.b.emit(eng, fn, deps=list(deps) + [self.writer], tag=tag)
        self.readers.append(op)
        return op




def setup_common(seg):
    seg.identf, seg.identf_op = seg.const("identf", np.eye(128, dtype=np.float32))
    seg.identb, seg.identb_op = seg.const("identb", bf(np.eye(128)))
    seg.onesc, seg.onesc_op = seg.const("onesc", np.ones((128, 1), np.float32))
    seg._onesr, seg._onesr_op = seg.const("onesr", np.ones((1, 128), np.float32))
    seg.onesr_b, seg.onesr_b_op = seg.const("onesr_b", bf(np.ones((1, 128))))
    seg.onesc_b, seg.onesc_b_op = seg.const("onesc_b", bf(np.ones((128, 1))))


def bcast_row(seg, bank, row_ap, n, deps=(), tag="bc"):
    if row_ap.tensor.dtype == BF16:
        return bank.mm_acc([(bank.tile[:, 0:n], seg.onesr_b[:], row_ap)],
                           deps=list(deps) + [seg.onesr_b_op], tag=tag)
    return bank.mm_acc([(bank.tile[:, 0:n], seg._onesr[:], row_ap)],
                       deps=list(deps) + [seg._onesr_op], tag=tag)


def emit_transpose_in(seg, x_dram, rv, dtype_sb=F32, pbs=None, alloc=None):
    b = seg.b
    alloc = alloc or seg.sb
    xrow = alloc("ti_xrow", [128, 2 * D], F32)
    seg._ti_xrow = xrow
    if pbs is None:
        pbs = [PB(seg, "ti_p0"), PB(seg, "ti_p1")]
    evac_ops = []
    last_use = [None, None]
    for tt in range(T // 128):
        dbuf = tt % 2
        src = x_dram[tt * 128:(tt + 1) * 128, :]
        dst = xrow[:, dbuf * D:(dbuf + 1) * D]
        ld = b.dma("sync", dst, src, deps=[last_use[dbuf]], tag="ti_ld",
                   stream=f"ti{dbuf}")
        tops = []
        for dc in range(DC):
            bank = pbs[(tt * DC + dc) % 2]
            bank.transpose(bank.tile[:, 0:128], dst[:, dc * 128:(dc + 1) * 128],
                           seg.identf[:], deps=[ld, seg.identf_op], tag="ti_tr")
            ev = bank.read(
                "vector",
                lambda e, bank=bank, dc=dc, tt=tt: e.tensor_copy(
                    rv[:, dc, tt * 128:(tt + 1) * 128], bank.tile[:, 0:128]),
                tag="ti_ev")
            tops.append(ev)
            evac_ops.append(ev)
        last_use[dbuf] = tops[-1]
    return evac_ops


def emit_transpose_out(seg, rv, out_dram, deps, pbs=None, add_dram=None,
                       alloc=None):
    b = seg.b
    orow = getattr(seg, "_ti_xrow", None) if alloc is None else None
    if orow is None:
        orow = (alloc or seg.sb)("to_orow", [128, 2 * D], F32)
    if pbs is None:
        pbs = [PB(seg, "to_p0"), PB(seg, "to_p1")]
    arow = (alloc or seg.sb)("to_arow", [128, 2 * D], F32)         if add_dram is not None else None
    dma_ops = []
    last_st = [None, None]
    for tt in range(T // 128):
        dbuf = tt % 2
        dst = orow[:, dbuf * D:(dbuf + 1) * D]
        evs = []
        adep = None
        if add_dram is not None:
            adst = arow[:, dbuf * D:(dbuf + 1) * D]
            adep = b.dma("sync", adst, add_dram[tt * 128:(tt + 1) * 128, :],
                         deps=[last_st[dbuf]], tag="to_ald",
                         stream=f"toa{dbuf}")
        for dc in range(DC):
            bank = pbs[(tt * DC + dc) % 2]
            bank.transpose(bank.tile[:, 0:128], rv[:, dc, tt * 128:(tt + 1) * 128],
                           seg.identf[:],
                           deps=list(deps) + [seg.identf_op, last_st[dbuf]],
                           tag="to_tr")
            if add_dram is not None:
                ev = bank.read(
                    "vector",
                    lambda e, bank=bank, dap=dst[:, dc * 128:(dc + 1) * 128],
                           aap=adst[:, dc * 128:(dc + 1) * 128]:
                        e.tensor_tensor(dap, aap, bank.tile[:, 0:128], OP.add),
                    deps=[adep], tag="to_ev")
            else:
                ev = bank.read(
                    "vector",
                    lambda e, bank=bank, dap=dst[:, dc * 128:(dc + 1) * 128]:
                        e.tensor_copy(dap, bank.tile[:, 0:128]),
                    tag="to_ev")
            evs.append(ev)
        st = b.dma("sync", out_dram[tt * 128:(tt + 1) * 128, :], dst,
                   deps=evs, tag="to_st", stream=f"to{dbuf}")
        last_st[dbuf] = st
        dma_ops.append(st)
    return dma_ops


def emit_layernorm(seg, rv, xn, deps, banks, eps=1e-5, tag="ln", rms=False,
                   gain=None, gain_op=None, beta=None, beta_op=None,
                   out2=None, alloc=None):
    b = seg.b
    alloc = alloc or seg.sb
    pb_s, pb_q, pb_m, pb_r = banks[0], banks[1], banks[2], banks[3]
    sq = [alloc(tag + f"_sq{i}", [128, TB], F32) for i in range(DC)]
    stats = alloc(tag + "_st", [1, 6 * TB], F32)
    sv = stats[:].rearrange("p (k t) -> p k t", k=6)
    tmp = [alloc(tag + f"_tmp{i}", [128, TB], F32) for i in range(2)]
    out_ops = []
    deps = list(deps)
    prev_mms = []
    for tb in range(NTB):
        ts = tb * TB
        te = ts + TB
        sum_ev = None
        if not rms:
            pb_s.mm_acc(
                [(pb_s.tile[0:1, 0:TB], seg.onesc[:], rv[:, dc, ts:te])
                 for dc in range(DC)],
                deps=deps + [seg.onesc_op], tag=tag + "_sum")
            sum_ev = pb_s.read(
                "vector",
                lambda e: e.tensor_copy(sv[:, 0, :], pb_s.tile[0:1, 0:TB]),
                tag=tag + "_sev")
        qmms = []
        for dc in range(DC):
            sq_ap = sq[dc][:]
            src_ap = rv[:, dc, ts:te]
            aop = b.emit(
                "scalar",
                lambda e, o=sq_ap, i=src_ap: e.activation(o, i, AF.Square),
                deps=list(deps) + prev_mms,
                tag=tag + "_sq")
            qmms.append((pb_q.tile[0:1, 0:TB], seg.onesc[:], sq_ap,
                         [aop, seg.onesc_op]))
        qg = pb_q.mm_acc(qmms, tag=tag + "_qsum")
        prev_mms = [qg]
        sq_ev = pb_q.read(
            "vector", lambda e: e.tensor_copy(sv[:, 1, :], pb_q.tile[0:1, 0:TB]),
            tag=tag + "_qev")
        if rms:
            op3 = b.emit("vector", lambda e: e.tensor_scalar(
                sv[:, 4, :], sv[:, 1, :], 1.0 / D, None, OP.mult),
                deps=[sq_ev], tag=tag + "_ms")
            mean_op = None
        else:
            op1 = b.emit("vector", lambda e: e.tensor_scalar(
                sv[:, 2, :], sv[:, 0, :], 1.0 / D, None, OP.mult),
                deps=[sum_ev, sq_ev], tag=tag + "_mean")
            op2 = b.emit("vector", lambda e: e.tensor_tensor(
                sv[:, 3, :], sv[:, 2, :], sv[:, 2, :], OP.mult),
                deps=[op1], tag=tag + "_msq")
            op3 = b.emit("vector", lambda e: e.scalar_tensor_tensor(
                sv[:, 4, :], sv[:, 1, :], 1.0 / D, sv[:, 3, :],
                OP.mult, OP.subtract), deps=[op2], tag=tag + "_var")
            mean_op = op1
        epsc, epsc_op = seg.constcol(eps)
        op4 = b.emit("scalar", lambda e: e.activation(
            sv[:, 5, :], sv[:, 4, :], AF.Ln, bias=epsc[0:1, 0:1]),
            deps=[op3, epsc_op], tag=tag + "_lnv")
        op5 = b.emit("scalar", lambda e: e.activation(
            sv[:, 5, :], sv[:, 5, :], AF.Exp, scale=-0.5),
            deps=[op4], tag=tag + "_expv")
        if not rms:
            bcast_row(seg, pb_m, sv[:, 2, :], TB, deps=[mean_op], tag=tag + "_mb")
        bcast_row(seg, pb_r, sv[:, 5, :], TB, deps=[op5], tag=tag + "_rb")
        chains = []
        tmp_war = getattr(seg, "_ln_tmp_war_" + tag, {})
        setattr(seg, "_ln_tmp_war_" + tag, tmp_war)
        for dc in range(DC):
            ch = []
            xn_ap = xn[:, dc, ts:te]
            rv_ap = rv[:, dc, ts:te]
            rb_ap = pb_r.tile[:, 0:TB]
            mb_ap = pb_m.tile[:, 0:TB]
            tmp_ap = tmp[dc % 2][:]
            g_ap = gain[:, dc:dc + 1] if gain is not None else None
            be_ap = beta[:, dc:dc + 1] if beta is not None else None
            if rms:
                def t2(deps2, xn_ap=xn_ap, rv_ap=rv_ap, rb_ap=rb_ap, g_ap=g_ap):
                    if g_ap is not None:
                        return pb_r.read(
                            "vector", lambda e: e.scalar_tensor_tensor(
                                xn_ap, rb_ap, g_ap, rv_ap, OP.mult, OP.mult),
                            deps=deps2 + deps + ([gain_op] if gain_op else []),
                            tag=tag + "_g")
                    return pb_r.read(
                        "vector", lambda e: e.tensor_tensor(
                            xn_ap, rv_ap, rb_ap, OP.mult),
                        deps=deps2 + deps, tag=tag + "_mul")
                ch.append(t2)
            else:
                def t1(deps2, tmp_ap=tmp_ap, rv_ap=rv_ap, mb_ap=mb_ap, dc=dc):
                    return pb_m.read(
                        "vector", lambda e: e.tensor_tensor(
                            tmp_ap, rv_ap, mb_ap, OP.subtract),
                        deps=deps2 + deps + [tmp_war.get(dc % 2)],
                        tag=tag + "_sub")
                ch.append(t1)

                def t2(deps2, xn_ap=xn_ap, tmp_ap=tmp_ap, rb_ap=rb_ap,
                       g_ap=g_ap, dc=dc):
                    if g_ap is not None:
                        op = pb_r.read(
                            "vector", lambda e: e.scalar_tensor_tensor(
                                xn_ap, rb_ap, g_ap, tmp_ap, OP.mult, OP.mult),
                            deps=deps2 + ([gain_op] if gain_op else []),
                            tag=tag + "_g")
                    else:
                        op = pb_r.read(
                            "vector", lambda e: e.tensor_tensor(
                                xn_ap, tmp_ap, rb_ap, OP.mult),
                            deps=deps2, tag=tag + "_mul")
                    tmp_war[dc % 2] = op
                    return op
                ch.append(t2)
            if beta is not None:
                def t3(deps2, xn_ap=xn_ap, be_ap=be_ap):
                    return b.emit(
                        "vector", lambda e: e.tensor_scalar(
                            xn_ap, xn_ap, be_ap, None, OP.add),
                        deps=deps2 + ([beta_op] if beta_op else []),
                        tag=tag + "_beta")
                ch.append(t3)
            if out2 is not None:
                def t4(deps2, o2_ap=out2[:, dc, ts:te], xn_ap=xn_ap):
                    return b.emit(
                        "vector", lambda e: e.tensor_copy(o2_ap, xn_ap),
                        deps=deps2, tag=tag + "_o2")
                ch.append(t4)
            chains.append(ch)
        out_ops += wavefront(chains)
    return out_ops


def load_weight_bf16(seg, name, w_ap, din, dout, deps=(), tile=None,
                     stream=None):
    kc = din // 128
    if tile is None:
        tile = seg.sb("w_" + name, [128, kc * dout], BF16)
    view = tile[:].rearrange("p (c m) -> p c m", c=kc)[:, :, 0:dout]
    src = w_ap.rearrange("(c p) m -> p c m", p=128)
    op = seg.b.dma("gpsimd", view, src, deps=deps, tag="wld_" + name,
                   stream=stream or ("w_" + name),
                   wait_all=(stream is None))
    return view, op


def emit_proj(seg, tag, wview, w_op, xv, nkc, dout, out_fn, deps, banks,
              extra_mms=None, tbs=None):
    evs = []
    nb = len(banks)
    i = 0
    for tb in (range(NTB) if tbs is None else tbs):
        for oc in range(dout // 128):
            pb = banks[i % nb]
            i += 1
            mms = [(pb.tile[:, 0:TB], wview[:, kc, oc * 128:(oc + 1) * 128],
                    xv[:, kc, tb * TB:(tb + 1) * TB]) for kc in range(nkc)]
            if extra_mms is not None:
                mms += extra_mms(oc, tb)
            mm = pb.mm_acc(mms, deps=list(deps) + [w_op], tag=tag + "_mm")
            evs.append(out_fn(oc, tb, pb, mm))
    return evs


def evac_to_dram(seg, tag, dram_fm, dtype=BF16, bias=None, bias_op=None,
                 nstage=3, alloc=None):
    alloc = alloc or seg.sb
    stages = [alloc(f"{tag}_stg{i}", [128, TB], dtype) for i in range(nstage)]
    state = {"i": 0, "last_st": [None] * nstage}

    def out_fn(oc, tb, pb, mm):
        si = state["i"] % nstage
        state["i"] += 1
        stg = stages[si]
        if bias is not None:
            ev = pb.read("vector",
                         lambda e, o=stg[:], p=pb.tile[:, 0:TB],
                                b_=bias[:, oc:oc + 1]:
                             e.tensor_scalar(o, p, b_, None, OP.add),
                         deps=[state["last_st"][si]] +
                              ([bias_op] if bias_op else []),
                         tag=tag + "_ev")
        else:
            ev = pb.read("vector",
                         lambda e, o=stg[:], p=pb.tile[:, 0:TB]:
                             e.tensor_copy(o, p),
                         deps=[state["last_st"][si]], tag=tag + "_ev")
        st = seg.b.dma("sync",
                       dram_fm[oc * 128:(oc + 1) * 128,
                               tb * TB:(tb + 1) * TB],
                       stg[:], deps=[ev], tag=tag + "_st",
                       stream=f"{tag}_st{si}")
        state["last_st"][si] = st
        return st

    return out_fn


def emit_proj_tok(seg, tag, wview, w_op, xv, nkc, dout, dram_tok, deps,
                  banks, bias=None, bias_op=None, alloc=None):
    evs = []
    alloc = alloc or seg.sb
    nb = len(banks)
    stages = [alloc(f"{tag}_stg{i}", [128, 512], BF16) for i in range(3)]
    state = {"i": 0, "last_st": [None] * 3}
    nvc = dout // 512
    for tt in range(T // 128):
        for vc in range(nvc):
            pb = banks[(tt * nvc + vc) % nb]
            mms = [(pb.tile[:, 0:512], xv[:, kc, tt * 128:(tt + 1) * 128],
                    wview[:, kc, vc * 512:(vc + 1) * 512]) for kc in range(nkc)]
            if bias is not None:
                row, row_op = bias
                mms.append((pb.tile[:, 0:512], seg.onesr_b[:],
                            row[0:1, vc * 512:(vc + 1) * 512],
                            [row_op, seg.onesr_b_op]))
            mm = pb.mm_acc(mms, deps=list(deps) + [w_op], tag=tag + "_mm")
            si = state["i"] % 3
            state["i"] += 1
            stg = stages[si]
            ev = pb.read("vector",
                         lambda e, o=stg[:], p=pb.tile[:, 0:512]:
                             e.tensor_copy(o, p),
                         deps=[state["last_st"][si]], tag=tag + "_ev")
            st = seg.b.dma("sync",
                           dram_tok[tt * 128:(tt + 1) * 128,
                                    vc * 512:(vc + 1) * 512],
                           stg[:], deps=[ev], tag=tag + "_st",
                           stream=f"{tag}_st{si}")
            state["last_st"][si] = st
            evs.append(st)
    return evs


def emit_attention(seg, tag, q_dram, k_dram, v_dram, o_dram, deps,
                   causal, scale, alibi_slopes=None, pbs=None, alloc=None):
    b = seg.b
    alloc = alloc or seg.sb
    nh = HEADS
    qh = [alloc(f"{tag}_q{i}", [64, T], BF16) for i in range(2)]
    kh = [alloc(f"{tag}_k{i}", [64, T], BF16) for i in range(2)]
    vh = [alloc(f"{tag}_v{i}", [128, 16 * HD], BF16) for i in range(2)]
    pt = alloc(f"{tag}_pt", [128, 16 * TB], BF16)
    ptv = pt[:].rearrange("p (k q) -> p k q", k=16)
    mkb = alloc(f"{tag}_mkb", [128, 8], F32)
    m1c = alloc(f"{tag}_m1c", [128, 4], F32)
    mb_sb = alloc(f"{tag}_mb", [128, TB], F32)
    rs_sb = alloc(f"{tag}_rs", [64, TB], F32)
    ts_sb = [alloc(f"{tag}_ts{i}", [128, TB], F32) for i in range(2)]
    o_stg = [alloc(f"{tag}_os{i}", [64, TB], BF16) for i in range(2)]
    srow = alloc(f"{tag}_srow", [1, TB], F32)
    rrow = alloc(f"{tag}_rrow", [1, TB], F32)
    if pbs is None:
        pbs = [PB(seg, f"{tag}_pb{i}") for i in range(6)]
    pA = pbs[0:2]
    pRow = pbs[2]
    pSum = pbs[3]
    pO = pbs[4]
    pR = pbs[5]
    if causal:
        mask01 = []
        for r in range(4):
            p_ = np.arange(128)[:, None]
            j_ = np.arange(TB)[None, :]
            mask01.append((j_ >= (r * 128 + p_)).astype(np.float32))
        maskc, maskc_op = seg.const(tag + "_mask", bf(np.stack(mask01, 1).reshape(128, 4 * TB)))
        maskv = maskc[:].rearrange("p (r q) -> p r q", r=4)
    if alibi_slopes is not None:
        cols = np.zeros((128, nh, 16), np.float32)
        madd = np.zeros((128, nh), np.float32)
        for h in range(nh):
            for kt in range(16):
                cols[:, h, kt] = (1.0 / scale) * alibi_slopes[h] * (kt * 128 + np.arange(128))
            madd[:, h] = (1.0 / scale) * alibi_slopes[h] * (T - 1)
        acols, acols_op = seg.const(tag + "_acols", cols.reshape(128, nh * 16))
        acolsv = acols[:].rearrange("p (h k) -> p h k", h=nh)
        amadd, amadd_op = seg.const(tag + "_amadd", madd)
    last_q = [None, None]
    last_k = [None, None]
    last_v = [None, None]
    last_ost = [None, None]
    out_ops = []
    for h in range(nh):
        par = h % 2
        ldq = b.dma("sync", qh[par][:], q_dram[h * 64:(h + 1) * 64, :],
                    deps=[last_q[par]] + list(deps), tag=tag + "_ldq",
                    stream=f"{tag}_q{par}")
        ldk = b.dma("sync", kh[par][:], k_dram[h * 64:(h + 1) * 64, :],
                    deps=[last_k[par]] + list(deps), tag=tag + "_ldk",
                    stream=f"{tag}_k{par}")
        ldv = b.dma("gpsimd",
                    vh[par][:].rearrange("p (k v) -> p k v", k=16),
                    v_dram[:, h * 64:(h + 1) * 64].rearrange(
                        "(k p) v -> p k v", p=128),
                    deps=[last_v[par]] + list(deps), tag=tag + "_ldv",
                    stream=f"{tag}_v{par}")
        Q, K, V = qh[par], kh[par], vh[par]
        q_uses, k_uses, v_uses = [], [], []
        pt_last_read = {}
        for qb in range(NTB):
            nkt = min(16, 4 * qb + 4) if causal else 16
            chains = []
            for qtl in range(4):
                qt = qb * 4 + qtl
                nkb = min(NTB, qt // 4 + 1) if causal else NTB
                ch = []
                red_ops = []

                def stA(deps2, qt=qt, nkb=nkb, qtl=qtl, red_ops=red_ops):
                    last = None
                    for kb in range(nkb):
                        bank = pA[kb % 2]
                        mm = bank.mm_acc(
                            [(bank.tile[:, 0:TB],
                              Q[:, qt * 128:(qt + 1) * 128],
                              K[:, kb * TB:(kb + 1) * TB])],
                            deps=[ldq, ldk], tag=tag + "_mmA")
                        q_uses.append(mm)
                        k_uses.append(mm)
                        red = bank.read(
                            "vector",
                            lambda e, o=mkb[:, (qtl % 2) * 4 + kb:
                                            (qtl % 2) * 4 + kb + 1],
                                   p=bank.tile[:, 0:TB]:
                                e.tensor_reduce(o, p, AX, OP.max),
                            deps=deps2, tag=tag + "_redA")
                        red_ops.append(red)
                        last = red
                    return last

                ch.append(stA)

                def stComb(deps2, qt=qt, nkb=nkb, qtl=qtl):
                    o = m1c[:, qtl:qtl + 1]
                    i = mkb[:, (qtl % 2) * 4:(qtl % 2) * 4 + nkb]
                    return b.emit(
                        "vector",
                        lambda e, o=o, i=i: e.tensor_reduce(o, i, AX, OP.max),
                        deps=deps2, tag=tag + "_comb")

                ch.append(stComb)
                if alibi_slopes is not None:
                    def stAl(deps2, qtl=qtl, h=h):
                        o = m1c[:, qtl:qtl + 1]
                        return b.emit(
                            "vector",
                            lambda e, o=o, a=amadd[:, h:h + 1]:
                                e.tensor_scalar(o, o, a, None, OP.add),
                            deps=deps2 + [amadd_op], tag=tag + "_al")
                    ch.append(stAl)

                def stTr(deps2, qtl=qtl):
                    return _acc_row(seg, pRow, qtl, m1c[:, qtl:qtl + 1],
                                    deps2)

                ch.append(stTr)
                chains.append(ch)
            wavefront(chains)
            mrow_ev = pRow.read(
                "vector", lambda e: e.tensor_copy(srow[:], pRow.tile[0:1, 0:TB]),
                tag=tag + "_mrowev")
            bcast_row(seg, pR, srow[:], TB, deps=[mrow_ev], tag=tag + "_mbc")
            mb_ev = pR.read(
                "vector", lambda e: e.tensor_copy(mb_sb[:], pR.tile[:, 0:TB]),
                tag=tag + "_mbev")
            sum_started = [False]
            exp_ops = []
            for kt in range(nkt):
                bank = pA[kt % 2]
                mm = bank.mm_acc(
                    [(bank.tile[:, 0:TB], K[:, kt * 128:(kt + 1) * 128],
                      Q[:, qb * TB:(qb + 1) * TB])],
                    deps=[ldq, ldk], tag=tag + "_mmB")
                q_uses.append(mm)
                k_uses.append(mm)
                tsb = ts_sb[kt % 2]
                if alibi_slopes is not None:
                    stt = bank.read(
                        "vector",
                        lambda e, o=tsb[:], p=bank.tile[:, 0:TB],
                               a=acolsv[:, h, kt:kt + 1]:
                            e.scalar_tensor_tensor(o, p, a, mb_sb[:],
                                                   OP.add, OP.subtract),
                        deps=[mb_ev, acols_op], tag=tag + "_stt")
                else:
                    stt = bank.read(
                        "vector",
                        lambda e, o=tsb[:], p=bank.tile[:, 0:TB]:
                            e.tensor_tensor(o, p, mb_sb[:], OP.subtract),
                        deps=[mb_ev], tag=tag + "_stt")
                ptap = ptv[:, kt, :]
                eop = b.emit(
                    "scalar",
                    lambda e, o=ptap, i=tsb[:]: e.activation(
                        o, i, AF.Exp, scale=float(scale)),
                    deps=[stt, pt_last_read.get(kt)], tag=tag + "_exp")
                last_pt_w = eop
                if causal and kt >= 4 * qb:
                    r = kt - 4 * qb
                    mop = b.emit(
                        "vector",
                        lambda e, o=ptap, m=maskv[:, r, :]: e.tensor_tensor(
                            o, o, m, OP.mult),
                        deps=[eop, maskc_op], tag=tag + "_msk")
                    last_pt_w = mop
                exp_ops.append((kt, last_pt_w))
                def fnsum(e, o=pSum.tile[0:1, 0:TB], w=seg.onesc_b[:],
                          i=ptap, first=(not sum_started[0]),
                          last=(kt == nkt - 1)):
                    return e.matmul(o, w, i, start=first, stop=last)
                war = list(pSum.readers) if not sum_started[0] else []
                smm = b.emit("tensor", fnsum,
                             deps=[last_pt_w, seg.onesc_b_op] + war,
                             tag=tag + "_sum")
                pSum.writer = smm
                if not sum_started[0]:
                    pSum.readers = []
                    sum_started[0] = True
            sum_ev = pSum.read(
                "vector", lambda e: e.tensor_copy(rrow[:], pSum.tile[0:1, 0:TB]),
                tag=tag + "_sev")
            rec = b.emit("vector", lambda e: e.reciprocal(rrow[:], rrow[:]),
                         deps=[sum_ev], tag=tag + "_rec")
            bcast_row(seg, pR, rrow[:], TB, deps=[rec], tag=tag + "_rbc")
            rs_ev = pR.read(
                "vector", lambda e: e.tensor_copy(rs_sb[:], pR.tile[0:64, 0:TB]),
                tag=tag + "_rsev")
            mms = []
            for kt in range(nkt):
                ktdep = exp_ops[kt][1]
                mms.append((pO.tile[0:64, 0:TB], vh[par][:].rearrange(
                    "p (k v) -> p k v", k=16)[:, kt, :], ptv[:, kt, :],
                    [ktdep, ldv]))
            pvop = pO.mm_acc(mms, tag=tag + "_pv")
            for kt in range(nkt):
                pt_last_read[kt] = pvop
            v_uses.append(pvop)
            q_uses.append(pvop)
            opar = (h * NTB + qb) % 2
            oev = pO.read(
                "vector",
                lambda e, o=o_stg[opar][:], p=pO.tile[0:64, 0:TB]:
                    e.tensor_tensor(o, p, rs_sb[:], OP.mult),
                deps=[rs_ev, last_ost[opar]], tag=tag + "_oev")
            pR.readers.append(oev)
            ost = b.dma("sync", o_dram[h * 64:(h + 1) * 64,
                                       qb * TB:(qb + 1) * TB],
                        o_stg[opar][:], deps=[oev], tag=tag + "_ost",
                        stream=f"{tag}_o{opar}")
            last_ost[opar] = ost
            out_ops.append(ost)
        last_q[par] = q_uses[-1]
        last_k[par] = k_uses[-1]
        last_v[par] = v_uses[-1]
    return out_ops


def _acc_row(seg, pRow, qtl, col_ap, deps2):
    war = list(pRow.readers) if qtl == 0 else []
    if qtl == 0:
        pRow.readers = []

    def fn(e, o=pRow.tile[0:1, qtl * 128:(qtl + 1) * 128], c=col_ap,
           i=seg.identf[:], first=(qtl == 0), last=(qtl == 3)):
        return e.matmul(o, c, i, start=first, stop=last)
    op = seg.b.emit("tensor", fn, deps=deps2 + war + [seg.identf_op],
                    tag="mtr_acc")
    pRow.writer = op
    return op


def wavefront(chains):
    n = len(chains)
    if n == 0:
        return []
    nst = max(len(c) for c in chains)
    last = [None] * n
    if n == 1:
        for s, fn in enumerate(chains[0]):
            last[0] = fn([last[0]] if last[0] else [])
        return last
    for w in range(n + nst - 1):
        for c in range(min(w, n - 1), -1, -1):
            s = w - c
            if s < len(chains[c]):
                last[c] = chains[c][s]([last[c]] if last[c] is not None else [])
    return last
